# version 05
# baseline (speedup 1.0000x reference)
import contextlib
import numpy as np
import ml_dtypes
import concourse.bass as bass
import concourse.mybir as mybir
from concourse.bass_utils import run_bass_kernel_spmd

F32 = mybir.dt.float32
BF16 = mybir.dt.bfloat16
AF = mybir.ActivationFunctionType
ALU = mybir.AluOpType
AX = mybir.AxisListType
BF = ml_dtypes.bfloat16

NCORE = 8
TOK = 1024
TB = 512
D = 2048
ND = 16
INTER = 8192
NI = 64
HK, HV, DK, DV = 16, 32, 128, 128
CC = 128

ENGS = ("sync", "scalar", "vector", "tensor", "gpsimd")
SERIALIZE = False
PRUNE = True
SER_SET = ()


class _Buf:
    __slots__ = ("w", "r")

    def __init__(self):
        self.w = None
        self.r = []


class _Ev:
    __slots__ = ("e", "c", "vc")

    def __init__(self, e, c, vc):
        self.e = e
        self.c = c
        self.vc = vc


RGROUPS = [[0, 1], [2, 3], [4, 5], [6, 7]]


class Prog:
    def __init__(self):
        self.nc = bass.Bass("TRN2", target_bir_lowering=False)
        self.stack = contextlib.ExitStack()
        self.plan = {e: [] for e in ENGS}
        self.cnt = dict.fromkeys(ENGS, 0)
        self.vc = {e: {} for e in ENGS}
        self.outev = {}
        self.pfx = ""
        self.scopes = []
        self.bar = None

    def push(self, pfx):
        self.pfx = pfx
        self.scopes.append(contextlib.ExitStack())

    def pop(self):
        self.barrier()
        self.scopes.pop().close()

    def barrier(self):
        self.bar = {e: self.cnt.get(e, 0) for e in ENGS}

    def din(self, name, shape, dt):
        h = self.nc.dram_tensor(name, list(shape), dt, kind="ExternalInput")
        return h.ap(), _Buf()

    def dout(self, name, shape, dt):
        h = self.nc.dram_tensor(name, list(shape), dt, kind="ExternalOutput")
        return h.ap(), _Buf()

    def dint(self, name, shape, dt):
        h = self.nc.dram_tensor(name, list(shape), dt)
        return h.ap(), _Buf()

    def sb(self, name, shape, dt):
        st = self.scopes[-1] if self.scopes else self.stack
        h = st.enter_context(self.nc.sbuf_tensor(self.pfx + name, list(shape), dt))
        return h, _Buf()

    def ps(self, name, shape, dt):
        st = self.scopes[-1] if self.scopes else self.stack
        h = st.enter_context(self.nc.psum_tensor(self.pfx + name, list(shape), dt))
        return h, _Buf()

    def emit(self, eng, fn, reads=(), writes=(), inc=1, out_dma=False, sem=None):
        deps = {}

        def add(ev):
            if ev is None:
                return
            cur = deps.get(ev.e)
            if cur is None or ev.c > cur.c:
                deps[ev.e] = ev

        for b in reads:
            if b is not None:
                add(b.w)
        for b in writes:
            if b is not None:
                add(b.w)
                for ev in b.r:
                    add(ev)
        if self.bar:
            for p, v in self.bar.items():
                if v > 0 and p != eng:
                    cur = deps.get(p)
                    if cur is None or v > cur.c:
                        deps[p] = _Ev(p, v, {})
        waits = []
        myvc = self.vc[eng]
        if SERIALIZE or SER_SET:
            ser = ENGS if SERIALIZE else SER_SET
            for p in ser:
                if p != eng and self.cnt[p] > 0:
                    ev0 = _Ev(p, self.cnt[p], {})
                    cur = deps.get(p)
                    if cur is None or ev0.c > cur.c:
                        deps[p] = ev0
        for p in list(deps):
            if p in ("sync", "gpsimd"):
                ev0 = deps[p]
                if ev0.c < self.cnt[p]:
                    deps[p] = _Ev(p, self.cnt[p], ev0.vc)
        for p in sorted(deps):
            ev = deps[p]
            if p == "tensor" and eng == "tensor":
                continue
            if PRUNE and myvc.get(p, 0) >= ev.c:
                continue
            waits.append((p, ev.c))
            for q, v in ev.vc.items():
                if myvc.get(q, 0) < v:
                    myvc[q] = v
            if myvc.get(p, 0) < ev.c:
                myvc[p] = ev.c
        key = sem or eng
        self.cnt[key] = self.cnt.get(key, 0) + inc
        ev = _Ev(key, self.cnt[key], dict(myvc))
        ev.vc[key] = ev.c
        self.plan[eng].append((waits, fn, inc, key))
        for b in writes:
            if b is not None:
                b.w = ev
                b.r = []
        for b in reads:
            if b is not None:
                b.r.append(ev)
        if out_dma:
            self.outev[eng] = self.cnt[eng]
        return ev

    def cc(self, kind, op, in_ap, in_buf, out_ap, out_buf, key):
        def fn(g, kind=kind, op=op, i=in_ap, o=out_ap):
            return g.collective_compute(kind, op, replica_groups=RGROUPS,
                                        ins=[i.opt()], outs=[o.opt()])
        self.emit("gpsimd", fn, reads=[in_buf], writes=[out_buf], inc=1, sem=key)

    def dma(self, dst_ap, src_ap, dst_buf, src_buf=None, cast=False, out=False, eng=None):
        e = eng or ("gpsimd" if cast else "sync")
        self.emit(e, lambda E, d=dst_ap, s=src_ap: E.dma_start(d, s),
                  reads=[src_buf] if src_buf is not None else [],
                  writes=[dst_buf] if dst_buf is not None else [],
                  inc=16, out_dma=out)

    def mm(self, ps_ap, ps_buf, pairs, reads, accum=False):
        n = len(pairs)

        def fn(te, pairs=tuple(pairs), o=ps_ap, n=n, accum=accum):
            ins = None
            for i, (l, r) in enumerate(pairs):
                ins = te.matmul(o, l, r, start=(i == 0 and not accum), stop=(i == n - 1))
            return ins
        self.emit("tensor", fn, reads=list(reads), writes=[ps_buf])

    def tr(self, ps_ap, ps_buf, in_ap, in_buf, iden_ap, iden_buf):
        self.emit("tensor", lambda te, o=ps_ap, i=in_ap, d=iden_ap: te.transpose(o, i, d),
                  reads=[in_buf, iden_buf], writes=[ps_buf])

    def run(self, in_maps, core_ids, trace=False):
        for e in ("sync", "gpsimd"):
            v = self.cnt[e]
            if v <= 0:
                continue
            tgt = "sync"
            myvc = self.vc[tgt]
            if myvc.get(e, 0) < v:
                self.plan[tgt].append(([(e, v)], None, 0, None))
                myvc[e] = v
        nc = self.nc
        sems = {}
        with contextlib.ExitStack() as st:
            for e in list(ENGS) + [k for k in self.cnt if k not in ENGS]:
                sems[e] = st.enter_context(nc.semaphore("S_" + e))
            with nc.Block() as block:
                def mk(eng_name):
                    def body(eng):
                        for waits, fn, inc, key in self.plan[eng_name]:
                            for (p, v) in waits:
                                eng.wait_ge(sems[p], v)
                            if fn is not None:
                                ins = fn(eng)
                                if inc:
                                    ins.then_inc(sems[key], inc)
                    return body
                for name, dec in (("sync", block.sync), ("scalar", block.scalar),
                                  ("vector", block.vector), ("tensor", block.tensor),
                                  ("gpsimd", block.gpsimd)):
                    if self.plan[name]:
                        dec(mk(name))
        self.stack.close()
        return run_bass_kernel_spmd(nc, in_maps, core_ids=core_ids, trace=trace)


def _had(n):
    H = np.array([[1.0]], np.float32)
    while H.shape[0] < n:
        H = np.block([[H, H], [H, -H]]).astype(np.float32)
    return H


def host_consts():
    c = {}
    c["H128"] = _had(128)
    H64 = _had(64)
    c["H64I2"] = np.kron(H64, np.eye(2, dtype=np.float32))
    c["IDEN"] = np.eye(128, dtype=np.float32)
    c["ONES1"] = np.ones((1, 128), np.float32)
    c["ONES128"] = np.ones((128, 1), np.float32)
    t = np.arange(128)
    c["MASKW"] = np.where(t[None, :] < t[:, None], 0.0, 1e9).astype(np.float32)
    c["MASKAT"] = np.where(t[:, None] <= t[None, :], 0.0, 1e9).astype(np.float32)
    E16 = np.zeros((8, 128), np.float32)
    for e in range(8):
        E16[e, e * 16:(e + 1) * 16] = 1.0
    c["E16"] = E16
    return c


def emit_ln_resident(P, xt, yt, psf, scr, ones128, ones1, post_cols=None, eps=1e-5):
    o128, o128b = ones128
    o1, o1b = ones1
    m_sb, m_b = P.sb("ln_m", [1, 1024], F32)
    r_sb, r_b = P.sb("ln_r", [1, 1024], F32)
    for tb in range(2):
        sl = slice(tb * TB, (tb + 1) * TB)
        ps_s, ps_sb_ = psf[0]
        ps_q, ps_qb = psf[1]
        for i in range(ND):
            xh, xb = xt[i]
            sh, sb_ = scr[i % 2]
            P.emit("vector", lambda e, o=sh[:, :TB], a=xh[:, sl]: e.tensor_mul(o, a, a),
                   reads=[xb], writes=[sb_])
            P.mm(ps_s[0:1, :], ps_sb_, [(o128[:, :], xh[:, sl])], [o128b, xb], accum=(i > 0))
            P.mm(ps_q[0:1, :], ps_qb, [(o128[:, :], sh[:, :TB])], [o128b, sb_], accum=(i > 0))
        P.emit("vector", lambda e, o=m_sb[0:1, sl], i=ps_s[0:1, :]: e.tensor_scalar_mul(o, i, 1.0 / D),
               reads=[ps_sb_], writes=[m_b])
        sh, sb_ = scr[0]
        P.emit("vector", lambda e, o=sh[0:1, 0:TB], i=m_sb[0:1, sl]: e.tensor_mul(o, i, i),
               reads=[m_b], writes=[sb_])
        P.emit("vector", lambda e, o=r_sb[0:1, sl], i=ps_q[0:1, :], s=sh[0:1, 0:TB]:
               e.scalar_tensor_tensor(o, i, 1.0 / D, s, ALU.mult, ALU.subtract),
               reads=[ps_qb, sb_], writes=[r_b])
        P.emit("vector", lambda e, o=r_sb[0:1, sl], i=r_sb[0:1, sl]: e.tensor_scalar_add(o, i, eps),
               reads=[r_b], writes=[r_b])
        P.emit("scalar", lambda e, o=r_sb[0:1, sl], i=r_sb[0:1, sl]: e.activation(o, i, AF.Ln),
               reads=[r_b], writes=[r_b])
        P.emit("scalar", lambda e, o=r_sb[0:1, sl], i=r_sb[0:1, sl]: e.activation(o, i, AF.Exp, scale=-0.5),
               reads=[r_b], writes=[r_b])
        ps_m, ps_mb = psf[2]
        ps_r, ps_rb = psf[3]
        P.mm(ps_m[:, :], ps_mb, [(o1[:, :], m_sb[0:1, sl])], [o1b, m_b])
        P.mm(ps_r[:, :], ps_rb, [(o1[:, :], r_sb[0:1, sl])], [o1b, r_b])
        for i in range(ND):
            xh, xb = xt[i]
            yh, yb = yt[i]
            sh, sb_ = scr[i % 2]
            P.emit("vector", lambda e, o=sh[:, :TB], a=xh[:, sl], b=ps_m[:, :]: e.tensor_sub(o, a, b),
                   reads=[xb, ps_mb], writes=[sb_])
            if post_cols is not None:
                ch, cb, cols = post_cols
                P.emit("vector", lambda e, o=yh[:, sl], a=sh[:, :TB], s=ch[:, cols[i]:cols[i] + 1], b=ps_r[:, :]:
                       e.scalar_tensor_tensor(o, a, s, b, ALU.mult, ALU.mult),
                       reads=[sb_, cb, ps_rb], writes=[yb])
            else:
                P.emit("vector", lambda e, o=yh[:, sl], a=sh[:, :TB], b=ps_r[:, :]: e.tensor_mul(o, a, b),
                       reads=[sb_, ps_rb], writes=[yb])
    return (m_sb, m_b, r_sb, r_b)


def emit_ln_streaming(P, x_ap, x_dbuf, yt, psf, scr, xs, ones128, ones1, post_cols=None, eps=1e-5):
    o128, o128b = ones128
    o1, o1b = ones1
    xr = x_ap.rearrange("(i p) t -> i p t", p=128)
    m_sb, m_b = P.sb("lns_m", [1, 1024], F32)
    r_sb, r_b = P.sb("lns_r", [1, 1024], F32)
    for i in range(ND):
        xh, xb = xs[i % 2]
        P.dma(xh[:, :], xr[i], xb, src_buf=x_dbuf)
        for tb in range(2):
            sl = slice(tb * TB, (tb + 1) * TB)
            sh, sb_ = scr[(2 * i + tb) % 2]
            P.emit("vector", lambda e, o=sh[:, :TB], a=xh[:, sl]: e.tensor_mul(o, a, a),
                   reads=[xb], writes=[sb_])
            ps_s, ps_sb_ = psf[tb]
            ps_q, ps_qb = psf[2 + tb]
            P.mm(ps_s[0:1, :], ps_sb_, [(o128[:, :], xh[:, sl])], [o128b, xb], accum=(i > 0))
            P.mm(ps_q[0:1, :], ps_qb, [(o128[:, :], sh[:, :TB])], [o128b, sb_], accum=(i > 0))
    for tb in range(2):
        sl = slice(tb * TB, (tb + 1) * TB)
        ps_s, ps_sb_ = psf[tb]
        ps_q, ps_qb = psf[2 + tb]
        P.emit("vector", lambda e, o=m_sb[0:1, sl], i=ps_s[0:1, :]: e.tensor_scalar_mul(o, i, 1.0 / D),
               reads=[ps_sb_], writes=[m_b])
        sh, sb_ = scr[0]
        P.emit("vector", lambda e, o=sh[0:1, 0:TB], i=m_sb[0:1, sl]: e.tensor_mul(o, i, i),
               reads=[m_b], writes=[sb_])
        P.emit("vector", lambda e, o=r_sb[0:1, sl], i=ps_q[0:1, :], s=sh[0:1, 0:TB]:
               e.scalar_tensor_tensor(o, i, 1.0 / D, s, ALU.mult, ALU.subtract),
               reads=[ps_qb, sb_], writes=[r_b])
        P.emit("vector", lambda e, o=r_sb[0:1, sl], i=r_sb[0:1, sl]: e.tensor_scalar_add(o, i, eps),
               reads=[r_b], writes=[r_b])
        P.emit("scalar", lambda e, o=r_sb[0:1, sl], i=r_sb[0:1, sl]: e.activation(o, i, AF.Ln),
               reads=[r_b], writes=[r_b])
        P.emit("scalar", lambda e, o=r_sb[0:1, sl], i=r_sb[0:1, sl]: e.activation(o, i, AF.Exp, scale=-0.5),
               reads=[r_b], writes=[r_b])
    for tb in range(2):
        sl = slice(tb * TB, (tb + 1) * TB)
        ps_m, ps_mb = psf[0 + tb]
        ps_r, ps_rb = psf[2 + tb]
        P.mm(ps_m[:, :], ps_mb, [(o1[:, :], m_sb[0:1, sl])], [o1b, m_b])
        P.mm(ps_r[:, :], ps_rb, [(o1[:, :], r_sb[0:1, sl])], [o1b, r_b])
    for i in range(ND):
        xh, xb = xs[i % 2]
        P.dma(xh[:, :], xr[i], xb, src_buf=x_dbuf)
        for tb in range(2):
            sl = slice(tb * TB, (tb + 1) * TB)
            ps_m, ps_mb = psf[0 + tb]
            ps_r, ps_rb = psf[2 + tb]
            yh, yb = yt[i]
            sh, sb_ = scr[(2 * i + tb) % 2]
            P.emit("vector", lambda e, o=sh[:, :TB], a=xh[:, sl], b=ps_m[:, :]: e.tensor_sub(o, a, b),
                   reads=[xb, ps_mb], writes=[sb_])
            if post_cols is not None:
                ch, cb, cols = post_cols
                P.emit("vector", lambda e, o=yh[:, sl], a=sh[:, :TB], s=ch[:, cols[i]:cols[i] + 1], b=ps_r[:, :]:
                       e.scalar_tensor_tensor(o, a, s, b, ALU.mult, ALU.mult),
                       reads=[sb_, cb, ps_rb], writes=[yb])
            else:
                P.emit("vector", lambda e, o=yh[:, sl], a=sh[:, :TB], b=ps_r[:, :]: e.tensor_mul(o, a, b),
                       reads=[sb_, ps_rb], writes=[yb])


def _alloc_common(P):
    psf = [P.ps(f"psf{k}", [128, 512], F32) for k in range(6)]
    psb = [P.ps(f"psb{k}", [128, 1024], BF16) for k in range(2)]
    scr = [P.sb(f"scr{k}", [128, 512], F32) for k in range(2)]
    return psf, psb, scr


def emit_prep(P, wdp_t, h64_t, wdf_t):
    wdp, wdp_b = wdp_t
    h64, h64_b = h64_t
    wdf, wdf_b = wdf_t
    wdfr = wdf.rearrange("(a g j) d -> g a j d", a=64, g=64, j=2)
    h_sb, h_b = P.sb("h64sb", [128, 128], BF16)
    P.dma(h_sb[:, :], h64[:, :], h_b, src_buf=h64_b, cast=True)
    grp = [P.sb(f"grp{k}", [128, D], BF16) for k in range(2)]
    osb = [P.sb(f"osb{k}", [128, D], BF16) for k in range(2)]
    ps = [P.ps(f"pp{k}", [128, 512], F32) for k in range(2)]
    sc = 1.0 / np.sqrt(INTER)
    for g in range(64):
        gh, gb = grp[g % 2]
        oh, ob = osb[g % 2]
        P.dma(gh[:, :], wdp[g], gb, src_buf=wdp_b, cast=True)
        for j in range(4):
            sl = slice(j * 512, (j + 1) * 512)
            ph, pb = ps[j % 2]
            P.mm(ph[:, :], pb, [(h_sb[:, :], gh[:, sl])], [h_b, gb])
            P.emit("scalar", lambda e, o=oh[:, sl], i=ph[:, :]: e.mul(o, i, sc),
                   reads=[pb], writes=[ob])
        P.dma(wdfr[g], oh[:, :], wdf_b, src_buf=ob)


def emit_ffn0(P, x_t, xo_t, W):
    x_ap, x_b = x_t
    xo_ap, xo_b = xo_t
    wupT, wupT_b = W["wupT"]
    wgT, wgT_b = W["wgT"]
    wdf, wdf_b = W["wdf"]
    su_ug, su_ug_b = W["su_ug"]
    sw_up, sw_up_b = W["sw_up"]
    ws_g, ws_g_b = W["ws_g"]
    ws_d, ws_d_b = W["ws_d"]
    h128, h128_b = W["h128"]
    ones1, ones1_b = W["ones1"]
    ones128, ones128_b = W["ones128"]

    psf, psb, scr = _alloc_common(P)
    h_sb, h_b = P.sb("h128sb", [128, 128], BF16)
    P.dma(h_sb[:, :], h128[:, :], h_b, src_buf=h128_b, cast=True)
    o1_sb, o1_b = P.sb("o1", [1, 128], F32)
    P.dma(o1_sb[:, :], ones1[:, :], o1_b, src_buf=ones1_b)
    o128_sb, o128_b = P.sb("o128", [128, 1], F32)
    P.dma(o128_sb[:, :], ones128[:, :], o128_b, src_buf=ones128_b)
    suh, sub = P.sb("suh", [128, 16], F32)
    P.dma(suh[:, :], su_ug[:, :], sub, src_buf=su_ug_b)
    swh, swb = P.sb("swh", [128, 64], F32)
    P.dma(swh[:, :], sw_up[:, :], swb, src_buf=sw_up_b)
    wgh, wgb = P.sb("wsgh", [128, 64], F32)
    P.dma(wgh[:, :], ws_g[:, :], wgb, src_buf=ws_g_b)
    wdh, wdb = P.sb("wsdh", [128, 16], F32)
    P.dma(wdh[:, :], ws_d[:, :], wdb, src_buf=ws_d_b)

    xs = [P.sb(f"xs{k}", [128, TOK], F32) for k in range(2)]
    sA = [P.sb(f"sA{i}", [128, TOK], BF16) for i in range(ND)]
    sB = [P.sb(f"sB{i}", [128, TOK], BF16) for i in range(ND)]
    h2 = [P.sb(f"h2_{i}", [128, TOK], BF16) for i in range(32)]
    wu_s = [P.sb(f"wu{k}", [128, D], BF16) for k in range(2)]
    wg_s = [P.sb(f"wg{k}", [128, D], BF16) for k in range(2)]
    wd_s = [P.sb(f"wd{k}", [128, 4096], BF16) for k in range(2)]
    ug_scr = [P.sb(f"ug{k}", [128, TB], F32) for k in range(2)]
    ub = [P.sb(f"ub{k}", [128, TB], BF16) for k in range(2)]
    xn = [P.sb(f"xn{k}", [128, TOK], F32) for k in range(2)]

    emit_ln_streaming(P, x_ap, x_b, sA, psf, scr, xs, (o128_sb, o128_b), (o1_sb, o1_b),
                      post_cols=(suh, sub, list(range(16))))
    cur, nxt = sA, sB
    for st in range(4):
        h = 1 << st
        for a in range(ND):
            bpart = a ^ h
            ah, ab = cur[a]
            bh, bb = cur[bpart]
            oh, ob = nxt[a]
            op = ALU.add if (a & h) == 0 else ALU.subtract
            if op == ALU.add:
                P.emit("vector", lambda e, o=oh[:, :], p=ah[:, :], q=bh[:, :]: e.tensor_add(o, p, q),
                       reads=[ab, bb], writes=[ob])
            else:
                P.emit("vector", lambda e, o=oh[:, :], p=bh[:, :], q=ah[:, :]: e.tensor_sub(o, p, q),
                       reads=[ab, bb], writes=[ob])
        cur, nxt = nxt, cur
    for i in range(ND):
        ah, ab = cur[i]
        th, tb_ = nxt[i]
        for tb in range(2):
            sl = slice(tb * TB, (tb + 1) * TB)
            ph, pb = psf[4]
            P.mm(ph[:, :], pb, [(h_sb[:, :], ah[:, sl])], [h_b, ab])
            P.emit("scalar", lambda e, o=th[:, sl], i=ph[:, :]: e.copy(o, i),
                   reads=[pb], writes=[tb_])
    t_tiles = nxt
    wupr = wupT.rearrange("(i p) (a c) -> a p i c", p=128, c=128)
    wgr = wgT.rearrange("(i p) (a c) -> a p i c", p=128, c=128)
    wdfr = wdf.rearrange("(a p) (d c) -> d p a c", p=128, c=128)
    xr = x_ap.rearrange("(i p) t -> i p t", p=128)
    xor_ = xo_ap.rearrange("(i p) t -> i p t", p=128)
    for half in range(2):
        for a2 in range(32):
            a = half * 32 + a2
            wuh, wub = wu_s[a2 % 2]
            wgh2, wgb2 = wg_s[a2 % 2]
            P.dma(wuh[:, :], wupr[a], wub, src_buf=wupT_b, cast=True)
            P.dma(wgh2[:, :], wgr[a], wgb2, src_buf=wgT_b, cast=True)
            hh, hb = h2[a2]
            for tb in range(2):
                sl = slice(tb * TB, (tb + 1) * TB)
                pu, pub = psf[0]
                pg, pgb = psf[1]
                pairs_u = [(wuh[:, i * 128:(i + 1) * 128], t_tiles[i][0][:, sl]) for i in range(ND)]
                pairs_g = [(wgh2[:, i * 128:(i + 1) * 128], t_tiles[i][0][:, sl]) for i in range(ND)]
                rd = [wub] + [t_tiles[i][1] for i in range(ND)]
                rdg = [wgb2] + [t_tiles[i][1] for i in range(ND)]
                P.mm(pu[:, :], pub, pairs_u, rd)
                P.mm(pg[:, :], pgb, pairs_g, rdg)
                gh, gb = ug_scr[tb]
                P.emit("scalar", lambda e, o=gh[:, :], i=pg[:, :], s=wgh[:, a:a + 1]:
                       e.activation(o, i, AF.Silu, scale=s),
                       reads=[pgb, wgb], writes=[gb])
                uh, ubf = ub[tb]
                P.emit("vector", lambda e, o=uh[:, :], i=gh[:, :], s=swh[:, a:a + 1], b=pu[:, :]:
                       e.scalar_tensor_tensor(o, i, s, b, ALU.mult, ALU.mult),
                       reads=[gb, swb, pub], writes=[ubf])
                ph2, pb2 = psf[4]
                P.mm(ph2[:, :], pb2, [(h_sb[:, :], uh[:, :])], [h_b, ubf])
                P.emit("scalar", lambda e, o=hh[:, sl], i=ph2[:, :]: e.copy(o, i),
                       reads=[pb2], writes=[hb])
        for d in range(ND):
            wdh2, wdb2 = wd_s[d % 2]
            P.dma(wdh2[:, :].rearrange("p (a c) -> p a c", c=128), wdfr[d][:, half * 32:(half + 1) * 32, :],
                  wdb2, src_buf=wdf_b)
            xh, xb = xs[d % 2]
            if half == 0:
                P.dma(xh[:, :], xr[d], xb, src_buf=x_b)
            else:
                P.dma(xh[:, :], xor_[d], xb, src_buf=xo_b)
            xnh, xnb = xn[d % 2]
            for tb in range(2):
                sl = slice(tb * TB, (tb + 1) * TB)
                pd, pdb = psf[2 + tb]
                pairs = [(wdh2[:, a2 * 128:(a2 + 1) * 128], h2[a2][0][:, sl]) for a2 in range(32)]
                rd = [wdb2] + [h2[a2][1] for a2 in range(32)]
                P.mm(pd[:, :], pdb, pairs, rd)
                P.emit("vector", lambda e, o=xnh[:, sl], i=pd[:, :], s=wdh[:, d:d + 1], b=xh[:, sl]:
                       e.scalar_tensor_tensor(o, i, s, b, ALU.mult, ALU.add),
                       reads=[pdb, wdb, xb], writes=[xnb])
            P.dma(xor_[d], xnh[:, :], xo_b, src_buf=xnb)


def emit_attn0(P, x_t, xo_t, W):
    x_ap, x_b = x_t
    xo_ap, xo_b = xo_t
    bT, bT_b = W["bT"]
    cT, cT_b = W["cT"]
    dvec, dvec_b = W["dvec"]
    ones1, ones1_b = W["ones1"]
    ones128, ones128_b = W["ones128"]

    psf, psb, scr = _alloc_common(P)
    o1_sb, o1_b = P.sb("o1", [1, 128], F32)
    P.dma(o1_sb[:, :], ones1[:, :], o1_b, src_buf=ones1_b)
    o128_sb, o128_b = P.sb("o128", [128, 1], F32)
    P.dma(o128_sb[:, :], ones128[:, :], o128_b, src_buf=ones128_b)
    dh, db = P.sb("dh", [128, 16], F32)
    P.dma(dh[:, :], dvec[:, :], db, src_buf=dvec_b)

    xt = [P.sb(f"x{i}", [128, TOK], F32) for i in range(ND)]
    yt = [P.sb(f"y{i}", [128, TOK], BF16) for i in range(ND)]
    st = [P.sb(f"st{i}", [128, TOK], BF16) for i in range(8)]
    wb_s = [P.sb(f"wb{k}", [128, D], BF16) for k in range(2)]
    wc_s = [P.sb(f"wc{k}", [128, 1024], BF16) for k in range(2)]
    xn = [P.sb(f"xn{k}", [128, TOK], F32) for k in range(2)]

    xr = x_ap.rearrange("(i p) t -> i p t", p=128)
    xor_ = xo_ap.rearrange("(i p) t -> i p t", p=128)
    for i in range(ND):
        P.dma(xt[i][0][:, :], xr[i], xt[i][1], src_buf=x_b)
    emit_ln_resident(P, xt, yt, psf, scr, (o128_sb, o128_b), (o1_sb, o1_b))
    bTr = bT.rearrange("(i p) (n c) -> n p i c", p=128, c=128)
    for n in range(8):
        wh, wb = wb_s[n % 2]
        P.dma(wh[:, :], bTr[n], wb, src_buf=bT_b, cast=True)
        for tb in range(2):
            sl = slice(tb * TB, (tb + 1) * TB)
            ph, pb = psf[tb]
            pairs = [(wh[:, i * 128:(i + 1) * 128], yt[i][0][:, sl]) for i in range(ND)]
            P.mm(ph[:, :], pb, pairs, [wb] + [yt[i][1] for i in range(ND)])
            P.emit("scalar", lambda e, o=st[n][0][:, sl], i=ph[:, :]: e.activation(o, i, AF.Tanh),
                   reads=[pb], writes=[st[n][1]])
    cTr = cT.rearrange("(n p) (d c) -> d p n c", p=128, c=128)
    for d in range(ND):
        wh, wb = wc_s[d % 2]
        P.dma(wh[:, :], cTr[d], wb, src_buf=cT_b, cast=True)
        xnh, xnb = xn[d % 2]
        for tb in range(2):
            sl = slice(tb * TB, (tb + 1) * TB)
            ph, pb = psf[2 + tb]
            pairs = [(wh[:, n * 128:(n + 1) * 128], st[n][0][:, sl]) for n in range(8)]
            P.mm(ph[:, :], pb, pairs, [wb] + [st[n][1] for n in range(8)])
            sh, sb_ = scr[tb]
            P.emit("vector", lambda e, o=sh[:, :], i=yt[d][0][:, sl], s=dh[:, d:d + 1], b=ph[:, :]:
                   e.scalar_tensor_tensor(o, i, s, b, ALU.mult, ALU.add),
                   reads=[yt[d][1], db, pb], writes=[sb_])
            P.emit("vector", lambda e, o=xnh[:, sl], a=xt[d][0][:, sl], b=sh[:, :]: e.tensor_add(o, a, b),
                   reads=[xt[d][1], sb_], writes=[xnb])
        P.dma(xor_[d], xnh[:, :], xo_b, src_buf=xnb)


def emit_moe(P, x_t, xo_t, W, final=False):
    x_ap, x_b = x_t
    xo_ap, xo_b = xo_t
    wr, wr_b = W["wr"]
    br, br_b = W["br"]
    wds, wds_b = W["wds"]
    bds, bds_b = W["bds"]
    wus, wus_b = W["wus"]
    bu, bu_b = W["bu"]
    e16, e16_b = W["e16"]
    iden, iden_b = W["iden"]
    ones1, ones1_b = W["ones1"]
    ones128, ones128_b = W["ones128"]

    psf, psb, scr = _alloc_common(P)
    o1_sb, o1_b = P.sb("o1", [1, 128], F32)
    P.dma(o1_sb[:, :], ones1[:, :], o1_b, src_buf=ones1_b)
    o128_sb, o128_b = P.sb("o128", [128, 1], F32)
    P.dma(o128_sb[:, :], ones128[:, :], o128_b, src_buf=ones128_b)
    idf, idf_b = P.sb("idf", [128, 128], F32)
    P.dma(idf[:, :], iden[:, :], idf_b, src_buf=iden_b)
    idb, idb_b = P.sb("idb", [128, 128], BF16)
    P.dma(idb[:, :], iden[:, :], idb_b, src_buf=iden_b, cast=True)
    e16s, e16sb = P.sb("e16s", [8, 128], BF16)
    P.dma(e16s[:, :], e16[:, :], e16sb, src_buf=e16_b, cast=True)
    wr32, wr32b = P.sb("wr32", [128, 128], F32)
    P.dma(wr32[:, :], wr.rearrange("(i p) e -> p i e", p=128), wr32b, src_buf=wr_b)
    brs, brsb = P.sb("brs", [8, 1], F32)
    P.dma(brs[:, :], br[:, :], brsb, src_buf=br_b)
    wdss, wdssb = P.sb("wdss", [128, D], BF16)
    P.dma(wdss[:, :], wds.rearrange("(i p) c -> p i c", p=128), wdssb, src_buf=wds_b, cast=True)
    bdss, bdssb = P.sb("bdss", [128, 1], F32)
    P.dma(bdss[:, :], bds[:, :], bdssb, src_buf=bds_b)
    wuss, wussb = P.sb("wuss", [128, D], BF16)
    P.dma(wuss[:, :], wus[:, :], wussb, src_buf=wus_b, cast=True)
    bus, busb = P.sb("bus", [8, D], BF16)
    P.dma(bus[:, :], bu[:, :], busb, src_buf=bu_b, cast=True)

    xt = [P.sb(f"x{i}", [128, TOK], F32) for i in range(ND)]
    yt = [P.sb(f"y{i}", [128, TOK], BF16) for i in range(ND)]
    xr = x_ap.rearrange("(i p) t -> i p t", p=128)
    xor_ = xo_ap.rearrange("(i p) t -> i p t", p=128)
    for i in range(ND):
        P.dma(xt[i][0][:, :], xr[i], xt[i][1], src_buf=x_b)
    m_sb, m_b, r_sb, r_b = emit_ln_resident(P, xt, yt, psf, scr, (o128_sb, o128_b), (o1_sb, o1_b))

    lg, lg_b = P.sb("lg", [8, TOK], F32)
    swr, swr_b = P.sb("swr", [8, 1], F32)
    s8a, s8a_b = P.sb("s8a", [8, TB], F32)
    s8b, s8b_b = P.sb("s8b", [8, TB], F32)
    ph, pb = psf[3]
    pairs = [(wr32[:, i * 8:(i + 1) * 8], o128_sb[:, :]) for i in range(ND)]
    P.mm(ph[0:8, 0:1], pb, pairs, [wr32b, o128_b])
    P.emit("scalar", lambda e, o=swr[:, :], i=ph[0:8, 0:1]: e.copy(o, i), reads=[pb], writes=[swr_b])
    for tb in range(2):
        sl = slice(tb * TB, (tb + 1) * TB)
        ph0, pb0 = psf[0]
        pairs = [(wr32[:, i * 8:(i + 1) * 8], xt[i][0][:, sl]) for i in range(ND)]
        P.mm(ph0[0:8, :], pb0, pairs, [wr32b] + [xt[i][1] for i in range(ND)])
        ph1, pb1 = psf[1]
        P.mm(ph1[0:8, :], pb1, [(o1_sb[0:1, 0:8], m_sb[0:1, sl])], [o1_b, m_b])
        ph2, pb2 = psf[2]
        P.mm(ph2[0:8, :], pb2, [(o1_sb[0:1, 0:8], r_sb[0:1, sl])], [o1_b, r_b])
        P.emit("vector", lambda e, o=s8a[:, :], i=ph1[0:8, :], sc=swr[:, :]: e.tensor_scalar_mul(o, i, sc),
               reads=[pb1, swr_b], writes=[s8a_b])
        P.emit("vector", lambda e, o=s8b[:, :], a=ph0[0:8, :], b=s8a[:, :]: e.tensor_sub(o, a, b),
               reads=[pb0, s8a_b], writes=[s8b_b])
        P.emit("vector", lambda e, o=lg[:, sl], a=s8b[:, :], b=ph2[0:8, :]: e.tensor_mul(o, a, b),
               reads=[s8b_b, pb2], writes=[lg_b])
        P.emit("vector", lambda e, o=lg[:, sl], i=lg[:, sl], s=brs[:, :]: e.tensor_scalar_add(o, i, s),
               reads=[lg_b, brsb], writes=[lg_b])
    lt, lt_b = P.sb("lt", [128, 8], F32)
    m1, m1_b = P.sb("m1", [128, 1], F32)
    m2, m2_b = P.sb("m2", [128, 1], F32)
    msk, msk_b = P.sb("msk", [128, 8], F32)
    gts, gts_b = P.sb("gts", [128, 8], F32)
    w8, w8_b = P.sb("w8", [128, 8], BF16)
    wT, wT_b = P.sb("wT", [8, TOK], BF16)
    for blk in range(8):
        bs = slice(blk * 128, (blk + 1) * 128)
        ph, pb = psf[1]
        P.tr(ph[0:128, 0:8], pb, lg[0:8, bs], lg_b, idf[0:8, 0:8], idf_b)
        P.emit("scalar", lambda e, o=lt[:, :], i=ph[0:128, 0:8]: e.copy(o, i), reads=[pb], writes=[lt_b])
        P.emit("vector", lambda e, o=m1[:, :], i=lt[:, :]: e.reduce_max(o, i, AX.X), reads=[lt_b], writes=[m1_b])
        P.emit("vector", lambda e, o=msk[:, :], i=lt[:, :], s=m1[:, :]:
               e.tensor_scalar(o, i, s, None, ALU.is_ge), reads=[lt_b, m1_b], writes=[msk_b])
        P.emit("vector", lambda e, o=msk[:, :], i=msk[:, :], b=lt[:, :]:
               e.scalar_tensor_tensor(o, i, -1e9, b, ALU.mult, ALU.add), reads=[msk_b, lt_b], writes=[msk_b])
        P.emit("vector", lambda e, o=m2[:, :], i=msk[:, :]: e.reduce_max(o, i, AX.X), reads=[msk_b], writes=[m2_b])
        P.emit("vector", lambda e, o=msk[:, :], i=lt[:, :], s=m2[:, :]:
               e.tensor_scalar(o, i, s, None, ALU.is_ge), reads=[lt_b, m2_b], writes=[msk_b])
        P.emit("vector", lambda e, o=m1[:, :], i=m1[:, :]: e.tensor_scalar_mul(o, i, -1.0), reads=[m1_b], writes=[m1_b])
        P.emit("scalar", lambda e, o=gts[:, :], i=lt[:, :], b=m1[:, :]: e.activation(o, i, AF.Exp, bias=b),
               reads=[lt_b, m1_b], writes=[gts_b])
        P.emit("vector", lambda e, o=gts[:, :], a=gts[:, :], b=msk[:, :]: e.tensor_mul(o, a, b),
               reads=[gts_b, msk_b], writes=[gts_b])
        P.emit("vector", lambda e, o=m2[:, :], i=gts[:, :]: e.reduce_sum(o, i, AX.X), reads=[gts_b], writes=[m2_b])
        P.emit("vector", lambda e, o=m2[:, :], i=m2[:, :]: e.reciprocal(o, i), reads=[m2_b], writes=[m2_b])
        P.emit("vector", lambda e, o=w8[:, :], i=gts[:, :], s=m2[:, :]: e.tensor_scalar_mul(o, i, s),
               reads=[gts_b, m2_b], writes=[w8_b])
        ph2, pb2 = psb[0]
        P.tr(ph2[0:8, 0:128], pb2, w8[:, :], w8_b, idb[:, :], idb_b)
        P.emit("scalar", lambda e, o=wT[:, bs], i=ph2[0:8, 0:128]: e.copy(o, i), reads=[pb2], writes=[wT_b])
    wS, wS_b = P.sb("wS", [128, TOK], BF16)
    hS, hS_b = P.sb("hS", [128, TOK], BF16)
    for tb in range(2):
        sl = slice(tb * TB, (tb + 1) * TB)
        ph, pb = psf[2]
        P.mm(ph[:, :], pb, [(e16s[:, :], wT[:, sl])], [e16sb, wT_b])
        P.emit("scalar", lambda e, o=wS[:, sl], i=ph[:, :]: e.copy(o, i), reads=[pb], writes=[wS_b])
        ph2, pb2 = psf[3]
        pairs = [(wdss[:, i * 128:(i + 1) * 128], yt[i][0][:, sl]) for i in range(ND)]
        P.mm(ph2[:, :], pb2, pairs, [wdssb] + [yt[i][1] for i in range(ND)])
        P.emit("scalar", lambda e, o=hS[:, sl], i=ph2[:, :], b=bdss[:, :]: e.activation(o, i, AF.Relu, bias=b),
               reads=[pb2, bdssb], writes=[hS_b])
        P.emit("vector", lambda e, o=hS[:, sl], a=hS[:, sl], b=wS[:, sl]: e.tensor_mul(o, a, b),
               reads=[hS_b, wS_b], writes=[hS_b])
    xn = [P.sb(f"xn{k}", [128, TOK], F32) for k in range(2)]
    for d in range(ND):
        xnh, xnb = xn[d % 2]
        for tb in range(2):
            sl = slice(tb * TB, (tb + 1) * TB)
            ph, pb = psf[4 + tb]
            pairs = [(wuss[:, d * 128:(d + 1) * 128], hS[:, sl]),
                     (bus[0:8, d * 128:(d + 1) * 128], wT[0:8, sl])]
            P.mm(ph[:, :], pb, pairs, [wussb, hS_b, busb, wT_b])
            P.emit("vector", lambda e, o=xnh[:, sl], i=ph[:, :], b=xt[d][0][:, sl]:
                   e.scalar_tensor_tensor(o, i, 0.5, b, ALU.mult, ALU.add),
                   reads=[pb, xt[d][1]], writes=[xnb])
        P.dma(xor_[d], xnh[:, :], xo_b, src_buf=xnb, out=final)


def emit_gdnz(P, x_t, W, xnin_t, xnf_t, zs_t):
    x_ap, x_b = x_t
    wz, wz_b = W["wz"]
    ones1, ones1_b = W["ones1"]
    ones128, ones128_b = W["ones128"]
    xnin_ap, xnin_b = xnin_t
    xnf_ap, xnf_b = xnf_t
    zs, zs_b = zs_t

    psf, psb, scr = _alloc_common(P)
    o1_sb, o1_b = P.sb("o1", [1, 128], F32)
    P.dma(o1_sb[:, :], ones1[:, :], o1_b, src_buf=ones1_b)
    o128_sb, o128_b = P.sb("o128", [128, 1], F32)
    P.dma(o128_sb[:, :], ones128[:, :], o128_b, src_buf=ones128_b)

    xt = [P.sb(f"x{i}", [128, TOK], F32) for i in range(ND)]
    yt = [P.sb(f"y{i}", [128, TOK], BF16) for i in range(ND)]
    xr = x_ap.rearrange("(i p) t -> i p t", p=128)
    for i in range(ND):
        P.dma(xt[i][0][:, :], xr[i], xt[i][1], src_buf=x_b)
    emit_ln_resident(P, xt, yt, psf, scr, (o128_sb, o128_b), (o1_sb, o1_b))

    xnr = xnin_ap.rearrange("(i p) t -> i p t", p=128)
    for i in range(ND):
        P.dma(xnr[i], yt[i][0][:, :], xnin_b, src_buf=yt[i][1])
    P.cc("AllGather", ALU.bypass, xnin_ap, xnin_b, xnf_ap, xnf_b, "cc0")

    wq_s = [P.sb(f"wq{k}", [128, D], BF16) for k in range(2)]
    mx = [P.sb(f"mx{k}", [128, TOK], BF16) for k in range(2)]
    wzr = wz.rearrange("(i p) (o c) -> o p i c", p=128, c=128)
    for o in range(32):
        wh, wb = wq_s[o % 2]
        P.dma(wh[:, :], wzr[o], wb, src_buf=wz_b, cast=True)
        mh, mb = mx[o % 2]
        for tb in range(2):
            sl = slice(tb * TB, (tb + 1) * TB)
            ph, pb = psf[tb]
            pairs = [(wh[:, i * 128:(i + 1) * 128], yt[i][0][:, sl]) for i in range(ND)]
            P.mm(ph[:, :], pb, pairs, [wb] + [yt[i][1] for i in range(ND)])
            P.emit("scalar", lambda e, o_=mh[:, sl], i=ph[:, :]: e.activation(o_, i, AF.Silu),
                   reads=[pb], writes=[mb])
        P.dma(zs[o], mh[:, :], zs_b, src_buf=mb)


def emit_gdnpc(P, xnf_t, W, qin_t, kin_t, vin_t, beta_t, g_t):
    S = 2048
    xnf, xnf_b = xnf_t
    wqkvc, wqkvc_b = W["wqkvc"]
    wbac, wbac_b = W["wbac"]
    alogc, alogc_b = W["alogc"]
    dtbc, dtbc_b = W["dtbc"]
    qin, qin_b = qin_t
    kin, kin_b = kin_t
    vin, vin_b = vin_t
    beta_o, beta_ob = beta_t
    g_o, g_ob = g_t

    psf, psb, scr = _alloc_common(P)
    xnr = xnf.rearrange("(h i p) t -> h i p t", p=128, i=ND)
    xh = [[P.sb(f"xn{h}_{i}", [128, TOK], BF16) for i in range(ND)] for h in range(2)]
    for h in range(2):
        for i in range(ND):
            P.dma(xh[h][i][0][:, :], xnr[h][i], xh[h][i][1], src_buf=xnf_b)

    wq_s = [P.sb(f"wq{k}", [128, D], BF16) for k in range(2)]
    mx = [P.sb(f"mx{k}", [128, S], BF16) for k in range(2)]
    wqr = wqkvc.rearrange("(i p) (o c) -> o p i c", p=128, c=128)
    for o in range(32):
        wh, wb = wq_s[o % 2]
        P.dma(wh[:, :], wqr[o], wb, src_buf=wqkvc_b, cast=True)
        mh, mb = mx[o % 2]
        for h in range(2):
            for tb in range(2):
                sl = slice(tb * TB, (tb + 1) * TB)
                ph, pb = psf[2 * h + tb]
                pairs = [(wh[:, i * 128:(i + 1) * 128], xh[h][i][0][:, sl]) for i in range(ND)]
                P.mm(ph[:, :], pb, pairs, [wb] + [xh[h][i][1] for i in range(ND)])
                P.emit("scalar",
                       lambda e, o_=mh[:, h * TOK + tb * TB:h * TOK + (tb + 1) * TB], i=ph[:, :]: e.copy(o_, i),
                       reads=[pb], writes=[mb])
        if o < 8:
            dst, dstb = qin[o], qin_b
        elif o < 16:
            dst, dstb = kin[o - 8], kin_b
        else:
            dst, dstb = vin[o - 16], vin_b
        P.dma(dst, mh[:, :], dstb, src_buf=mb)
    wbas, wbasb = P.sb("wbas", [128, 512], BF16)
    P.dma(wbas[:, :], wbac.rearrange("(i p) e -> p i e", p=128), wbasb, src_buf=wbac_b, cast=True)
    blog, blog_b = P.sb("blog", [16, S], F32)
    alg, alg_b = P.sb("alg", [16, S], F32)
    for h in range(2):
        for tb in range(2):
            sl = slice(tb * TB, (tb + 1) * TB)
            cs = slice(h * TOK + tb * TB, h * TOK + (tb + 1) * TB)
            ph, pb = psf[4]
            pairs = [(wbas[:, i * 32:i * 32 + 16], xh[h][i][0][:, sl]) for i in range(ND)]
            P.mm(ph[0:16, :], pb, pairs, [wbasb] + [xh[h][i][1] for i in range(ND)])
            P.emit("scalar", lambda e, o=blog[:, cs], i=ph[0:16, :]: e.copy(o, i),
                   reads=[pb], writes=[blog_b])
            ph2, pb2 = psf[5]
            pairs = [(wbas[:, i * 32 + 16:i * 32 + 32], xh[h][i][0][:, sl]) for i in range(ND)]
            P.mm(ph2[0:16, :], pb2, pairs, [wbasb] + [xh[h][i][1] for i in range(ND)])
            P.emit("scalar", lambda e, o=alg[:, cs], i=ph2[0:16, :]: e.copy(o, i),
                   reads=[pb2], writes=[alg_b])
    bt, bt_b = P.sb("bt", [16, S], F32)
    P.emit("scalar", lambda e, o=bt[:, :], i=blog[:, :]: e.activation(o, i, AF.Sigmoid),
           reads=[blog_b], writes=[bt_b])
    P.dma(beta_o[:, :], bt[:, :], beta_ob, src_buf=bt_b)
    ad, ad_b = P.sb("ad", [16, S], F32)
    dtbs, dtbsb = P.sb("dtbs", [16, 1], F32)
    P.dma(dtbs[:, :], dtbc[:, :], dtbsb, src_buf=dtbc_b)
    P.emit("vector", lambda e, o=ad[:, :], i=alg[:, :], s=dtbs[:, :]: e.tensor_scalar_add(o, i, s),
           reads=[alg_b, dtbsb], writes=[ad_b])
    t1, t1_b = P.sb("spt1", [16, S], F32)
    t2, t2_b = P.sb("spt2", [16, S], F32)
    P.emit("scalar", lambda e, o=t1[:, :], i=ad[:, :]: e.activation(o, i, AF.Abs),
           reads=[ad_b], writes=[t1_b])
    P.emit("scalar", lambda e, o=t1[:, :], i=t1[:, :]: e.activation(o, i, AF.Exp, scale=-1.0),
           reads=[t1_b], writes=[t1_b])
    P.emit("vector", lambda e, o=t1[:, :], i=t1[:, :]: e.tensor_scalar_add(o, i, 1.0),
           reads=[t1_b], writes=[t1_b])
    P.emit("scalar", lambda e, o=t1[:, :], i=t1[:, :]: e.activation(o, i, AF.Ln),
           reads=[t1_b], writes=[t1_b])
    P.emit("vector", lambda e, o=t2[:, :], i=ad[:, :]: e.tensor_scalar_max(o, i, 0.0),
           reads=[ad_b], writes=[t2_b])
    P.emit("vector", lambda e, o=t2[:, :], a=t2[:, :], b=t1[:, :]: e.tensor_add(o, a, b),
           reads=[t2_b, t1_b], writes=[t2_b])
    nea, nea_b = P.sb("nea", [16, 1], F32)
    P.dma(nea[:, :], alogc[:, :], nea_b, src_buf=alogc_b)
    P.emit("scalar", lambda e, o=nea[:, :], i=nea[:, :]: e.activation(o, i, AF.Exp), reads=[nea_b], writes=[nea_b])
    P.emit("vector", lambda e, o=nea[:, :], i=nea[:, :]: e.tensor_scalar_mul(o, i, -1.0), reads=[nea_b], writes=[nea_b])
    P.emit("vector", lambda e, o=t2[:, :], i=t2[:, :], s=nea[:, :]: e.tensor_scalar_mul(o, i, s),
           reads=[t2_b, nea_b], writes=[t2_b])
    for h in range(2):
        P.emit("vector", lambda e, o=t1[:, h * TOK:(h + 1) * TOK], a=t2[:, h * TOK:(h + 1) * TOK],
               b=t2[:, h * TOK:(h + 1) * TOK]:
               e.tensor_tensor_scan(o, a, b, 0.0, ALU.add, ALU.bypass),
               reads=[t2_b], writes=[t1_b])
    P.dma(g_o[:, :], t1[:, :], g_ob, src_buf=t1_b)


def emit_gdnchunk(P, W, qin_t, kin_t, vin_t, beta_t, g_t, rsin_t):
    S = 2048
    NCH = S // CC
    qin, qin_b = qin_t
    kin, kin_b = kin_t
    vin, vin_b = vin_t
    beta_i, beta_ib = beta_t
    g_i, g_ib = g_t
    rsin_ap, rsin_b = rsin_t
    cwq, cwq_b = W["cwq"]
    cwk, cwk_b = W["cwk"]
    cwv, cwv_b = W["cwv"]
    monorm, monorm_b = W["monorm"]
    nw_i, nw_ib = W["nw"]
    maskw, maskw_b = W["maskw"]
    maskat, maskat_b = W["maskat"]
    iden, iden_b = W["iden"]
    ones1, ones1_b = W["ones1"]
    ones128, ones128_b = W["ones128"]

    psf = [P.ps(f"psf{k}", [128, 512], F32) for k in range(5)]
    psb = [P.ps(f"psb{k}", [128, 1024], BF16) for k in range(3)]
    mk_sb, mk_b = P.sb("mks", [128, 2], F32)
    P.dma(mk_sb[:, :], monorm[:, :], mk_b, src_buf=monorm_b)
    mscr = [P.sb(f"mscr{k}", [128, S], BF16) for k in range(2)]
    rsv = rsin_ap.rearrange("(j r) p t -> r p j t", j=2)
    o1_sb, o1_b = P.sb("o1", [1, 128], F32)
    P.dma(o1_sb[:, :], ones1[:, :], o1_b, src_buf=ones1_b)
    o128_sb, o128_b = P.sb("o128", [128, 1], F32)
    P.dma(o128_sb[:, :], ones128[:, :], o128_b, src_buf=ones128_b)
    idf, idf_b = P.sb("idf", [128, 128], F32)
    P.dma(idf[:, :], iden[:, :], idf_b, src_buf=iden_b)
    idb, idb_b = P.sb("idb", [128, 128], BF16)
    P.dma(idb[:, :], iden[:, :], idb_b, src_buf=iden_b, cast=True)
    mw_sb, mw_b = P.sb("mw", [128, 128], F32)
    P.dma(mw_sb[:, :], maskw[:, :], mw_b, src_buf=maskw_b)
    ma_sb, ma_b = P.sb("ma", [128, 128], F32)
    P.dma(ma_sb[:, :], maskat[:, :], ma_b, src_buf=maskat_b)
    bsb, bsb_b = P.sb("bsb", [16, S], F32)
    P.dma(bsb[:, :], beta_i[:, :], bsb_b, src_buf=beta_ib)
    gsb, gsb_b = P.sb("gsb", [16, S], F32)
    P.dma(gsb[:, :], g_i[:, :], gsb_b, src_buf=g_ib)
    nws, nws_b = P.sb("nws", [1, 128], F32)
    P.dma(nws[:, :], nw_i[:, :], nws_b, src_buf=nw_ib)
    nwb, nwb_b = P.sb("nwb", [128, 128], F32)
    ph, pb = psf[0]
    P.mm(ph[:, 0:128], pb, [(o1_sb[:, :], nws[:, :])], [o1_b, nws_b])
    P.emit("scalar", lambda e, o=nwb[:, :], i=ph[:, 0:128]: e.copy(o, i), reads=[pb], writes=[nwb_b])

    cwq_s, cwq_sb = P.sb("cwqs", [128, 32], F32)
    P.dma(cwq_s[:, :].rearrange("p (i j) -> p i j", j=4), cwq.rearrange("i p j -> p i j"), cwq_sb, src_buf=cwq_b)
    cwk_s, cwk_sb = P.sb("cwks", [128, 32], F32)
    P.dma(cwk_s[:, :].rearrange("p (i j) -> p i j", j=4), cwk.rearrange("i p j -> p i j"), cwk_sb, src_buf=cwk_b)
    cwv_s, cwv_sb = P.sb("cwvs", [128, 64], F32)
    P.dma(cwv_s[:, :].rearrange("p (i j) -> p i j", j=4), cwv.rearrange("i p j -> p i j"), cwv_sb, src_buf=cwv_b)
    pad = [P.sb(f"pad{k}", [128, S + 3], BF16) for k in range(2)]
    acc = [P.sb(f"acc{k}", [128, 512], F32) for k in range(2)]
    sqs, sqs_b = P.sb("sqs", [128, 512], F32)
    rsr, rsr_b = P.sb("rsr", [1, S], F32)
    qt = [P.sb(f"qt{i}", [128, S], BF16) for i in range(8)]
    kt = [P.sb(f"kt{i}", [128, S], BF16) for i in range(8)]
    vt = [P.sb(f"vt{i}", [128, S], BF16) for i in range(16)]

    def conv_tile(dst, dst_b, src_ap, src_b, cw_ap, cw_b, idx, norm_scale):
        ph_, pb_ = pad[idx % 2]
        P.emit("vector", lambda e, o=ph_[:, 0:3]: e.memset(o, 0.0), writes=[pb_])
        P.dma(ph_[:, 3:S + 3], src_ap, pb_, src_buf=src_b)
        for t4 in range(4):
            off = t4 * 512
            sl = slice(off, off + 512)
            ah, ab = acc[t4 % 2]
            P.emit("vector", lambda e, o=ah[:, :], i=ph_[:, off:off + 512], s=cw_ap[:, 0:1]:
                   e.tensor_scalar_mul(o, i, s),
                   reads=[pb_, cw_b], writes=[ab])
            for j in range(1, 4):
                P.emit("vector", lambda e, o=ah[:, :], i=ph_[:, off + j:off + j + 512], s=cw_ap[:, j:j + 1], b=ah[:, :]:
                       e.scalar_tensor_tensor(o, i, s, b, ALU.mult, ALU.add),
                       reads=[pb_, cw_b, ab], writes=[ab])
            if norm_scale is None:
                P.emit("scalar", lambda e, o=dst[:, sl], i=ah[:, :]: e.activation(o, i, AF.Silu),
                       reads=[ab], writes=[dst_b])
                continue
            P.emit("scalar", lambda e, o=ah[:, :], i=ah[:, :]: e.activation(o, i, AF.Silu),
                   reads=[ab], writes=[ab])
            P.emit("vector", lambda e, o=sqs[:, :], a=ah[:, :]: e.tensor_mul(o, a, a), reads=[ab], writes=[sqs_b])
            ph2, pb2 = psf[t4 % 2]
            P.mm(ph2[0:1, :], pb2, [(o128_sb[:, :], sqs[:, :])], [o128_b, sqs_b])
            P.emit("vector", lambda e, o=rsr[0:1, sl], i=ph2[0:1, :]: e.tensor_scalar_add(o, i, 1e-6),
                   reads=[pb2], writes=[rsr_b])
            P.emit("scalar", lambda e, o=rsr[0:1, sl], i=rsr[0:1, sl]: e.activation(o, i, AF.Ln),
                   reads=[rsr_b], writes=[rsr_b])
            P.emit("scalar", lambda e, o=rsr[0:1, sl], i=rsr[0:1, sl]: e.activation(o, i, AF.Exp, scale=-0.5),
                   reads=[rsr_b], writes=[rsr_b])
            ph3, pb3 = psf[2 + t4 % 2]
            P.mm(ph3[:, :], pb3, [(o1_sb[:, :], rsr[0:1, sl])], [o1_b, rsr_b])
            P.emit("vector", lambda e, o=dst[:, sl], i=ah[:, :], b=ph3[:, :]:
                   e.scalar_tensor_tensor(o, i, norm_scale, b, ALU.mult, ALU.mult),
                   reads=[ab, pb3], writes=[dst_b])

    for i in range(8):
        conv_tile(qt[i][0], qt[i][1], qin[i], qin_b, cwq_s[:, i * 4:(i + 1) * 4], cwq_sb, i, float(DK ** -0.5))
    for i in range(8):
        conv_tile(kt[i][0], kt[i][1], kin[i], kin_b, cwk_s[:, i * 4:(i + 1) * 4], cwk_sb, i, 1.0)
    for i in range(16):
        conv_tile(vt[i][0], vt[i][1], vin[i], vin_b, cwv_s[:, i * 4:(i + 1) * 4], cwv_sb, i, None)

    gbt = [P.sb(f"gbt{c}", [128, 32], F32) for c in range(NCH)]
    npeb = [P.sb(f"npeb{c}", [128, 16], F32) for c in range(NCH)]
    ebt = [P.sb(f"ebt{c}", [128, 16], F32) for c in range(NCH)]
    acb = [P.sb(f"acb{c}", [128, 16], F32) for c in range(NCH)]
    rowt, rowt_b = P.sb("rowt", [1, 16], F32)
    for c in range(NCH):
        cs = slice(c * CC, (c + 1) * CC)
        gh, gb = gbt[c]
        ph1, pb1 = psf[0]
        P.tr(ph1[0:128, 0:16], pb1, gsb[0:16, cs], gsb_b, idf[0:16, 0:16], idf_b)
        P.emit("scalar", lambda e, o=gh[:, 0:16], i=ph1[0:128, 0:16]: e.copy(o, i), reads=[pb1], writes=[gb])
        ph2, pb2 = psf[1]
        P.tr(ph2[0:128, 0:16], pb2, bsb[0:16, cs], bsb_b, idf[0:16, 0:16], idf_b)
        P.emit("scalar", lambda e, o=gh[:, 16:32], i=ph2[0:128, 0:16]: e.copy(o, i), reads=[pb2], writes=[gb])
        nh, nb = npeb[c]
        if c % (NCH // 2) == 0:
            P.emit("vector", lambda e, o=nh[:, :]: e.memset(o, 0.0), writes=[nb])
        else:
            ph3, pb3 = psf[2]
            P.tr(ph3[0:1, 0:16], pb3, gsb[0:16, c * CC - 1:c * CC], gsb_b, idf[0:16, 0:16], idf_b)
            P.emit("scalar", lambda e, o=rowt[:, :], i=ph3[0:1, 0:16]: e.copy(o, i), reads=[pb3], writes=[rowt_b])
            ph4, pb4 = psf[3]
            P.mm(ph4[:, 0:16], pb4, [(o1_sb[:, :], rowt[:, :])], [o1_b, rowt_b])
            P.emit("scalar", lambda e, o=nh[:, :], i=ph4[:, 0:16]: e.mul(o, i, -1.0), reads=[pb4], writes=[nb])
        eh, eb = ebt[c]
        ph3, pb3 = psf[2]
        P.tr(ph3[0:1, 0:16], pb3, gsb[0:16, (c + 1) * CC - 1:(c + 1) * CC], gsb_b, idf[0:16, 0:16], idf_b)
        P.emit("scalar", lambda e, o=rowt[:, :], i=ph3[0:1, 0:16]: e.copy(o, i), reads=[pb3], writes=[rowt_b])
        ph4, pb4 = psf[3]
        P.mm(ph4[:, 0:16], pb4, [(o1_sb[:, :], rowt[:, :])], [o1_b, rowt_b])
        P.emit("scalar", lambda e, o=eh[:, :], i=ph4[:, 0:16]: e.copy(o, i), reads=[pb4], writes=[eb])
        ah2, ab2 = acb[c]
        P.emit("vector", lambda e, o=ah2[:, :], a=eh[:, :], b=nh[:, :]: e.tensor_add(o, a, b),
               reads=[eb, nb], writes=[ab2])
        P.emit("scalar", lambda e, o=ah2[:, :], i=ah2[:, :]: e.activation(o, i, AF.Exp), reads=[ab2], writes=[ab2])

    cgrow, cgrow_b = P.sb("cgrow", [1, 128], F32)
    cgb, cgb_b = P.sb("cgb", [128, 128], F32)
    t1w, t1w_b = P.sb("t1w", [128, 128], F32)
    ew, ew_b = P.sb("ew", [128, 128], F32)
    kk_sb, kk_b = P.sb("kk", [128, 128], BF16)
    pqk_sb, pqk_b = P.sb("pqk", [128, 128], BF16)
    kn_sb, kn_b = P.sb("kn", [128, 128], BF16)
    vn_sb, vn_b = P.sb("vn", [128, 128], BF16)
    w_sb, w_b = P.sb("wm", [128, 128], BF16)
    wt_sb, wt_b = P.sb("wtm", [128, 128], BF16)
    pw = [P.sb(f"pw{k}", [128, 128], BF16) for k in range(2)]
    pts = [P.sb(f"pts{k}", [128, 128], BF16) for k in range(7)]
    at_sb, at_b = P.sb("atm", [128, 128], BF16)
    rr, rr_b = P.sb("rr", [128, 256], BF16)
    uu = [P.sb(f"uu{k}", [128, 256], BF16) for k in range(2)]
    ukt_sb, ukt_b = P.sb("ukt", [128, 128], BF16)
    tc_sb, tc_b = P.sb("tc", [128, 128], BF16)
    alc, alc_b = P.sb("alc", [128, 1], F32)
    abc, abc_b = P.sb("abc", [128, 1], F32)
    ksc, ksc_b = P.sb("ksc", [128, 1], F32)
    ktl, ktl_b = P.sb("ktl", [128, 128], BF16)
    of_sb, of_b = P.sb("of", [128, 128], F32)
    on_sb, on_b = P.sb("on", [128, 128], BF16)
    ovar, ovar_b = P.sb("ovar", [128, 1], F32)
    st32 = [P.sb(f"st32_{k}", [128, 128], F32) for k in range(2)]
    stbf = [P.sb(f"stbf_{k}", [128, 128], BF16) for k in range(2)]
    ohead = [P.sb(f"oh{k}", [128, S], BF16) for k in range(2)]

    for kh in range(8):
        qh, qb = qt[kh]
        khh, khb = kt[kh]
        for sub in range(2):
            sh, sb_ = st32[sub]
            P.emit("vector", lambda e, o=sh[:, :]: e.memset(o, 0.0), writes=[sb_])
            bh, bb = stbf[sub]
            P.emit("vector", lambda e, o=bh[:, :]: e.memset(o, 0.0), writes=[bb])
        for c in range(NCH):
            cs = slice(c * CC, (c + 1) * CC)
            ph1, pb1 = psf[0]
            P.mm(ph1[:, 0:128], pb1, [(khh[:, cs], khh[:, cs])], [khb])
            P.emit("scalar", lambda e, o=kk_sb[:, :], i=ph1[:, 0:128]: e.copy(o, i), reads=[pb1], writes=[kk_b])
            ph2, pb2 = psf[1]
            P.mm(ph2[:, 0:128], pb2, [(khh[:, cs], qh[:, cs])], [khb, qb])
            P.emit("scalar", lambda e, o=pqk_sb[:, :], i=ph2[:, 0:128]: e.copy(o, i), reads=[pb2], writes=[pqk_b])
            pb_h, pb_b = psb[0]
            P.tr(pb_h[:, 0:128], pb_b, khh[:, cs], khb, idb[:, :], idb_b)
            P.emit("scalar", lambda e, o=kn_sb[:, :], i=pb_h[:, 0:128]: e.copy(o, i), reads=[pb_b], writes=[kn_b])
            for sub in range(2):
                h = 2 * kh + sub
                vh_, vb_ = vt[h]
                sh32, sb32 = st32[sub]
                shbf, sbbf = stbf[sub]
                gcol = gbt[c][0][:, h:h + 1]
                gcol_b = gbt[c][1]
                bcol = gbt[c][0][:, 32 - 16 + h:32 - 16 + h + 1] if False else gbt[c][0][:, 16 + h:16 + h + 1]
                ph3, pb3 = psf[2]
                P.tr(ph3[0:1, 0:128], pb3, gcol, gcol_b, idf[:, :], idf_b)
                P.emit("scalar", lambda e, o=cgrow[:, :], i=ph3[0:1, 0:128]: e.copy(o, i), reads=[pb3], writes=[cgrow_b])
                ph4, pb4 = psf[3]
                P.mm(ph4[:, 0:128], pb4, [(o1_sb[:, :], cgrow[:, :])], [o1_b, cgrow_b])
                P.emit("scalar", lambda e, o=cgb[:, :], i=ph4[:, 0:128]: e.copy(o, i), reads=[pb4], writes=[cgb_b])
                P.emit("vector", lambda e, o=t1w[:, :], i=cgb[:, :], s=gcol, b=mw_sb[:, :]:
                       e.scalar_tensor_tensor(o, i, s, b, ALU.subtract, ALU.add),
                       reads=[cgb_b, gcol_b, mw_b], writes=[t1w_b])
                P.emit("scalar", lambda e, o=ew[:, :], i=t1w[:, :]: e.activation(o, i, AF.Exp, scale=-1.0),
                       reads=[t1w_b], writes=[ew_b])
                P.emit("vector", lambda e, o=w_sb[:, :], i=ew[:, :], s=bcol, b=kk_sb[:, :]:
                       e.scalar_tensor_tensor(o, i, s, b, ALU.mult, ALU.mult),
                       reads=[ew_b, gcol_b, kk_b], writes=[w_b])
                pbh, pbb = psb[1]
                P.tr(pbh[:, 0:128], pbb, w_sb[:, :], w_b, idb[:, :], idb_b)
                P.emit("scalar", lambda e, o=wt_sb[:, :], i=pbh[:, 0:128]: e.copy(o, i), reads=[pbb], writes=[wt_b])
                P.emit("vector", lambda e, o=t1w[:, :], i=cgb[:, :], s=gcol, b=ma_sb[:, :]:
                       e.scalar_tensor_tensor(o, i, s, b, ALU.subtract, ALU.subtract),
                       reads=[cgb_b, gcol_b, ma_b], writes=[t1w_b])
                P.emit("scalar", lambda e, o=ew[:, :], i=t1w[:, :]: e.activation(o, i, AF.Exp),
                       reads=[t1w_b], writes=[ew_b])
                P.emit("vector", lambda e, o=at_sb[:, :], a=ew[:, :], b=pqk_sb[:, :]: e.tensor_mul(o, a, b),
                       reads=[ew_b, pqk_b], writes=[at_b])
                P.emit("scalar", lambda e, o=alc[:, :], i=gcol, b=npeb[c][0][:, h:h + 1]:
                       e.activation(o, i, AF.Exp, bias=b),
                       reads=[gcol_b, npeb[c][1]], writes=[alc_b])
                P.emit("vector", lambda e, o=abc[:, :], a=alc[:, :], b=bcol: e.tensor_mul(o, a, b),
                       reads=[alc_b, gcol_b], writes=[abc_b])
                P.emit("scalar", lambda e, o=ksc[:, :], i=gcol, b=ebt[c][0][:, h:h + 1]:
                       e.activation(o, i, AF.Exp, scale=-1.0, bias=b),
                       reads=[gcol_b, ebt[c][1]], writes=[ksc_b])
                pbh2, pbb2 = psb[2]
                P.tr(pbh2[:, 0:128], pbb2, vh_[:, cs], vb_, idb[:, :], idb_b)
                P.emit("scalar", lambda e, o=vn_sb[:, :], i=pbh2[:, 0:128]: e.copy(o, i), reads=[pbb2], writes=[vn_b])
                P.emit("vector", lambda e, o=rr[:, 0:128], i=vn_sb[:, :], s=bcol: e.tensor_scalar_mul(o, i, s),
                       reads=[vn_b, gcol_b], writes=[rr_b])
                P.emit("vector", lambda e, o=rr[:, 128:256], i=kn_sb[:, :], s=abc[:, :]: e.tensor_scalar_mul(o, i, s),
                       reads=[kn_b, abc_b], writes=[rr_b])
                P.emit("vector", lambda e, o=pw[0][0][:, :], i=w_sb[:, :]: e.tensor_copy(o, i),
                       reads=[w_b], writes=[pw[0][1]])
                P.emit("vector", lambda e, o=pts[0][0][:, :], i=wt_sb[:, :]: e.tensor_copy(o, i),
                       reads=[wt_b], writes=[pts[0][1]])
                for lvl in range(6):
                    src_p, src_pb = pw[lvl % 2]
                    dst_p, dst_pb = pw[(lvl + 1) % 2]
                    pt_s, pt_sb = pts[lvl]
                    pt_d, pt_db = pts[lvl + 1]
                    pbh3, pbb3 = psf[0]
                    if lvl < 5:
                        P.mm(pbh3[:, 0:128], pbb3, [(pt_s[:, :], src_p[:, :])], [pt_sb, src_pb])
                        P.emit("scalar", lambda e, o=dst_p[:, :], i=pbh3[:, 0:128]: e.copy(o, i),
                               reads=[pbb3], writes=[dst_pb])
                    pbh4, pbb4 = psf[1]
                    P.mm(pbh4[:, 0:128], pbb4, [(src_p[:, :], pt_s[:, :])], [src_pb, pt_sb])
                    P.emit("scalar", lambda e, o=pt_d[:, :], i=pbh4[:, 0:128]: e.copy(o, i),
                           reads=[pbb4], writes=[pt_db])
                P.emit("vector", lambda e, o=uu[0][0][:, :], i=rr[:, :]: e.tensor_copy(o, i),
                       reads=[rr_b], writes=[uu[0][1]])
                cu = 0
                for i in [6, 5, 4, 3, 2, 1]:
                    ph5, pb5 = psf[4]
                    P.mm(ph5[:, 0:256], pb5, [(pts[i][0][:, :], uu[cu][0][:, :])], [pts[i][1], uu[cu][1]])
                    P.emit("vector", lambda e, o=uu[1 - cu][0][:, :], a=uu[cu][0][:, :], b=ph5[:, 0:256]:
                           e.tensor_add(o, a, b), reads=[uu[cu][1], pb5], writes=[uu[1 - cu][1]])
                    cu = 1 - cu
                ph5, pb5 = psf[4]
                P.mm(ph5[:, 0:256], pb5, [(pts[0][0][:, :], uu[cu][0][:, :])], [pts[0][1], uu[cu][1]])
                P.emit("vector", lambda e, o=uu[1 - cu][0][:, :], a=uu[cu][0][:, :], b=ph5[:, 0:256]:
                       e.tensor_sub(o, a, b), reads=[uu[cu][1], pb5], writes=[uu[1 - cu][1]])
                cu = 1 - cu
                uh, ub_ = uu[cu]
                pbh5, pbb5 = psb[0]
                P.tr(pbh5[:, 0:128], pbb5, uh[:, 128:256], ub_, idb[:, :], idb_b)
                P.emit("scalar", lambda e, o=ukt_sb[:, :], i=pbh5[:, 0:128]: e.copy(o, i),
                       reads=[pbb5], writes=[ukt_b])
                ph6, pb6 = psf[0]
                P.mm(ph6[:, 0:128], pb6, [(ukt_sb[:, :], shbf[:, :])], [ukt_b, sbbf])
                P.emit("vector", lambda e, o=tc_sb[:, :], a=uh[:, 0:128], b=ph6[:, 0:128]:
                       e.tensor_sub(o, a, b), reads=[ub_, pb6], writes=[tc_b])
                ph7, pb7 = psf[1]
                P.mm(ph7[:, 0:128], pb7, [(qh[:, cs], shbf[:, :])], [qb, sbbf])
                P.emit("vector", lambda e, o=of_sb[:, :], i=ph7[:, 0:128], s=alc[:, :]:
                       e.tensor_scalar_mul(o, i, s), reads=[pb7, alc_b], writes=[of_b])
                ph8, pb8 = psf[2]
                P.mm(ph8[:, 0:128], pb8, [(at_sb[:, :], tc_sb[:, :])], [at_b, tc_b])
                P.emit("vector", lambda e, o=of_sb[:, :], a=of_sb[:, :], b=ph8[:, 0:128]:
                       e.tensor_add(o, a, b), reads=[of_b, pb8], writes=[of_b])
                P.emit("vector", lambda e, o=on_sb[:, :], a=of_sb[:, :], b=of_sb[:, :]: e.tensor_mul(o, a, b),
                       reads=[of_b], writes=[on_b])
                P.emit("vector", lambda e, o=ew[:, :], a=of_sb[:, :], b=of_sb[:, :]: e.tensor_mul(o, a, b),
                       reads=[of_b], writes=[ew_b])
                P.emit("vector", lambda e, o=ovar[:, :], i=ew[:, :]: e.reduce_sum(o, i, AX.X),
                       reads=[ew_b], writes=[ovar_b])
                P.emit("vector", lambda e, o=ovar[:, :], i=ovar[:, :]: e.tensor_scalar(o, i, 1.0 / DV, 1e-6, ALU.mult, ALU.add),
                       reads=[ovar_b], writes=[ovar_b])
                P.emit("scalar", lambda e, o=ovar[:, :], i=ovar[:, :]: e.activation(o, i, AF.Ln),
                       reads=[ovar_b], writes=[ovar_b])
                P.emit("scalar", lambda e, o=ovar[:, :], i=ovar[:, :]: e.activation(o, i, AF.Exp, scale=-0.5),
                       reads=[ovar_b], writes=[ovar_b])
                P.emit("vector", lambda e, o=on_sb[:, :], i=of_sb[:, :], s=ovar[:, :], b=nwb[:, :]:
                       e.scalar_tensor_tensor(o, i, s, b, ALU.mult, ALU.mult),
                       reads=[of_b, ovar_b, nwb_b], writes=[on_b])
                oh, ob = ohead[sub]
                pbh6, pbb6 = psb[2]
                P.tr(pbh6[:, 0:128], pbb6, on_sb[:, :], on_b, idb[:, :], idb_b)
                P.emit("scalar", lambda e, o=oh[:, cs], i=pbh6[:, 0:128]: e.copy(o, i),
                       reads=[pbb6], writes=[ob])
                P.emit("vector", lambda e, o=ktl[:, :], i=kn_sb[:, :], s=ksc[:, :]: e.tensor_scalar_mul(o, i, s),
                       reads=[kn_b, ksc_b], writes=[ktl_b])
                ph9, pb9 = psf[3]
                P.mm(ph9[:, 0:128], pb9, [(ktl[:, :], tc_sb[:, :])], [ktl_b, tc_b])
                P.emit("vector", lambda e, o=sh32[:, :], i=sh32[:, :], s=acb[c][0][:, h:h + 1], b=ph9[:, 0:128]:
                       e.scalar_tensor_tensor(o, i, s, b, ALU.mult, ALU.add),
                       reads=[sb32, acb[c][1], pb9], writes=[sb32])
                P.emit("vector", lambda e, o=shbf[:, :], i=sh32[:, :]: e.tensor_copy(o, i),
                       reads=[sb32], writes=[sbbf])
        for sub in range(2):
            h = 2 * kh + sub
            oh, ob = ohead[sub]
            for blk in range(2):
                mh2, mb2 = mscr[blk]
                P.emit("vector", lambda e, o=mh2[:, :], i=oh[:, :], s=mk_sb[:, blk:blk + 1]:
                       e.tensor_scalar_mul(o, i, s), reads=[ob, mk_b], writes=[mb2])
                P.dma(rsv[16 * blk + h], mh2[:, :].rearrange("p (j t) -> p j t", j=2),
                      rsin_b, src_buf=mb2)


def emit_gdnout(P, x_t, xo_t, onorm_t, zs_t, wout_t):
    x_ap, x_b = x_t
    xo_ap, xo_b = xo_t
    onorm, onorm_b = onorm_t
    zs, zs_b = zs_t
    wout, wout_b = wout_t

    psf, psb, scr = _alloc_common(P)
    xt = [P.sb(f"x{k}", [128, TOK], F32) for k in range(2)]
    xn = [P.sb(f"xn{k}", [128, TOK], F32) for k in range(2)]
    of = [P.sb(f"of{i}", [128, TOK], BF16) for i in range(32)]
    zt = [P.sb(f"zt{k}", [128, TOK], BF16) for k in range(2)]
    for h in range(32):
        zh, zb = zt[h % 2]
        P.dma(zh[:, :], zs[h], zb, src_buf=zs_b)
        oh, ob = of[h]
        P.dma(oh[:, :], onorm[h], ob, src_buf=onorm_b)
        P.emit("vector", lambda e, o=oh[:, :], a=oh[:, :], b=zh[:, :]: e.tensor_mul(o, a, b),
               reads=[ob, zb], writes=[ob])
    wo_s = [P.sb(f"wo{k}", [128, 4096], BF16) for k in range(2)]
    wor = wout.rearrange("(h p) (d c) -> d p h c", p=128, c=128)
    xr = x_ap.rearrange("(i p) t -> i p t", p=128)
    xor_ = xo_ap.rearrange("(i p) t -> i p t", p=128)
    for d in range(ND):
        wh, wb = wo_s[d % 2]
        P.dma(wh[:, :], wor[d], wb, src_buf=wout_b, cast=True)
        xh, xb = xt[d % 2]
        P.dma(xh[:, :], xr[d], xb, src_buf=x_b)
        xnh, xnb = xn[d % 2]
        for tb in range(2):
            sl = slice(tb * TB, (tb + 1) * TB)
            ph, pb = psf[2 + tb]
            pairs = [(wh[:, hh * 128:(hh + 1) * 128], of[hh][0][:, sl]) for hh in range(32)]
            P.mm(ph[:, :], pb, pairs, [wb] + [of[hh][1] for hh in range(32)])
            P.emit("vector", lambda e, o=xnh[:, sl], a=xh[:, sl], b=ph[:, :]: e.tensor_add(o, a, b),
                   reads=[xb, pb], writes=[xnb])
        P.dma(xor_[d], xnh[:, :], xo_b, src_buf=xnb)


def _f32(a):
    return np.ascontiguousarray(a, dtype=np.float32)

def kernel(**inp):
    C = host_consts()

    x = _f32(inp["L0_attn_x"])
    xs = []
    for c in range(NCORE):
        b, hf = c // 2, c % 2
        xs.append(np.ascontiguousarray(x[b, hf * TOK:(hf + 1) * TOK, :].T))

    WupT = np.ascontiguousarray(_f32(inp["L0_ffn_W_up"]).T)
    WgT = np.ascontiguousarray(_f32(inp["L0_ffn_W_gate"]).T)
    WdT = np.ascontiguousarray(_f32(inp["L0_ffn_W_down"]).T)
    Wdp = np.ascontiguousarray(WdT.reshape(64, 64, 2, D).transpose(1, 0, 2, 3).reshape(64, 128, D))
    su_ug = np.ascontiguousarray(_f32(inp["L0_ffn_SU_ug"]).reshape(16, 128).T) / np.sqrt(np.float32(D))
    su_dp = np.ascontiguousarray(_f32(inp["L0_ffn_SU_dp"]).reshape(64, 128).T)
    ws_up = np.ascontiguousarray(_f32(inp["L0_ffn_Wscale_ug"])[:INTER].reshape(64, 128).T)
    ws_g = np.ascontiguousarray(_f32(inp["L0_ffn_Wscale_ug"])[INTER:].reshape(64, 128).T)
    sw_up = np.ascontiguousarray(su_dp * ws_up)
    ws_d = np.ascontiguousarray(_f32(inp["L0_ffn_Wscale_dp"]).reshape(16, 128).T) * 0.5
    BT = np.ascontiguousarray(_f32(inp["L0_attn_B"]).T)
    CT = np.ascontiguousarray(_f32(inp["L0_attn_C"]).T)
    Dv = np.ascontiguousarray(_f32(inp["L0_attn_D"]).reshape(16, 128).T)

    Wr = _f32(inp["L1_ffn_Wr"])
    br = _f32(inp["L1_ffn_br"]).reshape(8, 1)
    WdS = np.ascontiguousarray(_f32(inp["L1_ffn_Wd"]).transpose(1, 0, 2).reshape(D, 128))
    bdS = _f32(inp["L1_ffn_bd"]).reshape(128, 1)
    WuS = np.ascontiguousarray(_f32(inp["L1_ffn_Wu"]).reshape(128, D))
    bu = _f32(inp["L1_ffn_bu"])

    wqkv = _f32(inp["L1_attn_w_qkv"])
    wz = _f32(inp["L1_attn_w_z"])
    wba = np.ascontiguousarray(np.concatenate([_f32(inp["L1_attn_w_b"]), _f32(inp["L1_attn_w_a"])], axis=1))
    alog = _f32(inp["L1_attn_A_log"]).reshape(32, 1)
    dtb = _f32(inp["L1_attn_dt_bias"]).reshape(32, 1)
    cwr = np.ascontiguousarray(_f32(inp["L1_attn_conv_w"]).reshape(64, 128, 4))
    nw = _f32(inp["L1_attn_norm_weight"]).reshape(1, 128)
    wout = _f32(inp["L1_attn_w_out"])

    wqkvc, wbacs, alogcs, dtbcs, cwqs, cwks, cwvs, monorms = [], [], [], [], [], [], [], []
    for hg in range(2):
        wqkvc.append(np.ascontiguousarray(np.concatenate(
            [wqkv[:, 1024 * hg:1024 * hg + 1024],
             wqkv[:, 2048 + 1024 * hg:2048 + 1024 * hg + 1024],
             wqkv[:, 4096 + 2048 * hg:4096 + 2048 * hg + 2048]], axis=1)))
        wbacs.append(np.ascontiguousarray(np.concatenate(
            [wba[:, 16 * hg:16 * hg + 16], wba[:, 32 + 16 * hg:32 + 16 * hg + 16]], axis=1)))
        alogcs.append(np.ascontiguousarray(alog[16 * hg:16 * hg + 16]))
        dtbcs.append(np.ascontiguousarray(dtb[16 * hg:16 * hg + 16]))
        cwqs.append(np.ascontiguousarray(cwr[8 * hg:8 * hg + 8]))
        cwks.append(np.ascontiguousarray(cwr[16 + 8 * hg:16 + 8 * hg + 8]))
        cwvs.append(np.ascontiguousarray(cwr[32 + 16 * hg:32 + 16 * hg + 16]))
        monorms.append(np.tile(np.array([1.0 - hg, float(hg)], np.float32), (128, 1)))

    P = Prog()
    x0 = P.din("x", [D, TOK], F32)
    xo_t = P.dout("xo", [D, TOK], F32)
    ones1_t = P.din("ones1", [1, 128], F32)
    ones128_t = P.din("ones128", [128, 1], F32)
    wdp_t = P.din("wdp", [64, 128, D], F32)
    h64_t = P.din("h64i2", [128, 128], F32)
    iden_t = P.din("iden", [128, 128], F32)
    wdf_t = P.dint("wdf", [INTER, D], BF16)
    x1 = P.dint("x1", [D, TOK], F32)
    x2 = P.dint("x2", [D, TOK], F32)
    x3 = P.dint("x3", [D, TOK], F32)
    x4 = P.dint("x4", [D, TOK], F32)
    x5 = P.dint("x5", [D, TOK], F32)
    xnin = P.dint("xnin", [D, TOK], BF16)
    xnf = P.dint("xnf", [2 * D, TOK], BF16)
    zsd = P.dint("zsd", [32, 128, TOK], BF16)
    qind = P.dint("qind", [8, 128, 2048], BF16)
    kind_ = P.dint("kind", [8, 128, 2048], BF16)
    vind = P.dint("vind", [16, 128, 2048], BF16)
    betad = P.dint("betad", [16, 2048], F32)
    gd = P.dint("gd", [16, 2048], F32)
    rsin = P.dint("rsin", [64, 128, TOK], BF16)
    rsout = P.dint("rsout", [32, 128, TOK], BF16)

    W0 = dict(wupT=P.din("wupT", [D, INTER], F32), wgT=P.din("wgT", [D, INTER], F32),
              wdf=wdf_t, su_ug=P.din("su_ug", [128, 16], F32), sw_up=P.din("sw_up", [128, 64], F32),
              ws_g=P.din("ws_g", [128, 64], F32), ws_d=P.din("ws_d", [128, 16], F32),
              h128=P.din("h128", [128, 128], F32), ones1=ones1_t, ones128=ones128_t)
    WA = dict(bT=P.din("bT", [D, 1024], F32), cT=P.din("cT", [1024, D], F32),
              dvec=P.din("dvec", [128, 16], F32), ones1=ones1_t, ones128=ones128_t)
    WM = dict(wr=P.din("wr", [D, 8], F32), br=P.din("br", [8, 1], F32),
              wds=P.din("wds", [D, 128], F32), bds=P.din("bds", [128, 1], F32),
              wus=P.din("wus", [128, D], F32), bu=P.din("bu", [8, D], F32),
              e16=P.din("e16", [8, 128], F32), iden=iden_t, ones1=ones1_t, ones128=ones128_t)
    WZ = dict(wz=P.din("wz", [D, 4096], F32), ones1=ones1_t, ones128=ones128_t)
    WP = dict(wqkvc=P.din("wqkvc", [D, 4096], F32), wbac=P.din("wbac", [D, 32], F32),
              alogc=P.din("alogc", [16, 1], F32), dtbc=P.din("dtbc", [16, 1], F32))
    WC = dict(cwq=P.din("cwq", [8, 128, 4], F32), cwk=P.din("cwk", [8, 128, 4], F32),
              cwv=P.din("cwv", [16, 128, 4], F32), monorm=P.din("monorm", [128, 2], F32),
              nw=P.din("nw", [1, 128], F32), maskw=P.din("maskw", [128, 128], F32),
              maskat=P.din("maskat", [128, 128], F32), iden=iden_t,
              ones1=ones1_t, ones128=ones128_t)
    wout_t = P.din("wout", [4096, D], F32)

    P.push("s0_")
    emit_prep(P, wdp_t, h64_t, wdf_t)
    P.pop()
    P.push("s1_")
    emit_ffn0(P, x0, x1, W0)
    P.pop()
    P.push("s2_")
    emit_attn0(P, x1, x2, WA)
    P.pop()
    P.push("s3_")
    emit_ffn0(P, x2, x3, W0)
    P.pop()
    P.push("s4_")
    emit_moe(P, x3, x4, WM)
    P.pop()
    P.push("s5_")
    emit_gdnz(P, x4, WZ, xnin, xnf, zsd)
    P.pop()
    P.push("s6_")
    emit_gdnpc(P, xnf, WP, qind, kind_, vind, betad, gd)
    P.pop()
    P.push("s7_")
    emit_gdnchunk(P, WC, qind, kind_, vind, betad, gd, rsin)
    P.cc("ReduceScatter", ALU.add, rsin[0], rsin[1], rsout[0], rsout[1], "cc1")
    P.pop()
    P.push("s8_")
    emit_gdnout(P, x4, x5, rsout, zsd, wout_t)
    P.pop()
    P.push("s9_")
    emit_moe(P, x5, xo_t, WM, final=True)
    P.pop()

    maps = []
    for c in range(NCORE):
        hg = c % 2
        maps.append(dict(
            x=xs[c], ones1=C["ONES1"], ones128=C["ONES128"], wdp=Wdp, h64i2=C["H64I2"],
            iden=C["IDEN"], wupT=WupT, wgT=WgT, su_ug=su_ug, sw_up=sw_up, ws_g=ws_g,
            ws_d=ws_d, h128=C["H128"], bT=BT, cT=CT, dvec=Dv, wr=Wr, br=br, wds=WdS,
            bds=bdS, wus=WuS, bu=bu, e16=C["E16"], wz=wz, wqkvc=wqkvc[hg],
            wbac=wbacs[hg], alogc=alogcs[hg], dtbc=dtbcs[hg], cwq=cwqs[hg],
            cwk=cwks[hg], cwv=cwvs[hg], monorm=monorms[hg], nw=nw, maskw=C["MASKW"],
            maskat=C["MASKAT"], wout=wout))

    r = P.run(maps, core_ids=list(range(NCORE)))
    out = np.empty((4, 2048, D), np.float32)
    for c in range(NCORE):
        b, hf = c // 2, c % 2
        out[b, hf * TOK:(hf + 1) * TOK, :] = np.asarray(r.results[c]["xo"], np.float32).T
    return out


# version 07
# speedup vs baseline: 1.0341x; 1.0341x over previous
import contextlib
import numpy as np
import ml_dtypes
import concourse.bass as bass
import concourse.mybir as mybir
from concourse.bass_utils import run_bass_kernel_spmd

F32 = mybir.dt.float32
BF16 = mybir.dt.bfloat16
AF = mybir.ActivationFunctionType
ALU = mybir.AluOpType
AX = mybir.AxisListType
BF = ml_dtypes.bfloat16

NCORE = 8
TOK = 1024
TB = 512
D = 2048
ND = 16
INTER = 8192
NI = 64
HK, HV, DK, DV = 16, 32, 128, 128
CC = 128

ENGS = ("sync", "scalar", "vector", "tensor", "gpsimd")
SERIALIZE = False
PRUNE = True
SER_SET = ()


class _Buf:
    __slots__ = ("w", "r")

    def __init__(self):
        self.w = None
        self.r = []


class _Ev:
    __slots__ = ("e", "c", "vc")

    def __init__(self, e, c, vc):
        self.e = e
        self.c = c
        self.vc = vc


def _clb(b):
    nb = _Buf()
    nb.w = b.w
    nb.r = list(b.r)
    return nb


RGROUPS = [[0, 1], [2, 3], [4, 5], [6, 7]]


class Prog:
    def __init__(self):
        self.nc = bass.Bass("TRN2", target_bir_lowering=False)
        self.stack = contextlib.ExitStack()
        self.plan = {e: [] for e in ENGS}
        self.cnt = dict.fromkeys(ENGS, 0)
        self.vc = {e: {} for e in ENGS}
        self.outev = {}
        self.pfx = ""
        self.scopes = []
        self.bar = None

    def push(self, pfx):
        self.pfx = pfx
        self.scopes.append(contextlib.ExitStack())

    def pop(self):
        self.barrier()
        self.scopes.pop().close()

    def barrier(self):
        self.bar = {e: self.cnt.get(e, 0) for e in ENGS}

    def din(self, name, shape, dt):
        h = self.nc.dram_tensor(name, list(shape), dt, kind="ExternalInput")
        return h.ap(), _Buf()

    def dout(self, name, shape, dt):
        h = self.nc.dram_tensor(name, list(shape), dt, kind="ExternalOutput")
        return h.ap(), _Buf()

    def dint(self, name, shape, dt):
        h = self.nc.dram_tensor(name, list(shape), dt)
        return h.ap(), _Buf()

    def sb(self, name, shape, dt):
        st = self.scopes[-1] if self.scopes else self.stack
        h = st.enter_context(self.nc.sbuf_tensor(self.pfx + name, list(shape), dt))
        return h, _Buf()

    def ps(self, name, shape, dt):
        st = self.scopes[-1] if self.scopes else self.stack
        h = st.enter_context(self.nc.psum_tensor(self.pfx + name, list(shape), dt))
        return h, _Buf()

    def emit(self, eng, fn, reads=(), writes=(), inc=1, out_dma=False, sem=None):
        deps = {}

        def add(ev):
            if ev is None:
                return
            cur = deps.get(ev.e)
            if cur is None or ev.c > cur.c:
                deps[ev.e] = ev

        for b in reads:
            if b is not None:
                add(b.w)
        for b in writes:
            if b is not None:
                add(b.w)
                for ev in b.r:
                    add(ev)
        if self.bar:
            for p, v in self.bar.items():
                if v > 0 and p != eng:
                    cur = deps.get(p)
                    if cur is None or v > cur.c:
                        deps[p] = _Ev(p, v, {})
        waits = []
        myvc = self.vc[eng]
        if SERIALIZE or SER_SET:
            ser = ENGS if SERIALIZE else SER_SET
            for p in ser:
                if p != eng and self.cnt[p] > 0:
                    ev0 = _Ev(p, self.cnt[p], {})
                    cur = deps.get(p)
                    if cur is None or ev0.c > cur.c:
                        deps[p] = ev0
        for p in list(deps):
            if p in ("sync", "gpsimd"):
                ev0 = deps[p]
                if ev0.c < self.cnt[p]:
                    deps[p] = _Ev(p, self.cnt[p], ev0.vc)
        for p in sorted(deps):
            ev = deps[p]
            if p == "tensor" and eng == "tensor":
                continue
            if PRUNE and myvc.get(p, 0) >= ev.c:
                continue
            waits.append((p, ev.c))
            for q, v in ev.vc.items():
                if myvc.get(q, 0) < v:
                    myvc[q] = v
            if myvc.get(p, 0) < ev.c:
                myvc[p] = ev.c
        key = sem or eng
        self.cnt[key] = self.cnt.get(key, 0) + inc
        ev = _Ev(key, self.cnt[key], dict(myvc))
        ev.vc[key] = ev.c
        self.plan[eng].append((waits, fn, inc, key))
        for b in writes:
            if b is not None:
                b.w = ev
                b.r = []
        for b in reads:
            if b is not None:
                b.r.append(ev)
        if out_dma:
            self.outev[eng] = self.cnt[eng]
        return ev

    def cc(self, kind, op, in_ap, in_buf, out_ap, out_buf, key):
        def fn(g, kind=kind, op=op, i=in_ap, o=out_ap):
            return g.collective_compute(kind, op, replica_groups=RGROUPS,
                                        ins=[i.opt()], outs=[o.opt()])
        self.emit("gpsimd", fn, reads=[in_buf], writes=[out_buf], inc=1, sem=key)

    def dma(self, dst_ap, src_ap, dst_buf, src_buf=None, cast=False, out=False, eng=None):
        e = eng or ("gpsimd" if cast else "sync")
        self.emit(e, lambda E, d=dst_ap, s=src_ap: E.dma_start(d, s),
                  reads=[src_buf] if src_buf is not None else [],
                  writes=[dst_buf] if dst_buf is not None else [],
                  inc=16, out_dma=out)

    def mm(self, ps_ap, ps_buf, pairs, reads, accum=False):
        n = len(pairs)

        def fn(te, pairs=tuple(pairs), o=ps_ap, n=n, accum=accum):
            ins = None
            for i, (l, r) in enumerate(pairs):
                ins = te.matmul(o, l, r, start=(i == 0 and not accum), stop=(i == n - 1))
            return ins
        self.emit("tensor", fn, reads=list(reads), writes=[ps_buf])

    def tr(self, ps_ap, ps_buf, in_ap, in_buf, iden_ap, iden_buf):
        self.emit("tensor", lambda te, o=ps_ap, i=in_ap, d=iden_ap: te.transpose(o, i, d),
                  reads=[in_buf, iden_buf], writes=[ps_buf])

    def run(self, in_maps, core_ids, trace=False):
        for e in ("sync", "gpsimd"):
            v = self.cnt[e]
            if v <= 0:
                continue
            tgt = "sync"
            myvc = self.vc[tgt]
            if myvc.get(e, 0) < v:
                self.plan[tgt].append(([(e, v)], None, 0, None))
                myvc[e] = v
        nc = self.nc
        sems = {}
        with contextlib.ExitStack() as st:
            for e in list(ENGS) + [k for k in self.cnt if k not in ENGS]:
                sems[e] = st.enter_context(nc.semaphore("S_" + e))
            with nc.Block() as block:
                def mk(eng_name):
                    def body(eng):
                        for waits, fn, inc, key in self.plan[eng_name]:
                            for (p, v) in waits:
                                eng.wait_ge(sems[p], v)
                            if fn is not None:
                                ins = fn(eng)
                                if inc:
                                    ins.then_inc(sems[key], inc)
                    return body
                for name, dec in (("sync", block.sync), ("scalar", block.scalar),
                                  ("vector", block.vector), ("tensor", block.tensor),
                                  ("gpsimd", block.gpsimd)):
                    if self.plan[name]:
                        dec(mk(name))
        self.stack.close()
        return run_bass_kernel_spmd(nc, in_maps, core_ids=core_ids, trace=trace)


def _had(n):
    H = np.array([[1.0]], np.float32)
    while H.shape[0] < n:
        H = np.block([[H, H], [H, -H]]).astype(np.float32)
    return H


def host_consts():
    c = {}
    c["H128"] = _had(128)
    H64 = _had(64)
    c["H64I2"] = np.kron(H64, np.eye(2, dtype=np.float32))
    c["IDEN"] = np.eye(128, dtype=np.float32)
    c["ONES1"] = np.ones((1, 128), np.float32)
    c["ONES128"] = np.ones((128, 1), np.float32)
    t = np.arange(128)
    c["MASKW"] = np.where(t[None, :] < t[:, None], 0.0, 1e9).astype(np.float32)
    c["MASKAT"] = np.where(t[:, None] <= t[None, :], 0.0, 1e9).astype(np.float32)
    E16 = np.zeros((8, 128), np.float32)
    for e in range(8):
        E16[e, e * 16:(e + 1) * 16] = 1.0
    c["E16"] = E16
    return c


def emit_ln_resident(P, xt, yt, psf, scr, ones128, ones1, post_cols=None, eps=1e-5):
    o128, o128b = ones128
    o1, o1b = ones1
    m_sb, m_b = P.sb("ln_m", [1, 1024], F32)
    r_sb, r_b = P.sb("ln_r", [1, 1024], F32)
    for tb in range(2):
        sl = slice(tb * TB, (tb + 1) * TB)
        ps_s, ps_sb_ = psf[0]
        ps_q, ps_qb = psf[1]
        for i in range(ND):
            xh, xb = xt[i]
            sh, sb_ = scr[i % 2]
            P.emit("vector", lambda e, o=sh[:, :TB], a=xh[:, sl]: e.tensor_mul(o, a, a),
                   reads=[xb], writes=[sb_])
            P.mm(ps_s[0:1, :], ps_sb_, [(o128[:, :], xh[:, sl])], [o128b, xb], accum=(i > 0))
            P.mm(ps_q[0:1, :], ps_qb, [(o128[:, :], sh[:, :TB])], [o128b, sb_], accum=(i > 0))
        P.emit("vector", lambda e, o=m_sb[0:1, sl], i=ps_s[0:1, :]: e.tensor_scalar_mul(o, i, 1.0 / D),
               reads=[ps_sb_], writes=[m_b])
        sh, sb_ = scr[0]
        P.emit("vector", lambda e, o=sh[0:1, 0:TB], i=m_sb[0:1, sl]: e.tensor_mul(o, i, i),
               reads=[m_b], writes=[sb_])
        P.emit("vector", lambda e, o=r_sb[0:1, sl], i=ps_q[0:1, :], s=sh[0:1, 0:TB]:
               e.scalar_tensor_tensor(o, i, 1.0 / D, s, ALU.mult, ALU.subtract),
               reads=[ps_qb, sb_], writes=[r_b])
        P.emit("vector", lambda e, o=r_sb[0:1, sl], i=r_sb[0:1, sl]: e.tensor_scalar_add(o, i, eps),
               reads=[r_b], writes=[r_b])
        P.emit("scalar", lambda e, o=r_sb[0:1, sl], i=r_sb[0:1, sl]: e.activation(o, i, AF.Ln),
               reads=[r_b], writes=[r_b])
        P.emit("scalar", lambda e, o=r_sb[0:1, sl], i=r_sb[0:1, sl]: e.activation(o, i, AF.Exp, scale=-0.5),
               reads=[r_b], writes=[r_b])
        ps_m, ps_mb = psf[2]
        ps_r, ps_rb = psf[3]
        P.mm(ps_m[:, :], ps_mb, [(o1[:, :], m_sb[0:1, sl])], [o1b, m_b])
        P.mm(ps_r[:, :], ps_rb, [(o1[:, :], r_sb[0:1, sl])], [o1b, r_b])
        for i in range(ND):
            xh, xb = xt[i]
            yh, yb = yt[i]
            sh, sb_ = scr[i % 2]
            P.emit("vector", lambda e, o=sh[:, :TB], a=xh[:, sl], b=ps_m[:, :]: e.tensor_sub(o, a, b),
                   reads=[xb, ps_mb], writes=[sb_])
            if post_cols is not None:
                ch, cb, cols = post_cols
                P.emit("vector", lambda e, o=yh[:, sl], a=sh[:, :TB], s=ch[:, cols[i]:cols[i] + 1], b=ps_r[:, :]:
                       e.scalar_tensor_tensor(o, a, s, b, ALU.mult, ALU.mult),
                       reads=[sb_, cb, ps_rb], writes=[yb])
            else:
                P.emit("vector", lambda e, o=yh[:, sl], a=sh[:, :TB], b=ps_r[:, :]: e.tensor_mul(o, a, b),
                       reads=[sb_, ps_rb], writes=[yb])
    return (m_sb, m_b, r_sb, r_b)


def emit_ln_streaming(P, x_ap, x_dbuf, yt, psf, scr, xs, ones128, ones1, post_cols=None, eps=1e-5):
    o128, o128b = ones128
    o1, o1b = ones1
    xr = x_ap.rearrange("(i p) t -> i p t", p=128)
    m_sb, m_b = P.sb("lns_m", [1, 1024], F32)
    r_sb, r_b = P.sb("lns_r", [1, 1024], F32)
    for i in range(ND):
        xh, xb = xs[i % 2]
        P.dma(xh[:, :], xr[i], xb, src_buf=x_dbuf)
        for tb in range(2):
            sl = slice(tb * TB, (tb + 1) * TB)
            sh, sb_ = scr[(2 * i + tb) % 2]
            P.emit("vector", lambda e, o=sh[:, :TB], a=xh[:, sl]: e.tensor_mul(o, a, a),
                   reads=[xb], writes=[sb_])
            ps_s, ps_sb_ = psf[tb]
            ps_q, ps_qb = psf[2 + tb]
            P.mm(ps_s[0:1, :], ps_sb_, [(o128[:, :], xh[:, sl])], [o128b, xb], accum=(i > 0))
            P.mm(ps_q[0:1, :], ps_qb, [(o128[:, :], sh[:, :TB])], [o128b, sb_], accum=(i > 0))
    for tb in range(2):
        sl = slice(tb * TB, (tb + 1) * TB)
        ps_s, ps_sb_ = psf[tb]
        ps_q, ps_qb = psf[2 + tb]
        P.emit("vector", lambda e, o=m_sb[0:1, sl], i=ps_s[0:1, :]: e.tensor_scalar_mul(o, i, 1.0 / D),
               reads=[ps_sb_], writes=[m_b])
        sh, sb_ = scr[0]
        P.emit("vector", lambda e, o=sh[0:1, 0:TB], i=m_sb[0:1, sl]: e.tensor_mul(o, i, i),
               reads=[m_b], writes=[sb_])
        P.emit("vector", lambda e, o=r_sb[0:1, sl], i=ps_q[0:1, :], s=sh[0:1, 0:TB]:
               e.scalar_tensor_tensor(o, i, 1.0 / D, s, ALU.mult, ALU.subtract),
               reads=[ps_qb, sb_], writes=[r_b])
        P.emit("vector", lambda e, o=r_sb[0:1, sl], i=r_sb[0:1, sl]: e.tensor_scalar_add(o, i, eps),
               reads=[r_b], writes=[r_b])
        P.emit("scalar", lambda e, o=r_sb[0:1, sl], i=r_sb[0:1, sl]: e.activation(o, i, AF.Ln),
               reads=[r_b], writes=[r_b])
        P.emit("scalar", lambda e, o=r_sb[0:1, sl], i=r_sb[0:1, sl]: e.activation(o, i, AF.Exp, scale=-0.5),
               reads=[r_b], writes=[r_b])
    for tb in range(2):
        sl = slice(tb * TB, (tb + 1) * TB)
        ps_m, ps_mb = psf[0 + tb]
        ps_r, ps_rb = psf[2 + tb]
        P.mm(ps_m[:, :], ps_mb, [(o1[:, :], m_sb[0:1, sl])], [o1b, m_b])
        P.mm(ps_r[:, :], ps_rb, [(o1[:, :], r_sb[0:1, sl])], [o1b, r_b])
    for i in range(ND):
        xh, xb = xs[i % 2]
        P.dma(xh[:, :], xr[i], xb, src_buf=x_dbuf)
        for tb in range(2):
            sl = slice(tb * TB, (tb + 1) * TB)
            ps_m, ps_mb = psf[0 + tb]
            ps_r, ps_rb = psf[2 + tb]
            yh, yb = yt[i]
            sh, sb_ = scr[(2 * i + tb) % 2]
            P.emit("vector", lambda e, o=sh[:, :TB], a=xh[:, sl], b=ps_m[:, :]: e.tensor_sub(o, a, b),
                   reads=[xb, ps_mb], writes=[sb_])
            if post_cols is not None:
                ch, cb, cols = post_cols
                P.emit("vector", lambda e, o=yh[:, sl], a=sh[:, :TB], s=ch[:, cols[i]:cols[i] + 1], b=ps_r[:, :]:
                       e.scalar_tensor_tensor(o, a, s, b, ALU.mult, ALU.mult),
                       reads=[sb_, cb, ps_rb], writes=[yb])
            else:
                P.emit("vector", lambda e, o=yh[:, sl], a=sh[:, :TB], b=ps_r[:, :]: e.tensor_mul(o, a, b),
                       reads=[sb_, ps_rb], writes=[yb])


def _alloc_common(P):
    psf = [P.ps(f"psf{k}", [128, 512], F32) for k in range(6)]
    psb = [P.ps(f"psb{k}", [128, 1024], BF16) for k in range(2)]
    scr = [P.sb(f"scr{k}", [128, 512], F32) for k in range(2)]
    return psf, psb, scr


def emit_prep(P, wdp_t, h64_t, wdf_t):
    wdp, wdp_b = wdp_t
    h64, h64_b = h64_t
    wdf, wdf_b = wdf_t
    wdfr = wdf.rearrange("(a g j) d -> g a j d", a=64, g=64, j=2)
    h_sb, h_b = P.sb("h64sb", [128, 128], BF16)
    P.dma(h_sb[:, :], h64[:, :], h_b, src_buf=h64_b, cast=True)
    grp = [P.sb(f"grp{k}", [128, D], BF16) for k in range(2)]
    osb = [P.sb(f"osb{k}", [128, D], BF16) for k in range(2)]
    ps = [P.ps(f"pp{k}", [128, 512], F32) for k in range(2)]
    sc = 1.0 / np.sqrt(INTER)
    for g in range(64):
        gh, gb = grp[g % 2]
        oh, ob = osb[g % 2]
        P.dma(gh[:, :], wdp[g], gb, src_buf=wdp_b, cast=True)
        for j in range(4):
            sl = slice(j * 512, (j + 1) * 512)
            ph, pb = ps[j % 2]
            P.mm(ph[:, :], pb, [(h_sb[:, :], gh[:, sl])], [h_b, gb])
            P.emit("scalar", lambda e, o=oh[:, sl], i=ph[:, :]: e.mul(o, i, sc),
                   reads=[pb], writes=[ob])
        P.dma(wdfr[g], oh[:, :], wdf_b, src_buf=ob)


def emit_ffn0(P, x_t, xo_t, W):
    x_ap, x_b = x_t
    xo_ap, xo_b = xo_t
    wupT, wupT_b = W["wupT"]
    wgT, wgT_b = W["wgT"]
    wdf, wdf_b = W["wdf"]
    su_ug, su_ug_b = W["su_ug"]
    sw_up, sw_up_b = W["sw_up"]
    ws_g, ws_g_b = W["ws_g"]
    ws_d, ws_d_b = W["ws_d"]
    h128, h128_b = W["h128"]
    ones1, ones1_b = W["ones1"]
    ones128, ones128_b = W["ones128"]

    psf, psb, scr = _alloc_common(P)
    h_sb, h_b = P.sb("h128sb", [128, 128], BF16)
    P.dma(h_sb[:, :], h128[:, :], h_b, src_buf=h128_b, cast=True)
    o1_sb, o1_b = P.sb("o1", [1, 128], F32)
    P.dma(o1_sb[:, :], ones1[:, :], o1_b, src_buf=ones1_b)
    o128_sb, o128_b = P.sb("o128", [128, 1], F32)
    P.dma(o128_sb[:, :], ones128[:, :], o128_b, src_buf=ones128_b)
    suh, sub = P.sb("suh", [128, 16], F32)
    P.dma(suh[:, :], su_ug[:, :], sub, src_buf=su_ug_b)
    swh, swb = P.sb("swh", [128, 64], F32)
    P.dma(swh[:, :], sw_up[:, :], swb, src_buf=sw_up_b)
    wgh, wgb = P.sb("wsgh", [128, 64], F32)
    P.dma(wgh[:, :], ws_g[:, :], wgb, src_buf=ws_g_b)
    wdh, wdb = P.sb("wsdh", [128, 16], F32)
    P.dma(wdh[:, :], ws_d[:, :], wdb, src_buf=ws_d_b)

    xs = [P.sb(f"xs{k}", [128, TOK], F32) for k in range(2)]
    sA = [P.sb(f"sA{i}", [128, TOK], BF16) for i in range(ND)]
    sB = [P.sb(f"sB{i}", [128, TOK], BF16) for i in range(ND)]
    h2 = [P.sb(f"h2_{i}", [128, TOK], BF16) for i in range(32)]
    wu_s = [P.sb(f"wu{k}", [128, D], BF16) for k in range(2)]
    wg_s = [P.sb(f"wg{k}", [128, D], BF16) for k in range(2)]
    wd_s = [P.sb(f"wd{k}", [128, 4096], BF16) for k in range(2)]
    ug_scr = [P.sb(f"ug{k}", [128, TB], F32) for k in range(2)]
    ub = [P.sb(f"ub{k}", [128, TB], BF16) for k in range(2)]
    xn = [P.sb(f"xn{k}", [128, TOK], F32) for k in range(2)]

    emit_ln_streaming(P, x_ap, x_b, sA, psf, scr, xs, (o128_sb, o128_b), (o1_sb, o1_b),
                      post_cols=(suh, sub, list(range(16))))
    cur, nxt = sA, sB
    for st in range(4):
        h = 1 << st
        for a in range(ND):
            bpart = a ^ h
            ah, ab = cur[a]
            bh, bb = cur[bpart]
            oh, ob = nxt[a]
            op = ALU.add if (a & h) == 0 else ALU.subtract
            if op == ALU.add:
                P.emit("vector", lambda e, o=oh[:, :], p=ah[:, :], q=bh[:, :]: e.tensor_add(o, p, q),
                       reads=[ab, bb], writes=[ob])
            else:
                P.emit("vector", lambda e, o=oh[:, :], p=bh[:, :], q=ah[:, :]: e.tensor_sub(o, p, q),
                       reads=[ab, bb], writes=[ob])
        cur, nxt = nxt, cur
    for i in range(ND):
        ah, ab = cur[i]
        th, tb_ = nxt[i]
        for tb in range(2):
            sl = slice(tb * TB, (tb + 1) * TB)
            ph, pb = psf[4]
            P.mm(ph[:, :], pb, [(h_sb[:, :], ah[:, sl])], [h_b, ab])
            P.emit("scalar", lambda e, o=th[:, sl], i=ph[:, :]: e.copy(o, i),
                   reads=[pb], writes=[tb_])
    t_tiles = nxt
    wupr = wupT.rearrange("(i p) (a c) -> a p i c", p=128, c=128)
    wgr = wgT.rearrange("(i p) (a c) -> a p i c", p=128, c=128)
    wdfr = wdf.rearrange("(a p) (d c) -> d p a c", p=128, c=128)
    xr = x_ap.rearrange("(i p) t -> i p t", p=128)
    xor_ = xo_ap.rearrange("(i p) t -> i p t", p=128)
    for half in range(2):
        for a2 in range(32):
            a = half * 32 + a2
            wuh, wub = wu_s[a2 % 2]
            wgh2, wgb2 = wg_s[a2 % 2]
            P.dma(wuh[:, :], wupr[a], wub, src_buf=wupT_b, cast=True)
            P.dma(wgh2[:, :], wgr[a], wgb2, src_buf=wgT_b, cast=True)
            hh, hb = h2[a2]
            for tb in range(2):
                sl = slice(tb * TB, (tb + 1) * TB)
                pu, pub = psf[0]
                pg, pgb = psf[1]
                pairs_u = [(wuh[:, i * 128:(i + 1) * 128], t_tiles[i][0][:, sl]) for i in range(ND)]
                pairs_g = [(wgh2[:, i * 128:(i + 1) * 128], t_tiles[i][0][:, sl]) for i in range(ND)]
                rd = [wub] + [t_tiles[i][1] for i in range(ND)]
                rdg = [wgb2] + [t_tiles[i][1] for i in range(ND)]
                P.mm(pu[:, :], pub, pairs_u, rd)
                P.mm(pg[:, :], pgb, pairs_g, rdg)
                gh, gb = ug_scr[tb]
                P.emit("scalar", lambda e, o=gh[:, :], i=pg[:, :], s=wgh[:, a:a + 1]:
                       e.activation(o, i, AF.Silu, scale=s),
                       reads=[pgb, wgb], writes=[gb])
                uh, ubf = ub[tb]
                P.emit("vector", lambda e, o=uh[:, :], i=gh[:, :], s=swh[:, a:a + 1], b=pu[:, :]:
                       e.scalar_tensor_tensor(o, i, s, b, ALU.mult, ALU.mult),
                       reads=[gb, swb, pub], writes=[ubf])
                ph2, pb2 = psf[4]
                P.mm(ph2[:, :], pb2, [(h_sb[:, :], uh[:, :])], [h_b, ubf])
                P.emit("scalar", lambda e, o=hh[:, sl], i=ph2[:, :]: e.copy(o, i),
                       reads=[pb2], writes=[hb])
        for d in range(ND):
            wdh2, wdb2 = wd_s[d % 2]
            P.dma(wdh2[:, :].rearrange("p (a c) -> p a c", c=128), wdfr[d][:, half * 32:(half + 1) * 32, :],
                  wdb2, src_buf=wdf_b)
            xh, xb = xs[d % 2]
            if half == 0:
                P.dma(xh[:, :], xr[d], xb, src_buf=x_b)
            else:
                P.dma(xh[:, :], xor_[d], xb, src_buf=xo_b)
            xnh, xnb = xn[d % 2]
            for tb in range(2):
                sl = slice(tb * TB, (tb + 1) * TB)
                pd, pdb = psf[2 + tb]
                pairs = [(wdh2[:, a2 * 128:(a2 + 1) * 128], h2[a2][0][:, sl]) for a2 in range(32)]
                rd = [wdb2] + [h2[a2][1] for a2 in range(32)]
                P.mm(pd[:, :], pdb, pairs, rd)
                P.emit("vector", lambda e, o=xnh[:, sl], i=pd[:, :], s=wdh[:, d:d + 1], b=xh[:, sl]:
                       e.scalar_tensor_tensor(o, i, s, b, ALU.mult, ALU.add),
                       reads=[pdb, wdb, xb], writes=[xnb])
            P.dma(xor_[d], xnh[:, :], xo_b, src_buf=xnb)


def emit_attn0(P, x_t, xo_t, W):
    x_ap, x_b = x_t
    xo_ap, xo_b = xo_t
    bT, bT_b = W["bT"]
    cT, cT_b = W["cT"]
    dvec, dvec_b = W["dvec"]
    ones1, ones1_b = W["ones1"]
    ones128, ones128_b = W["ones128"]

    psf, psb, scr = _alloc_common(P)
    o1_sb, o1_b = P.sb("o1", [1, 128], F32)
    P.dma(o1_sb[:, :], ones1[:, :], o1_b, src_buf=ones1_b)
    o128_sb, o128_b = P.sb("o128", [128, 1], F32)
    P.dma(o128_sb[:, :], ones128[:, :], o128_b, src_buf=ones128_b)
    dh, db = P.sb("dh", [128, 16], F32)
    P.dma(dh[:, :], dvec[:, :], db, src_buf=dvec_b)

    xt = [P.sb(f"x{i}", [128, TOK], F32) for i in range(ND)]
    yt = [P.sb(f"y{i}", [128, TOK], BF16) for i in range(ND)]
    st = [P.sb(f"st{i}", [128, TOK], BF16) for i in range(8)]
    wb_s = [P.sb(f"wb{k}", [128, D], BF16) for k in range(2)]
    wc_s = [P.sb(f"wc{k}", [128, 1024], BF16) for k in range(2)]
    xn = [P.sb(f"xn{k}", [128, TOK], F32) for k in range(2)]

    xr = x_ap.rearrange("(i p) t -> i p t", p=128)
    xor_ = xo_ap.rearrange("(i p) t -> i p t", p=128)
    for i in range(ND):
        P.dma(xt[i][0][:, :], xr[i], xt[i][1], src_buf=x_b)
    emit_ln_resident(P, xt, yt, psf, scr, (o128_sb, o128_b), (o1_sb, o1_b))
    bTr = bT.rearrange("(i p) (n c) -> n p i c", p=128, c=128)
    for n in range(8):
        wh, wb = wb_s[n % 2]
        P.dma(wh[:, :], bTr[n], wb, src_buf=bT_b, cast=True)
        for tb in range(2):
            sl = slice(tb * TB, (tb + 1) * TB)
            ph, pb = psf[tb]
            pairs = [(wh[:, i * 128:(i + 1) * 128], yt[i][0][:, sl]) for i in range(ND)]
            P.mm(ph[:, :], pb, pairs, [wb] + [yt[i][1] for i in range(ND)])
            P.emit("scalar", lambda e, o=st[n][0][:, sl], i=ph[:, :]: e.activation(o, i, AF.Tanh),
                   reads=[pb], writes=[st[n][1]])
    cTr = cT.rearrange("(n p) (d c) -> d p n c", p=128, c=128)
    for d in range(ND):
        wh, wb = wc_s[d % 2]
        P.dma(wh[:, :], cTr[d], wb, src_buf=cT_b, cast=True)
        xnh, xnb = xn[d % 2]
        for tb in range(2):
            sl = slice(tb * TB, (tb + 1) * TB)
            ph, pb = psf[2 + tb]
            pairs = [(wh[:, n * 128:(n + 1) * 128], st[n][0][:, sl]) for n in range(8)]
            P.mm(ph[:, :], pb, pairs, [wb] + [st[n][1] for n in range(8)])
            sh, sb_ = scr[tb]
            P.emit("vector", lambda e, o=sh[:, :], i=yt[d][0][:, sl], s=dh[:, d:d + 1], b=ph[:, :]:
                   e.scalar_tensor_tensor(o, i, s, b, ALU.mult, ALU.add),
                   reads=[yt[d][1], db, pb], writes=[sb_])
            P.emit("vector", lambda e, o=xnh[:, sl], a=xt[d][0][:, sl], b=sh[:, :]: e.tensor_add(o, a, b),
                   reads=[xt[d][1], sb_], writes=[xnb])
        P.dma(xor_[d], xnh[:, :], xo_b, src_buf=xnb)


def emit_moe(P, x_t, xo_t, W, final=False):
    x_ap, x_b = x_t
    xo_ap, xo_b = xo_t
    wr, wr_b = W["wr"]
    br, br_b = W["br"]
    wds, wds_b = W["wds"]
    bds, bds_b = W["bds"]
    wus, wus_b = W["wus"]
    bu, bu_b = W["bu"]
    e16, e16_b = W["e16"]
    iden, iden_b = W["iden"]
    ones1, ones1_b = W["ones1"]
    ones128, ones128_b = W["ones128"]

    psf, psb, scr = _alloc_common(P)
    o1_sb, o1_b = P.sb("o1", [1, 128], F32)
    P.dma(o1_sb[:, :], ones1[:, :], o1_b, src_buf=ones1_b)
    o128_sb, o128_b = P.sb("o128", [128, 1], F32)
    P.dma(o128_sb[:, :], ones128[:, :], o128_b, src_buf=ones128_b)
    idf, idf_b = P.sb("idf", [128, 128], F32)
    P.dma(idf[:, :], iden[:, :], idf_b, src_buf=iden_b)
    idb, idb_b = P.sb("idb", [128, 128], BF16)
    P.dma(idb[:, :], iden[:, :], idb_b, src_buf=iden_b, cast=True)
    e16s, e16sb = P.sb("e16s", [8, 128], BF16)
    P.dma(e16s[:, :], e16[:, :], e16sb, src_buf=e16_b, cast=True)
    wr32, wr32b = P.sb("wr32", [128, 128], F32)
    P.dma(wr32[:, :], wr.rearrange("(i p) e -> p i e", p=128), wr32b, src_buf=wr_b)
    brs, brsb = P.sb("brs", [8, 1], F32)
    P.dma(brs[:, :], br[:, :], brsb, src_buf=br_b)
    wdss, wdssb = P.sb("wdss", [128, D], BF16)
    P.dma(wdss[:, :], wds.rearrange("(i p) c -> p i c", p=128), wdssb, src_buf=wds_b, cast=True)
    bdss, bdssb = P.sb("bdss", [128, 1], F32)
    P.dma(bdss[:, :], bds[:, :], bdssb, src_buf=bds_b)
    wuss, wussb = P.sb("wuss", [128, D], BF16)
    P.dma(wuss[:, :], wus[:, :], wussb, src_buf=wus_b, cast=True)
    bus, busb = P.sb("bus", [8, D], BF16)
    P.dma(bus[:, :], bu[:, :], busb, src_buf=bu_b, cast=True)

    xt = [P.sb(f"x{i}", [128, TOK], F32) for i in range(ND)]
    yt = [P.sb(f"y{i}", [128, TOK], BF16) for i in range(ND)]
    xr = x_ap.rearrange("(i p) t -> i p t", p=128)
    xor_ = xo_ap.rearrange("(i p) t -> i p t", p=128)
    for i in range(ND):
        P.dma(xt[i][0][:, :], xr[i], xt[i][1], src_buf=x_b)
    m_sb, m_b, r_sb, r_b = emit_ln_resident(P, xt, yt, psf, scr, (o128_sb, o128_b), (o1_sb, o1_b))

    lg, lg_b = P.sb("lg", [8, TOK], F32)
    swr, swr_b = P.sb("swr", [8, 1], F32)
    s8a, s8a_b = P.sb("s8a", [8, TB], F32)
    s8b, s8b_b = P.sb("s8b", [8, TB], F32)
    ph, pb = psf[3]
    pairs = [(wr32[:, i * 8:(i + 1) * 8], o128_sb[:, :]) for i in range(ND)]
    P.mm(ph[0:8, 0:1], pb, pairs, [wr32b, o128_b])
    P.emit("scalar", lambda e, o=swr[:, :], i=ph[0:8, 0:1]: e.copy(o, i), reads=[pb], writes=[swr_b])
    for tb in range(2):
        sl = slice(tb * TB, (tb + 1) * TB)
        ph0, pb0 = psf[0]
        pairs = [(wr32[:, i * 8:(i + 1) * 8], xt[i][0][:, sl]) for i in range(ND)]
        P.mm(ph0[0:8, :], pb0, pairs, [wr32b] + [xt[i][1] for i in range(ND)])
        ph1, pb1 = psf[1]
        P.mm(ph1[0:8, :], pb1, [(o1_sb[0:1, 0:8], m_sb[0:1, sl])], [o1_b, m_b])
        ph2, pb2 = psf[2]
        P.mm(ph2[0:8, :], pb2, [(o1_sb[0:1, 0:8], r_sb[0:1, sl])], [o1_b, r_b])
        P.emit("vector", lambda e, o=s8a[:, :], i=ph1[0:8, :], sc=swr[:, :]: e.tensor_scalar_mul(o, i, sc),
               reads=[pb1, swr_b], writes=[s8a_b])
        P.emit("vector", lambda e, o=s8b[:, :], a=ph0[0:8, :], b=s8a[:, :]: e.tensor_sub(o, a, b),
               reads=[pb0, s8a_b], writes=[s8b_b])
        P.emit("vector", lambda e, o=lg[:, sl], a=s8b[:, :], b=ph2[0:8, :]: e.tensor_mul(o, a, b),
               reads=[s8b_b, pb2], writes=[lg_b])
        P.emit("vector", lambda e, o=lg[:, sl], i=lg[:, sl], s=brs[:, :]: e.tensor_scalar_add(o, i, s),
               reads=[lg_b, brsb], writes=[lg_b])
    lt, lt_b = P.sb("lt", [128, 8], F32)
    m1, m1_b = P.sb("m1", [128, 1], F32)
    m2, m2_b = P.sb("m2", [128, 1], F32)
    msk, msk_b = P.sb("msk", [128, 8], F32)
    gts, gts_b = P.sb("gts", [128, 8], F32)
    w8, w8_b = P.sb("w8", [128, 8], BF16)
    wT, wT_b = P.sb("wT", [8, TOK], BF16)
    for blk in range(8):
        bs = slice(blk * 128, (blk + 1) * 128)
        ph, pb = psf[1]
        P.tr(ph[0:128, 0:8], pb, lg[0:8, bs], lg_b, idf[0:8, 0:8], idf_b)
        P.emit("scalar", lambda e, o=lt[:, :], i=ph[0:128, 0:8]: e.copy(o, i), reads=[pb], writes=[lt_b])
        P.emit("vector", lambda e, o=m1[:, :], i=lt[:, :]: e.reduce_max(o, i, AX.X), reads=[lt_b], writes=[m1_b])
        P.emit("vector", lambda e, o=msk[:, :], i=lt[:, :], s=m1[:, :]:
               e.tensor_scalar(o, i, s, None, ALU.is_ge), reads=[lt_b, m1_b], writes=[msk_b])
        P.emit("vector", lambda e, o=msk[:, :], i=msk[:, :], b=lt[:, :]:
               e.scalar_tensor_tensor(o, i, -1e9, b, ALU.mult, ALU.add), reads=[msk_b, lt_b], writes=[msk_b])
        P.emit("vector", lambda e, o=m2[:, :], i=msk[:, :]: e.reduce_max(o, i, AX.X), reads=[msk_b], writes=[m2_b])
        P.emit("vector", lambda e, o=msk[:, :], i=lt[:, :], s=m2[:, :]:
               e.tensor_scalar(o, i, s, None, ALU.is_ge), reads=[lt_b, m2_b], writes=[msk_b])
        P.emit("vector", lambda e, o=m1[:, :], i=m1[:, :]: e.tensor_scalar_mul(o, i, -1.0), reads=[m1_b], writes=[m1_b])
        P.emit("scalar", lambda e, o=gts[:, :], i=lt[:, :], b=m1[:, :]: e.activation(o, i, AF.Exp, bias=b),
               reads=[lt_b, m1_b], writes=[gts_b])
        P.emit("vector", lambda e, o=gts[:, :], a=gts[:, :], b=msk[:, :]: e.tensor_mul(o, a, b),
               reads=[gts_b, msk_b], writes=[gts_b])
        P.emit("vector", lambda e, o=m2[:, :], i=gts[:, :]: e.reduce_sum(o, i, AX.X), reads=[gts_b], writes=[m2_b])
        P.emit("vector", lambda e, o=m2[:, :], i=m2[:, :]: e.reciprocal(o, i), reads=[m2_b], writes=[m2_b])
        P.emit("vector", lambda e, o=w8[:, :], i=gts[:, :], s=m2[:, :]: e.tensor_scalar_mul(o, i, s),
               reads=[gts_b, m2_b], writes=[w8_b])
        ph2, pb2 = psb[0]
        P.tr(ph2[0:8, 0:128], pb2, w8[:, :], w8_b, idb[:, :], idb_b)
        P.emit("scalar", lambda e, o=wT[:, bs], i=ph2[0:8, 0:128]: e.copy(o, i), reads=[pb2], writes=[wT_b])
    wS, wS_b = P.sb("wS", [128, TOK], BF16)
    hS, hS_b = P.sb("hS", [128, TOK], BF16)
    for tb in range(2):
        sl = slice(tb * TB, (tb + 1) * TB)
        ph, pb = psf[2]
        P.mm(ph[:, :], pb, [(e16s[:, :], wT[:, sl])], [e16sb, wT_b])
        P.emit("scalar", lambda e, o=wS[:, sl], i=ph[:, :]: e.copy(o, i), reads=[pb], writes=[wS_b])
        ph2, pb2 = psf[3]
        pairs = [(wdss[:, i * 128:(i + 1) * 128], yt[i][0][:, sl]) for i in range(ND)]
        P.mm(ph2[:, :], pb2, pairs, [wdssb] + [yt[i][1] for i in range(ND)])
        P.emit("scalar", lambda e, o=hS[:, sl], i=ph2[:, :], b=bdss[:, :]: e.activation(o, i, AF.Relu, bias=b),
               reads=[pb2, bdssb], writes=[hS_b])
        P.emit("vector", lambda e, o=hS[:, sl], a=hS[:, sl], b=wS[:, sl]: e.tensor_mul(o, a, b),
               reads=[hS_b, wS_b], writes=[hS_b])
    xn = [P.sb(f"xn{k}", [128, TOK], F32) for k in range(2)]
    for d in range(ND):
        xnh, xnb = xn[d % 2]
        for tb in range(2):
            sl = slice(tb * TB, (tb + 1) * TB)
            ph, pb = psf[4 + tb]
            pairs = [(wuss[:, d * 128:(d + 1) * 128], hS[:, sl]),
                     (bus[0:8, d * 128:(d + 1) * 128], wT[0:8, sl])]
            P.mm(ph[:, :], pb, pairs, [wussb, hS_b, busb, wT_b])
            P.emit("vector", lambda e, o=xnh[:, sl], i=ph[:, :], b=xt[d][0][:, sl]:
                   e.scalar_tensor_tensor(o, i, 0.5, b, ALU.mult, ALU.add),
                   reads=[pb, xt[d][1]], writes=[xnb])
        P.dma(xor_[d], xnh[:, :], xo_b, src_buf=xnb, out=final)


def emit_gdnz(P, x_t, W, xnin_t, xnf_t, zs_t):
    x_ap, x_b = x_t
    wz, wz_b = W["wz"]
    ones1, ones1_b = W["ones1"]
    ones128, ones128_b = W["ones128"]
    xnin_ap, xnin_b = xnin_t
    xnf_ap, xnf_b = xnf_t
    zs, zs_b = zs_t

    psf, psb, scr = _alloc_common(P)
    o1_sb, o1_b = P.sb("o1", [1, 128], F32)
    P.dma(o1_sb[:, :], ones1[:, :], o1_b, src_buf=ones1_b)
    o128_sb, o128_b = P.sb("o128", [128, 1], F32)
    P.dma(o128_sb[:, :], ones128[:, :], o128_b, src_buf=ones128_b)

    xt = [P.sb(f"x{i}", [128, TOK], F32) for i in range(ND)]
    yt = [P.sb(f"y{i}", [128, TOK], BF16) for i in range(ND)]
    xr = x_ap.rearrange("(i p) t -> i p t", p=128)
    for i in range(ND):
        P.dma(xt[i][0][:, :], xr[i], xt[i][1], src_buf=x_b)
    emit_ln_resident(P, xt, yt, psf, scr, (o128_sb, o128_b), (o1_sb, o1_b))

    xnr = xnin_ap.rearrange("(i p) t -> i p t", p=128)
    for i in range(ND):
        P.dma(xnr[i], yt[i][0][:, :], xnin_b, src_buf=yt[i][1])
    P.cc("AllGather", ALU.bypass, xnin_ap, xnin_b, xnf_ap, xnf_b, "cc0")

    wq_s = [P.sb(f"wq{k}", [128, D], BF16) for k in range(2)]
    mx = [P.sb(f"mx{k}", [128, TOK], BF16) for k in range(2)]
    wzr = wz.rearrange("(i p) (o c) -> o p i c", p=128, c=128)
    for o in range(32):
        wh, wb = wq_s[o % 2]
        P.dma(wh[:, :], wzr[o], wb, src_buf=wz_b, cast=True)
        mh, mb = mx[o % 2]
        for tb in range(2):
            sl = slice(tb * TB, (tb + 1) * TB)
            ph, pb = psf[tb]
            pairs = [(wh[:, i * 128:(i + 1) * 128], yt[i][0][:, sl]) for i in range(ND)]
            P.mm(ph[:, :], pb, pairs, [wb] + [yt[i][1] for i in range(ND)])
            P.emit("scalar", lambda e, o_=mh[:, sl], i=ph[:, :]: e.activation(o_, i, AF.Silu),
                   reads=[pb], writes=[mb])
        P.dma(zs[o], mh[:, :], zs_b, src_buf=mb)


def emit_gdnpc(P, xnf_t, W, qin_t, kin_t, vin_t, beta_t, g_t):
    S = 2048
    xnf, xnf_b = xnf_t
    wqkvc, wqkvc_b = W["wqkvc"]
    wbac, wbac_b = W["wbac"]
    alogc, alogc_b = W["alogc"]
    dtbc, dtbc_b = W["dtbc"]
    qin, qin_b = qin_t
    kin, kin_b = kin_t
    vin, vin_b = vin_t
    beta_o, beta_ob = beta_t
    g_o, g_ob = g_t

    psf, psb, scr = _alloc_common(P)
    xnr = xnf.rearrange("(h i p) t -> h i p t", p=128, i=ND)
    xh = [[P.sb(f"xn{h}_{i}", [128, TOK], BF16) for i in range(ND)] for h in range(2)]
    for h in range(2):
        for i in range(ND):
            P.dma(xh[h][i][0][:, :], xnr[h][i], xh[h][i][1], src_buf=xnf_b)

    wq_s = [P.sb(f"wq{k}", [128, D], BF16) for k in range(2)]
    mx = [P.sb(f"mx{k}", [128, S], BF16) for k in range(2)]
    wqr = wqkvc.rearrange("(i p) (o c) -> o p i c", p=128, c=128)
    for o in range(32):
        wh, wb = wq_s[o % 2]
        P.dma(wh[:, :], wqr[o], wb, src_buf=wqkvc_b, cast=True)
        mh, mb = mx[o % 2]
        for h in range(2):
            for tb in range(2):
                sl = slice(tb * TB, (tb + 1) * TB)
                ph, pb = psf[2 * h + tb]
                pairs = [(wh[:, i * 128:(i + 1) * 128], xh[h][i][0][:, sl]) for i in range(ND)]
                P.mm(ph[:, :], pb, pairs, [wb] + [xh[h][i][1] for i in range(ND)])
                P.emit("scalar",
                       lambda e, o_=mh[:, h * TOK + tb * TB:h * TOK + (tb + 1) * TB], i=ph[:, :]: e.copy(o_, i),
                       reads=[pb], writes=[mb])
        if o < 8:
            dst, dstb = qin[o], qin_b
        elif o < 16:
            dst, dstb = kin[o - 8], kin_b
        else:
            dst, dstb = vin[o - 16], vin_b
        P.dma(dst, mh[:, :], dstb, src_buf=mb)
    wbas, wbasb = P.sb("wbas", [128, 512], BF16)
    P.dma(wbas[:, :], wbac.rearrange("(i p) e -> p i e", p=128), wbasb, src_buf=wbac_b, cast=True)
    blog, blog_b = P.sb("blog", [16, S], F32)
    alg, alg_b = P.sb("alg", [16, S], F32)
    for h in range(2):
        for tb in range(2):
            sl = slice(tb * TB, (tb + 1) * TB)
            cs = slice(h * TOK + tb * TB, h * TOK + (tb + 1) * TB)
            ph, pb = psf[4]
            pairs = [(wbas[:, i * 32:i * 32 + 16], xh[h][i][0][:, sl]) for i in range(ND)]
            P.mm(ph[0:16, :], pb, pairs, [wbasb] + [xh[h][i][1] for i in range(ND)])
            P.emit("scalar", lambda e, o=blog[:, cs], i=ph[0:16, :]: e.copy(o, i),
                   reads=[pb], writes=[blog_b])
            ph2, pb2 = psf[5]
            pairs = [(wbas[:, i * 32 + 16:i * 32 + 32], xh[h][i][0][:, sl]) for i in range(ND)]
            P.mm(ph2[0:16, :], pb2, pairs, [wbasb] + [xh[h][i][1] for i in range(ND)])
            P.emit("scalar", lambda e, o=alg[:, cs], i=ph2[0:16, :]: e.copy(o, i),
                   reads=[pb2], writes=[alg_b])
    bt, bt_b = P.sb("bt", [16, S], F32)
    P.emit("scalar", lambda e, o=bt[:, :], i=blog[:, :]: e.activation(o, i, AF.Sigmoid),
           reads=[blog_b], writes=[bt_b])
    P.dma(beta_o[:, :], bt[:, :], beta_ob, src_buf=bt_b)
    ad, ad_b = P.sb("ad", [16, S], F32)
    dtbs, dtbsb = P.sb("dtbs", [16, 1], F32)
    P.dma(dtbs[:, :], dtbc[:, :], dtbsb, src_buf=dtbc_b)
    P.emit("vector", lambda e, o=ad[:, :], i=alg[:, :], s=dtbs[:, :]: e.tensor_scalar_add(o, i, s),
           reads=[alg_b, dtbsb], writes=[ad_b])
    t1, t1_b = P.sb("spt1", [16, S], F32)
    t2, t2_b = P.sb("spt2", [16, S], F32)
    P.emit("scalar", lambda e, o=t1[:, :], i=ad[:, :]: e.activation(o, i, AF.Abs),
           reads=[ad_b], writes=[t1_b])
    P.emit("scalar", lambda e, o=t1[:, :], i=t1[:, :]: e.activation(o, i, AF.Exp, scale=-1.0),
           reads=[t1_b], writes=[t1_b])
    P.emit("vector", lambda e, o=t1[:, :], i=t1[:, :]: e.tensor_scalar_add(o, i, 1.0),
           reads=[t1_b], writes=[t1_b])
    P.emit("scalar", lambda e, o=t1[:, :], i=t1[:, :]: e.activation(o, i, AF.Ln),
           reads=[t1_b], writes=[t1_b])
    P.emit("vector", lambda e, o=t2[:, :], i=ad[:, :]: e.tensor_scalar_max(o, i, 0.0),
           reads=[ad_b], writes=[t2_b])
    P.emit("vector", lambda e, o=t2[:, :], a=t2[:, :], b=t1[:, :]: e.tensor_add(o, a, b),
           reads=[t2_b, t1_b], writes=[t2_b])
    nea, nea_b = P.sb("nea", [16, 1], F32)
    P.dma(nea[:, :], alogc[:, :], nea_b, src_buf=alogc_b)
    P.emit("scalar", lambda e, o=nea[:, :], i=nea[:, :]: e.activation(o, i, AF.Exp), reads=[nea_b], writes=[nea_b])
    P.emit("vector", lambda e, o=nea[:, :], i=nea[:, :]: e.tensor_scalar_mul(o, i, -1.0), reads=[nea_b], writes=[nea_b])
    P.emit("vector", lambda e, o=t2[:, :], i=t2[:, :], s=nea[:, :]: e.tensor_scalar_mul(o, i, s),
           reads=[t2_b, nea_b], writes=[t2_b])
    for h in range(2):
        P.emit("vector", lambda e, o=t1[:, h * TOK:(h + 1) * TOK], a=t2[:, h * TOK:(h + 1) * TOK],
               b=t2[:, h * TOK:(h + 1) * TOK]:
               e.tensor_tensor_scan(o, a, b, 0.0, ALU.add, ALU.bypass),
               reads=[t2_b], writes=[t1_b])
    P.dma(g_o[:, :], t1[:, :], g_ob, src_buf=t1_b)


def emit_gdnchunk(P, W, qin_t, kin_t, vin_t, beta_t, g_t, rsin_t):
    S = 2048
    NCH = S // CC
    qin, qin_b = qin_t
    kin, kin_b = kin_t
    vin, vin_b = vin_t
    beta_i, beta_ib = beta_t
    g_i, g_ib = g_t
    rsin_ap, rsin_b = rsin_t
    cwq, cwq_b = W["cwq"]
    cwk, cwk_b = W["cwk"]
    cwv, cwv_b = W["cwv"]
    monorm, monorm_b = W["monorm"]
    nw_i, nw_ib = W["nw"]
    maskw, maskw_b = W["maskw"]
    maskat, maskat_b = W["maskat"]
    iden, iden_b = W["iden"]
    ones1, ones1_b = W["ones1"]
    ones128, ones128_b = W["ones128"]

    psf = [P.ps(f"psf{k}", [128, 512], F32) for k in range(5)]
    psb = [P.ps(f"psb{k}", [128, 1024], BF16) for k in range(3)]
    mk_sb, mk_b = P.sb("mks", [128, 2], F32)
    P.dma(mk_sb[:, :], monorm[:, :], mk_b, src_buf=monorm_b)
    mscr = [P.sb("mscr0", [128, S], BF16)]
    rsv = rsin_ap.rearrange("(j r) p t -> r p j t", j=2)
    o1_sb, o1_b = P.sb("o1", [1, 128], F32)
    P.dma(o1_sb[:, :], ones1[:, :], o1_b, src_buf=ones1_b)
    o128_sb, o128_b = P.sb("o128", [128, 1], F32)
    P.dma(o128_sb[:, :], ones128[:, :], o128_b, src_buf=ones128_b)
    idf, idf_b = P.sb("idf", [128, 128], F32)
    P.dma(idf[:, :], iden[:, :], idf_b, src_buf=iden_b)
    idb, idb_b = P.sb("idb", [128, 128], BF16)
    P.dma(idb[:, :], iden[:, :], idb_b, src_buf=iden_b, cast=True)
    mw_sb, mw_b = P.sb("mw", [128, 128], F32)
    P.dma(mw_sb[:, :], maskw[:, :], mw_b, src_buf=maskw_b)
    ma_sb, ma_b = P.sb("ma", [128, 128], F32)
    P.dma(ma_sb[:, :], maskat[:, :], ma_b, src_buf=maskat_b)
    bsb, bsb_b = P.sb("bsb", [16, S], F32)
    P.dma(bsb[:, :], beta_i[:, :], bsb_b, src_buf=beta_ib)
    gsb, gsb_b = P.sb("gsb", [16, S], F32)
    P.dma(gsb[:, :], g_i[:, :], gsb_b, src_buf=g_ib)
    nws, nws_b = P.sb("nws", [1, 128], F32)
    P.dma(nws[:, :], nw_i[:, :], nws_b, src_buf=nw_ib)
    nwb, nwb_b = P.sb("nwb", [128, 128], F32)
    ph, pb = psf[0]
    P.mm(ph[:, 0:128], pb, [(o1_sb[:, :], nws[:, :])], [o1_b, nws_b])
    P.emit("scalar", lambda e, o=nwb[:, :], i=ph[:, 0:128]: e.copy(o, i), reads=[pb], writes=[nwb_b])

    cwq_s, cwq_sb = P.sb("cwqs", [128, 32], F32)
    P.dma(cwq_s[:, :].rearrange("p (i j) -> p i j", j=4), cwq.rearrange("i p j -> p i j"), cwq_sb, src_buf=cwq_b)
    cwk_s, cwk_sb = P.sb("cwks", [128, 32], F32)
    P.dma(cwk_s[:, :].rearrange("p (i j) -> p i j", j=4), cwk.rearrange("i p j -> p i j"), cwk_sb, src_buf=cwk_b)
    cwv_s, cwv_sb = P.sb("cwvs", [128, 64], F32)
    P.dma(cwv_s[:, :].rearrange("p (i j) -> p i j", j=4), cwv.rearrange("i p j -> p i j"), cwv_sb, src_buf=cwv_b)
    pad = [P.sb(f"pad{k}", [128, S + 3], BF16) for k in range(2)]
    acc = [P.sb(f"acc{k}", [128, 512], F32) for k in range(2)]
    sqs, sqs_b = P.sb("sqs", [128, 512], F32)
    rsr, rsr_b = P.sb("rsr", [1, S], F32)
    qt = [P.sb(f"qt{i}", [128, S], BF16) for i in range(8)]
    kt = [P.sb(f"kt{i}", [128, S], BF16) for i in range(8)]
    vt = [P.sb(f"vt{i}", [128, S], BF16) for i in range(16)]

    def conv_tile(dst, dst_b, src_ap, src_b, cw_ap, cw_b, idx, norm_scale):
        ph_, pb_ = pad[idx % 2]
        P.emit("vector", lambda e, o=ph_[:, 0:3]: e.memset(o, 0.0), writes=[pb_])
        P.dma(ph_[:, 3:S + 3], src_ap, pb_, src_buf=src_b)
        for t4 in range(4):
            off = t4 * 512
            sl = slice(off, off + 512)
            ah, ab = acc[t4 % 2]
            P.emit("vector", lambda e, o=ah[:, :], i=ph_[:, off:off + 512], s=cw_ap[:, 0:1]:
                   e.tensor_scalar_mul(o, i, s),
                   reads=[pb_, cw_b], writes=[ab])
            for j in range(1, 4):
                P.emit("vector", lambda e, o=ah[:, :], i=ph_[:, off + j:off + j + 512], s=cw_ap[:, j:j + 1], b=ah[:, :]:
                       e.scalar_tensor_tensor(o, i, s, b, ALU.mult, ALU.add),
                       reads=[pb_, cw_b, ab], writes=[ab])
            if norm_scale is None:
                P.emit("scalar", lambda e, o=dst[:, sl], i=ah[:, :]: e.activation(o, i, AF.Silu),
                       reads=[ab], writes=[dst_b])
                continue
            P.emit("scalar", lambda e, o=ah[:, :], i=ah[:, :]: e.activation(o, i, AF.Silu),
                   reads=[ab], writes=[ab])
            P.emit("vector", lambda e, o=sqs[:, :], a=ah[:, :]: e.tensor_mul(o, a, a), reads=[ab], writes=[sqs_b])
            ph2, pb2 = psf[t4 % 2]
            P.mm(ph2[0:1, :], pb2, [(o128_sb[:, :], sqs[:, :])], [o128_b, sqs_b])
            P.emit("vector", lambda e, o=rsr[0:1, sl], i=ph2[0:1, :]: e.tensor_scalar_add(o, i, 1e-6),
                   reads=[pb2], writes=[rsr_b])
            P.emit("scalar", lambda e, o=rsr[0:1, sl], i=rsr[0:1, sl]: e.activation(o, i, AF.Ln),
                   reads=[rsr_b], writes=[rsr_b])
            P.emit("scalar", lambda e, o=rsr[0:1, sl], i=rsr[0:1, sl]: e.activation(o, i, AF.Exp, scale=-0.5),
                   reads=[rsr_b], writes=[rsr_b])
            ph3, pb3 = psf[2 + t4 % 2]
            P.mm(ph3[:, :], pb3, [(o1_sb[:, :], rsr[0:1, sl])], [o1_b, rsr_b])
            P.emit("vector", lambda e, o=dst[:, sl], i=ah[:, :], b=ph3[:, :]:
                   e.scalar_tensor_tensor(o, i, norm_scale, b, ALU.mult, ALU.mult),
                   reads=[ab, pb3], writes=[dst_b])

    for i in range(8):
        conv_tile(qt[i][0], qt[i][1], qin[i], qin_b, cwq_s[:, i * 4:(i + 1) * 4], cwq_sb, i, float(DK ** -0.5))
    for i in range(8):
        conv_tile(kt[i][0], kt[i][1], kin[i], kin_b, cwk_s[:, i * 4:(i + 1) * 4], cwk_sb, i, 1.0)
    for i in range(16):
        conv_tile(vt[i][0], vt[i][1], vin[i], vin_b, cwv_s[:, i * 4:(i + 1) * 4], cwv_sb, i, None)

    gbt = [P.sb(f"gbt{c}", [128, 32], F32) for c in range(NCH)]
    npeb = [P.sb(f"npeb{c}", [128, 16], F32) for c in range(NCH)]
    ebt = [P.sb(f"ebt{c}", [128, 16], F32) for c in range(NCH)]
    acb = [P.sb(f"acb{c}", [128, 16], F32) for c in range(NCH)]
    rowt, rowt_b = P.sb("rowt", [1, 16], F32)
    for c in range(NCH):
        cs = slice(c * CC, (c + 1) * CC)
        gh, gb = gbt[c]
        ph1, pb1 = psf[0]
        P.tr(ph1[0:128, 0:16], pb1, gsb[0:16, cs], gsb_b, idf[0:16, 0:16], idf_b)
        P.emit("scalar", lambda e, o=gh[:, 0:16], i=ph1[0:128, 0:16]: e.copy(o, i), reads=[pb1], writes=[gb])
        ph2, pb2 = psf[1]
        P.tr(ph2[0:128, 0:16], pb2, bsb[0:16, cs], bsb_b, idf[0:16, 0:16], idf_b)
        P.emit("scalar", lambda e, o=gh[:, 16:32], i=ph2[0:128, 0:16]: e.copy(o, i), reads=[pb2], writes=[gb])
        nh, nb = npeb[c]
        if c % (NCH // 2) == 0:
            P.emit("vector", lambda e, o=nh[:, :]: e.memset(o, 0.0), writes=[nb])
        else:
            ph3, pb3 = psf[2]
            P.tr(ph3[0:1, 0:16], pb3, gsb[0:16, c * CC - 1:c * CC], gsb_b, idf[0:16, 0:16], idf_b)
            P.emit("scalar", lambda e, o=rowt[:, :], i=ph3[0:1, 0:16]: e.copy(o, i), reads=[pb3], writes=[rowt_b])
            ph4, pb4 = psf[3]
            P.mm(ph4[:, 0:16], pb4, [(o1_sb[:, :], rowt[:, :])], [o1_b, rowt_b])
            P.emit("scalar", lambda e, o=nh[:, :], i=ph4[:, 0:16]: e.mul(o, i, -1.0), reads=[pb4], writes=[nb])
        eh, eb = ebt[c]
        ph3, pb3 = psf[2]
        P.tr(ph3[0:1, 0:16], pb3, gsb[0:16, (c + 1) * CC - 1:(c + 1) * CC], gsb_b, idf[0:16, 0:16], idf_b)
        P.emit("scalar", lambda e, o=rowt[:, :], i=ph3[0:1, 0:16]: e.copy(o, i), reads=[pb3], writes=[rowt_b])
        ph4, pb4 = psf[3]
        P.mm(ph4[:, 0:16], pb4, [(o1_sb[:, :], rowt[:, :])], [o1_b, rowt_b])
        P.emit("scalar", lambda e, o=eh[:, :], i=ph4[:, 0:16]: e.copy(o, i), reads=[pb4], writes=[eb])
        ah2, ab2 = acb[c]
        P.emit("vector", lambda e, o=ah2[:, :], a=eh[:, :], b=nh[:, :]: e.tensor_add(o, a, b),
               reads=[eb, nb], writes=[ab2])
        P.emit("scalar", lambda e, o=ah2[:, :], i=ah2[:, :]: e.activation(o, i, AF.Exp), reads=[ab2], writes=[ab2])

    kkp = [P.sb(f"kk{k}", [128, 128], BF16) for k in range(2)]
    pqkp = [P.sb(f"pqk{k}", [128, 128], BF16) for k in range(2)]
    knp = [P.sb(f"kn{k}", [128, 128], BF16) for k in range(2)]
    SC = []
    for s_ in range(2):
        d = dict(
            cgrow=P.sb(f"cgrow{s_}", [1, 128], F32),
            cgb=P.sb(f"cgb{s_}", [128, 128], F32),
            t1w=P.sb(f"t1w{s_}", [128, 128], F32),
            ew=P.sb(f"ew{s_}", [128, 128], F32),
            vn=P.sb(f"vn{s_}", [128, 128], BF16),
            w=P.sb(f"wm{s_}", [128, 128], BF16),
            wt=P.sb(f"wtm{s_}", [128, 128], BF16),
            pw=[P.sb(f"pw{s_}_{k}", [128, 128], BF16) for k in range(2)],
            pts=[P.sb(f"pts{s_}_{k}", [128, 128], BF16) for k in range(7)],
            at=P.sb(f"atm{s_}", [128, 128], BF16),
            rr=P.sb(f"rr{s_}", [128, 256], BF16),
            uu=[P.sb(f"uu{s_}_{k}", [128, 256], BF16) for k in range(2)],
            ukt=P.sb(f"ukt{s_}", [128, 128], BF16),
            tc=P.sb(f"tc{s_}", [128, 128], BF16),
            alc=P.sb(f"alc{s_}", [128, 1], F32),
            abc=P.sb(f"abc{s_}", [128, 1], F32),
            ksc=P.sb(f"ksc{s_}", [128, 1], F32),
            ktl=P.sb(f"ktl{s_}", [128, 128], BF16),
            of=P.sb(f"of{s_}", [128, 128], F32),
            on=P.sb(f"on{s_}", [128, 128], BF16),
            ovar=P.sb(f"ovar{s_}", [128, 1], F32),
        )
        d["fA"] = psf[2 * s_]
        d["fB"] = psf[2 * s_ + 1]
        d["bT"] = psb[s_]
        SC.append(d)
    pre_f0 = psf[4]
    pre_f1 = psf[4]
    pre_b0 = psb[2]
    st32 = [P.sb(f"st32_{k}", [128, 128], F32) for k in range(2)]
    stbf = [P.sb(f"stbf_{k}", [128, 128], BF16) for k in range(2)]
    ohead = [P.sb(f"oh{k}", [128, S], BF16) for k in range(2)]

    def sub_ops(kh, c, sub):
        ops = []

        def E(*a, **k):
            ops.append(lambda a=a, k=k: P.emit(*a, **k))

        def M(*a, **k):
            ops.append(lambda a=a, k=k: P.mm(*a, **k))

        def T(*a):
            ops.append(lambda a=a: P.tr(*a))

        S_ = SC[sub]
        cs = slice(c * CC, (c + 1) * CC)
        h = 2 * kh + sub
        qh, qb = qt[kh]
        vh_, vb_ = vt[h]
        kkh, kkb = kkp[c % 2]
        pqh, pqb = pqkp[c % 2]
        knh, knb = knp[c % 2]
        sh32, sb32 = st32[sub]
        shbf, sbbf = stbf[sub]
        cgrow, cgrow_b = S_["cgrow"]
        cgb, cgb_b = S_["cgb"]
        t1w, t1w_b = S_["t1w"]
        ew, ew_b = S_["ew"]
        vn_sb, vn_b = S_["vn"]
        w_sb, w_b = S_["w"]
        wt_sb, wt_b = S_["wt"]
        pw = S_["pw"]
        pts = S_["pts"]
        at_sb, at_b = S_["at"]
        rr, rr_b = S_["rr"]
        uu = S_["uu"]
        ukt_sb, ukt_b = S_["ukt"]
        tc_sb, tc_b = S_["tc"]
        alc, alc_b = S_["alc"]
        abc, abc_b = S_["abc"]
        ksc, ksc_b = S_["ksc"]
        ktl, ktl_b = S_["ktl"]
        of_sb, of_b = S_["of"]
        on_sb, on_b = S_["on"]
        ovar, ovar_b = S_["ovar"]
        fA = S_["fA"]
        fB = S_["fB"]
        bT_ = S_["bT"]
        gcol = gbt[c][0][:, h:h + 1]
        gcol_b = gbt[c][1]
        bcol = gbt[c][0][:, 16 + h:16 + h + 1]
        ph3, pb3 = fA
        T(ph3[0:1, 0:128], pb3, gcol, gcol_b, idf[:, :], idf_b)
        E("scalar", lambda e, o=cgrow[:, :], i=ph3[0:1, 0:128]: e.copy(o, i), reads=[pb3], writes=[cgrow_b])
        ph4, pb4 = fB
        M(ph4[:, 0:128], pb4, [(o1_sb[:, :], cgrow[:, :])], [o1_b, cgrow_b])
        E("scalar", lambda e, o=cgb[:, :], i=ph4[:, 0:128]: e.copy(o, i), reads=[pb4], writes=[cgb_b])
        E("vector", lambda e, o=t1w[:, :], i=cgb[:, :], s=gcol, b=mw_sb[:, :]:
          e.scalar_tensor_tensor(o, i, s, b, ALU.subtract, ALU.add),
          reads=[cgb_b, gcol_b, mw_b], writes=[t1w_b])
        E("scalar", lambda e, o=ew[:, :], i=t1w[:, :]: e.activation(o, i, AF.Exp, scale=-1.0),
          reads=[t1w_b], writes=[ew_b])
        E("vector", lambda e, o=w_sb[:, :], i=ew[:, :], s=bcol, b=kkh[:, :]:
          e.scalar_tensor_tensor(o, i, s, b, ALU.mult, ALU.mult),
          reads=[ew_b, gcol_b, kkb], writes=[w_b])
        pbh, pbb = bT_
        T(pbh[:, 0:128], pbb, w_sb[:, :], w_b, idb[:, :], idb_b)
        E("scalar", lambda e, o=wt_sb[:, :], i=pbh[:, 0:128]: e.copy(o, i), reads=[pbb], writes=[wt_b])
        E("vector", lambda e, o=t1w[:, :], i=cgb[:, :], s=gcol, b=ma_sb[:, :]:
          e.scalar_tensor_tensor(o, i, s, b, ALU.subtract, ALU.subtract),
          reads=[cgb_b, gcol_b, ma_b], writes=[t1w_b])
        E("scalar", lambda e, o=ew[:, :], i=t1w[:, :]: e.activation(o, i, AF.Exp),
          reads=[t1w_b], writes=[ew_b])
        E("vector", lambda e, o=at_sb[:, :], a=ew[:, :], b=pqh[:, :]: e.tensor_mul(o, a, b),
          reads=[ew_b, pqb], writes=[at_b])
        E("scalar", lambda e, o=alc[:, :], i=gcol, b=npeb[c][0][:, h:h + 1]:
          e.activation(o, i, AF.Exp, bias=b),
          reads=[gcol_b, npeb[c][1]], writes=[alc_b])
        E("vector", lambda e, o=abc[:, :], a=alc[:, :], b=bcol: e.tensor_mul(o, a, b),
          reads=[alc_b, gcol_b], writes=[abc_b])
        E("scalar", lambda e, o=ksc[:, :], i=gcol, b=ebt[c][0][:, h:h + 1]:
          e.activation(o, i, AF.Exp, scale=-1.0, bias=b),
          reads=[gcol_b, ebt[c][1]], writes=[ksc_b])
        pbh2, pbb2 = bT_
        T(pbh2[:, 0:128], pbb2, vh_[:, cs], vb_, idb[:, :], idb_b)
        E("scalar", lambda e, o=vn_sb[:, :], i=pbh2[:, 0:128]: e.copy(o, i), reads=[pbb2], writes=[vn_b])
        E("vector", lambda e, o=rr[:, 0:128], i=vn_sb[:, :], s=bcol: e.tensor_scalar_mul(o, i, s),
          reads=[vn_b, gcol_b], writes=[rr_b])
        E("vector", lambda e, o=rr[:, 128:256], i=knh[:, :], s=abc[:, :]: e.tensor_scalar_mul(o, i, s),
          reads=[knb, abc_b], writes=[rr_b])
        E("vector", lambda e, o=pw[0][0][:, :], i=w_sb[:, :]: e.tensor_copy(o, i),
          reads=[w_b], writes=[pw[0][1]])
        E("vector", lambda e, o=pts[0][0][:, :], i=wt_sb[:, :]: e.tensor_copy(o, i),
          reads=[wt_b], writes=[pts[0][1]])
        for lvl in range(6):
            src_p, src_pb = pw[lvl % 2]
            dst_p, dst_pb = pw[(lvl + 1) % 2]
            pt_s, pt_sb = pts[lvl]
            pt_d, pt_db = pts[lvl + 1]
            pbh3, pbb3 = fA
            if lvl < 5:
                M(pbh3[:, 0:128], pbb3, [(pt_s[:, :], src_p[:, :])], [pt_sb, src_pb])
                E("scalar", lambda e, o=dst_p[:, :], i=pbh3[:, 0:128]: e.copy(o, i),
                  reads=[pbb3], writes=[dst_pb])
            pbh4, pbb4 = fB
            M(pbh4[:, 0:128], pbb4, [(src_p[:, :], pt_s[:, :])], [src_pb, pt_sb])
            E("scalar", lambda e, o=pt_d[:, :], i=pbh4[:, 0:128]: e.copy(o, i),
              reads=[pbb4], writes=[pt_db])
        E("vector", lambda e, o=uu[0][0][:, :], i=rr[:, :]: e.tensor_copy(o, i),
          reads=[rr_b], writes=[uu[0][1]])
        cu = 0
        for i in [6, 5, 4, 3, 2, 1]:
            ph5, pb5 = fA
            M(ph5[:, 0:256], pb5, [(pts[i][0][:, :], uu[cu][0][:, :])], [pts[i][1], uu[cu][1]])
            E("vector", lambda e, o=uu[1 - cu][0][:, :], a=uu[cu][0][:, :], b=ph5[:, 0:256]:
              e.tensor_add(o, a, b), reads=[uu[cu][1], pb5], writes=[uu[1 - cu][1]])
            cu = 1 - cu
        ph5, pb5 = fA
        M(ph5[:, 0:256], pb5, [(pts[0][0][:, :], uu[cu][0][:, :])], [pts[0][1], uu[cu][1]])
        E("vector", lambda e, o=uu[1 - cu][0][:, :], a=uu[cu][0][:, :], b=ph5[:, 0:256]:
          e.tensor_sub(o, a, b), reads=[uu[cu][1], pb5], writes=[uu[1 - cu][1]])
        cu = 1 - cu
        uh, ub_ = uu[cu]
        pbh5, pbb5 = bT_
        T(pbh5[:, 0:128], pbb5, uh[:, 128:256], ub_, idb[:, :], idb_b)
        E("scalar", lambda e, o=ukt_sb[:, :], i=pbh5[:, 0:128]: e.copy(o, i),
          reads=[pbb5], writes=[ukt_b])
        ph6, pb6 = fA
        M(ph6[:, 0:128], pb6, [(ukt_sb[:, :], shbf[:, :])], [ukt_b, sbbf])
        E("vector", lambda e, o=tc_sb[:, :], a=uh[:, 0:128], b=ph6[:, 0:128]:
          e.tensor_sub(o, a, b), reads=[ub_, pb6], writes=[tc_b])
        ph7, pb7 = fB
        M(ph7[:, 0:128], pb7, [(qh[:, cs], shbf[:, :])], [qb, sbbf])
        E("vector", lambda e, o=of_sb[:, :], i=ph7[:, 0:128], s=alc[:, :]:
          e.tensor_scalar_mul(o, i, s), reads=[pb7, alc_b], writes=[of_b])
        ph8, pb8 = fA
        M(ph8[:, 0:128], pb8, [(at_sb[:, :], tc_sb[:, :])], [at_b, tc_b])
        E("vector", lambda e, o=of_sb[:, :], a=of_sb[:, :], b=ph8[:, 0:128]:
          e.tensor_add(o, a, b), reads=[of_b, pb8], writes=[of_b])
        E("vector", lambda e, o=ew[:, :], a=of_sb[:, :], b=of_sb[:, :]: e.tensor_mul(o, a, b),
          reads=[of_b], writes=[ew_b])
        E("vector", lambda e, o=ovar[:, :], i=ew[:, :]: e.reduce_sum(o, i, AX.X),
          reads=[ew_b], writes=[ovar_b])
        E("vector", lambda e, o=ovar[:, :], i=ovar[:, :]: e.tensor_scalar(o, i, 1.0 / DV, 1e-6, ALU.mult, ALU.add),
          reads=[ovar_b], writes=[ovar_b])
        E("scalar", lambda e, o=ovar[:, :], i=ovar[:, :]: e.activation(o, i, AF.Ln),
          reads=[ovar_b], writes=[ovar_b])
        E("scalar", lambda e, o=ovar[:, :], i=ovar[:, :]: e.activation(o, i, AF.Exp, scale=-0.5),
          reads=[ovar_b], writes=[ovar_b])
        E("vector", lambda e, o=on_sb[:, :], i=of_sb[:, :], s=ovar[:, :], b=nwb[:, :]:
          e.scalar_tensor_tensor(o, i, s, b, ALU.mult, ALU.mult),
          reads=[of_b, ovar_b, nwb_b], writes=[on_b])
        oh, ob = ohead[sub]
        pbh6, pbb6 = bT_
        T(pbh6[:, 0:128], pbb6, on_sb[:, :], on_b, idb[:, :], idb_b)
        E("scalar", lambda e, o=oh[:, cs], i=pbh6[:, 0:128]: e.copy(o, i),
          reads=[pbb6], writes=[ob])
        E("vector", lambda e, o=ktl[:, :], i=knh[:, :], s=ksc[:, :]: e.tensor_scalar_mul(o, i, s),
          reads=[knb, ksc_b], writes=[ktl_b])
        ph9, pb9 = fB
        M(ph9[:, 0:128], pb9, [(ktl[:, :], tc_sb[:, :])], [ktl_b, tc_b])
        E("vector", lambda e, o=sh32[:, :], i=sh32[:, :], s=acb[c][0][:, h:h + 1], b=ph9[:, 0:128]:
          e.scalar_tensor_tensor(o, i, s, b, ALU.mult, ALU.add),
          reads=[sb32, acb[c][1], pb9], writes=[sb32])
        E("vector", lambda e, o=shbf[:, :], i=sh32[:, :]: e.tensor_copy(o, i),
          reads=[sb32], writes=[sbbf])
        return ops

    for kh in range(8):
        qh, qb = qt[kh]
        khh, khb = kt[kh]
        for sub in range(2):
            sh, sb_ = st32[sub]
            P.emit("vector", lambda e, o=sh[:, :]: e.memset(o, 0.0), writes=[sb_])
            bh, bb = stbf[sub]
            P.emit("vector", lambda e, o=bh[:, :]: e.memset(o, 0.0), writes=[bb])
        for c in range(NCH):
            cs = slice(c * CC, (c + 1) * CC)
            kkh, kkb = kkp[c % 2]
            pqh, pqb = pqkp[c % 2]
            knh, knb = knp[c % 2]
            ph1, pb1 = pre_f0
            P.mm(ph1[:, 0:128], pb1, [(khh[:, cs], khh[:, cs])], [khb])
            P.emit("scalar", lambda e, o=kkh[:, :], i=ph1[:, 0:128]: e.copy(o, i), reads=[pb1], writes=[kkb])
            ph2, pb2 = pre_f1
            P.mm(ph2[:, 0:128], pb2, [(khh[:, cs], qh[:, cs])], [khb, qb])
            P.emit("scalar", lambda e, o=pqh[:, :], i=ph2[:, 0:128]: e.copy(o, i), reads=[pb2], writes=[pqb])
            pb_h, pb_b = pre_b0
            P.tr(pb_h[:, 0:128], pb_b, khh[:, cs], khb, idb[:, :], idb_b)
            P.emit("scalar", lambda e, o=knh[:, :], i=pb_h[:, 0:128]: e.copy(o, i), reads=[pb_b], writes=[knb])
            o0 = sub_ops(kh, c, 0)
            o1 = sub_ops(kh, c, 1)
            for a, b in zip(o0, o1):
                a()
                b()
        for sub in range(2):
            h = 2 * kh + sub
            oh, ob = ohead[sub]
            for blk in range(2):
                mh2, mb2 = mscr[0]
                P.emit("vector", lambda e, o=mh2[:, :], i=oh[:, :], s=mk_sb[:, blk:blk + 1]:
                       e.tensor_scalar_mul(o, i, s), reads=[ob, mk_b], writes=[mb2])
                P.dma(rsv[16 * blk + h], mh2[:, :].rearrange("p (j t) -> p j t", j=2),
                      rsin_b, src_buf=mb2)


def emit_gdnout(P, x_t, xo_t, onorm_t, zs_t, wout_t):
    x_ap, x_b = x_t
    xo_ap, xo_b = xo_t
    onorm, onorm_b = onorm_t
    zs, zs_b = zs_t
    wout, wout_b = wout_t

    psf, psb, scr = _alloc_common(P)
    xt = [P.sb(f"x{k}", [128, TOK], F32) for k in range(2)]
    xn = [P.sb(f"xn{k}", [128, TOK], F32) for k in range(2)]
    of = [P.sb(f"of{i}", [128, TOK], BF16) for i in range(32)]
    zt = [P.sb(f"zt{k}", [128, TOK], BF16) for k in range(2)]
    for h in range(32):
        zh, zb = zt[h % 2]
        P.dma(zh[:, :], zs[h], zb, src_buf=zs_b)
        oh, ob = of[h]
        P.dma(oh[:, :], onorm[h], ob, src_buf=onorm_b)
        P.emit("vector", lambda e, o=oh[:, :], a=oh[:, :], b=zh[:, :]: e.tensor_mul(o, a, b),
               reads=[ob, zb], writes=[ob])
    wo_s = [P.sb(f"wo{k}", [128, 4096], BF16) for k in range(2)]
    wor = wout.rearrange("(h p) (d c) -> d p h c", p=128, c=128)
    xr = x_ap.rearrange("(i p) t -> i p t", p=128)
    xor_ = xo_ap.rearrange("(i p) t -> i p t", p=128)
    for d in range(ND):
        wh, wb = wo_s[d % 2]
        P.dma(wh[:, :], wor[d], wb, src_buf=wout_b, cast=True)
        xh, xb = xt[d % 2]
        P.dma(xh[:, :], xr[d], xb, src_buf=x_b)
        xnh, xnb = xn[d % 2]
        for tb in range(2):
            sl = slice(tb * TB, (tb + 1) * TB)
            ph, pb = psf[2 + tb]
            pairs = [(wh[:, hh * 128:(hh + 1) * 128], of[hh][0][:, sl]) for hh in range(32)]
            P.mm(ph[:, :], pb, pairs, [wb] + [of[hh][1] for hh in range(32)])
            P.emit("vector", lambda e, o=xnh[:, sl], a=xh[:, sl], b=ph[:, :]: e.tensor_add(o, a, b),
                   reads=[xb, pb], writes=[xnb])
        P.dma(xor_[d], xnh[:, :], xo_b, src_buf=xnb)


def _f32(a):
    return np.ascontiguousarray(a, dtype=np.float32)

def kernel(**inp):
    C = host_consts()

    x = _f32(inp["L0_attn_x"])
    xs = []
    for c in range(NCORE):
        b, hf = c // 2, c % 2
        xs.append(np.ascontiguousarray(x[b, hf * TOK:(hf + 1) * TOK, :].T))

    WupT = np.ascontiguousarray(_f32(inp["L0_ffn_W_up"]).T)
    WgT = np.ascontiguousarray(_f32(inp["L0_ffn_W_gate"]).T)
    WdT = np.ascontiguousarray(_f32(inp["L0_ffn_W_down"]).T)
    Wdp = np.ascontiguousarray(WdT.reshape(64, 64, 2, D).transpose(1, 0, 2, 3).reshape(64, 128, D))
    su_ug = np.ascontiguousarray(_f32(inp["L0_ffn_SU_ug"]).reshape(16, 128).T) / np.sqrt(np.float32(D))
    su_dp = np.ascontiguousarray(_f32(inp["L0_ffn_SU_dp"]).reshape(64, 128).T)
    ws_up = np.ascontiguousarray(_f32(inp["L0_ffn_Wscale_ug"])[:INTER].reshape(64, 128).T)
    ws_g = np.ascontiguousarray(_f32(inp["L0_ffn_Wscale_ug"])[INTER:].reshape(64, 128).T)
    sw_up = np.ascontiguousarray(su_dp * ws_up)
    ws_d = np.ascontiguousarray(_f32(inp["L0_ffn_Wscale_dp"]).reshape(16, 128).T) * 0.5
    BT = np.ascontiguousarray(_f32(inp["L0_attn_B"]).T)
    CT = np.ascontiguousarray(_f32(inp["L0_attn_C"]).T)
    Dv = np.ascontiguousarray(_f32(inp["L0_attn_D"]).reshape(16, 128).T)

    Wr = _f32(inp["L1_ffn_Wr"])
    br = _f32(inp["L1_ffn_br"]).reshape(8, 1)
    WdS = np.ascontiguousarray(_f32(inp["L1_ffn_Wd"]).transpose(1, 0, 2).reshape(D, 128))
    bdS = _f32(inp["L1_ffn_bd"]).reshape(128, 1)
    WuS = np.ascontiguousarray(_f32(inp["L1_ffn_Wu"]).reshape(128, D))
    bu = _f32(inp["L1_ffn_bu"])

    wqkv = _f32(inp["L1_attn_w_qkv"])
    wz = _f32(inp["L1_attn_w_z"])
    wba = np.ascontiguousarray(np.concatenate([_f32(inp["L1_attn_w_b"]), _f32(inp["L1_attn_w_a"])], axis=1))
    alog = _f32(inp["L1_attn_A_log"]).reshape(32, 1)
    dtb = _f32(inp["L1_attn_dt_bias"]).reshape(32, 1)
    cwr = np.ascontiguousarray(_f32(inp["L1_attn_conv_w"]).reshape(64, 128, 4))
    nw = _f32(inp["L1_attn_norm_weight"]).reshape(1, 128)
    wout = _f32(inp["L1_attn_w_out"])

    wqkvc, wbacs, alogcs, dtbcs, cwqs, cwks, cwvs, monorms = [], [], [], [], [], [], [], []
    for hg in range(2):
        wqkvc.append(np.ascontiguousarray(np.concatenate(
            [wqkv[:, 1024 * hg:1024 * hg + 1024],
             wqkv[:, 2048 + 1024 * hg:2048 + 1024 * hg + 1024],
             wqkv[:, 4096 + 2048 * hg:4096 + 2048 * hg + 2048]], axis=1)))
        wbacs.append(np.ascontiguousarray(np.concatenate(
            [wba[:, 16 * hg:16 * hg + 16], wba[:, 32 + 16 * hg:32 + 16 * hg + 16]], axis=1)))
        alogcs.append(np.ascontiguousarray(alog[16 * hg:16 * hg + 16]))
        dtbcs.append(np.ascontiguousarray(dtb[16 * hg:16 * hg + 16]))
        cwqs.append(np.ascontiguousarray(cwr[8 * hg:8 * hg + 8]))
        cwks.append(np.ascontiguousarray(cwr[16 + 8 * hg:16 + 8 * hg + 8]))
        cwvs.append(np.ascontiguousarray(cwr[32 + 16 * hg:32 + 16 * hg + 16]))
        monorms.append(np.tile(np.array([1.0 - hg, float(hg)], np.float32), (128, 1)))

    P = Prog()
    x0 = P.din("x", [D, TOK], F32)
    xo_t = P.dout("xo", [D, TOK], F32)
    ones1_t = P.din("ones1", [1, 128], F32)
    ones128_t = P.din("ones128", [128, 1], F32)
    wdp_t = P.din("wdp", [64, 128, D], F32)
    h64_t = P.din("h64i2", [128, 128], F32)
    iden_t = P.din("iden", [128, 128], F32)
    wdf_t = P.dint("wdf", [INTER, D], BF16)
    x1 = P.dint("x1", [D, TOK], F32)
    x2 = P.dint("x2", [D, TOK], F32)
    x3 = P.dint("x3", [D, TOK], F32)
    x4 = P.dint("x4", [D, TOK], F32)
    x5 = P.dint("x5", [D, TOK], F32)
    xnin = P.dint("xnin", [D, TOK], BF16)
    xnf = P.dint("xnf", [2 * D, TOK], BF16)
    zsd = P.dint("zsd", [32, 128, TOK], BF16)
    qind = P.dint("qind", [8, 128, 2048], BF16)
    kind_ = P.dint("kind", [8, 128, 2048], BF16)
    vind = P.dint("vind", [16, 128, 2048], BF16)
    betad = P.dint("betad", [16, 2048], F32)
    gd = P.dint("gd", [16, 2048], F32)
    rsin = P.dint("rsin", [64, 128, TOK], BF16)
    rsout = P.dint("rsout", [32, 128, TOK], BF16)

    W0 = dict(wupT=P.din("wupT", [D, INTER], F32), wgT=P.din("wgT", [D, INTER], F32),
              wdf=wdf_t, su_ug=P.din("su_ug", [128, 16], F32), sw_up=P.din("sw_up", [128, 64], F32),
              ws_g=P.din("ws_g", [128, 64], F32), ws_d=P.din("ws_d", [128, 16], F32),
              h128=P.din("h128", [128, 128], F32), ones1=ones1_t, ones128=ones128_t)
    WA = dict(bT=P.din("bT", [D, 1024], F32), cT=P.din("cT", [1024, D], F32),
              dvec=P.din("dvec", [128, 16], F32), ones1=ones1_t, ones128=ones128_t)
    WM = dict(wr=P.din("wr", [D, 8], F32), br=P.din("br", [8, 1], F32),
              wds=P.din("wds", [D, 128], F32), bds=P.din("bds", [128, 1], F32),
              wus=P.din("wus", [128, D], F32), bu=P.din("bu", [8, D], F32),
              e16=P.din("e16", [8, 128], F32), iden=iden_t, ones1=ones1_t, ones128=ones128_t)
    WZ = dict(wz=P.din("wz", [D, 4096], F32), ones1=ones1_t, ones128=ones128_t)
    WP = dict(wqkvc=P.din("wqkvc", [D, 4096], F32), wbac=P.din("wbac", [D, 32], F32),
              alogc=P.din("alogc", [16, 1], F32), dtbc=P.din("dtbc", [16, 1], F32))
    WC = dict(cwq=P.din("cwq", [8, 128, 4], F32), cwk=P.din("cwk", [8, 128, 4], F32),
              cwv=P.din("cwv", [16, 128, 4], F32), monorm=P.din("monorm", [128, 2], F32),
              nw=P.din("nw", [1, 128], F32), maskw=P.din("maskw", [128, 128], F32),
              maskat=P.din("maskat", [128, 128], F32), iden=iden_t,
              ones1=ones1_t, ones128=ones128_t)
    wout_t = P.din("wout", [4096, D], F32)

    P.push("s0_")
    emit_prep(P, wdp_t, h64_t, wdf_t)
    P.pop()
    P.push("s1_")
    emit_ffn0(P, x0, x1, W0)
    P.pop()
    P.push("s2_")
    emit_attn0(P, x1, x2, WA)
    P.pop()
    P.push("s3_")
    emit_ffn0(P, x2, x3, W0)
    P.pop()
    P.push("s4_")
    emit_moe(P, x3, x4, WM)
    P.pop()
    P.push("s5_")
    emit_gdnz(P, x4, WZ, xnin, xnf, zsd)
    P.pop()
    P.push("s6_")
    emit_gdnpc(P, xnf, WP, qind, kind_, vind, betad, gd)
    P.pop()
    P.push("s7_")
    emit_gdnchunk(P, WC, qind, kind_, vind, betad, gd, rsin)
    P.cc("ReduceScatter", ALU.add, rsin[0], rsin[1], rsout[0], rsout[1], "cc1")
    P.pop()
    P.push("s8_")
    emit_gdnout(P, x4, x5, rsout, zsd, wout_t)
    P.pop()
    P.push("s9_")
    emit_moe(P, x5, xo_t, WM, final=True)
    P.pop()

    maps = []
    for c in range(NCORE):
        hg = c % 2
        maps.append(dict(
            x=xs[c], ones1=C["ONES1"], ones128=C["ONES128"], wdp=Wdp, h64i2=C["H64I2"],
            iden=C["IDEN"], wupT=WupT, wgT=WgT, su_ug=su_ug, sw_up=sw_up, ws_g=ws_g,
            ws_d=ws_d, h128=C["H128"], bT=BT, cT=CT, dvec=Dv, wr=Wr, br=br, wds=WdS,
            bds=bdS, wus=WuS, bu=bu, e16=C["E16"], wz=wz, wqkvc=wqkvc[hg],
            wbac=wbacs[hg], alogc=alogcs[hg], dtbc=dtbcs[hg], cwq=cwqs[hg],
            cwk=cwks[hg], cwv=cwvs[hg], monorm=monorms[hg], nw=nw, maskw=C["MASKW"],
            maskat=C["MASKAT"], wout=wout))

    r = P.run(maps, core_ids=list(range(NCORE)))
    out = np.empty((4, 2048, D), np.float32)
    for c in range(NCORE):
        b, hf = c // 2, c % 2
        out[b, hf * TOK:(hf + 1) * TOK, :] = np.asarray(r.results[c]["xo"], np.float32).T
    return out


# version 08
# speedup vs baseline: 1.1971x; 1.1577x over previous
import contextlib
import numpy as np
import ml_dtypes
import concourse.bass as bass
import concourse.mybir as mybir
from concourse.bass_utils import run_bass_kernel_spmd

F32 = mybir.dt.float32
BF16 = mybir.dt.bfloat16
AF = mybir.ActivationFunctionType
ALU = mybir.AluOpType
AX = mybir.AxisListType
BF = ml_dtypes.bfloat16

NCORE = 8
TOK = 1024
TB = 512
D = 2048
ND = 16
INTER = 8192
NI = 64
HK, HV, DK, DV = 16, 32, 128, 128
CC = 128

ENGS = ("sync", "scalar", "vector", "tensor", "gpsimd")
SERIALIZE = False
PRUNE = True
SER_SET = ()


class _Buf:
    __slots__ = ("w", "r")

    def __init__(self):
        self.w = None
        self.r = []


class _Ev:
    __slots__ = ("e", "c", "vc")

    def __init__(self, e, c, vc):
        self.e = e
        self.c = c
        self.vc = vc


def _clb(b):
    nb = _Buf()
    nb.w = b.w
    nb.r = list(b.r)
    return nb


RGROUPS = [[0, 1], [2, 3], [4, 5], [6, 7]]


class Prog:
    def __init__(self):
        self.nc = bass.Bass("TRN2", target_bir_lowering=False)
        self.stack = contextlib.ExitStack()
        self.plan = {e: [] for e in ENGS}
        self.cnt = dict.fromkeys(ENGS, 0)
        self.vc = {e: {} for e in ENGS}
        self.outev = {}
        self.pfx = ""
        self.scopes = []
        self.bar = None

    def push(self, pfx):
        self.pfx = pfx
        self.scopes.append(contextlib.ExitStack())

    def pop(self):
        self.barrier()
        self.scopes.pop().close()

    def barrier(self):
        self.bar = {e: self.cnt.get(e, 0) for e in ENGS}

    def din(self, name, shape, dt):
        h = self.nc.dram_tensor(name, list(shape), dt, kind="ExternalInput")
        return h.ap(), _Buf()

    def dout(self, name, shape, dt):
        h = self.nc.dram_tensor(name, list(shape), dt, kind="ExternalOutput")
        return h.ap(), _Buf()

    def dint(self, name, shape, dt):
        h = self.nc.dram_tensor(name, list(shape), dt)
        return h.ap(), _Buf()

    def sb(self, name, shape, dt):
        st = self.scopes[-1] if self.scopes else self.stack
        h = st.enter_context(self.nc.sbuf_tensor(self.pfx + name, list(shape), dt))
        return h, _Buf()

    def ps(self, name, shape, dt):
        st = self.scopes[-1] if self.scopes else self.stack
        h = st.enter_context(self.nc.psum_tensor(self.pfx + name, list(shape), dt))
        return h, _Buf()

    def emit(self, eng, fn, reads=(), writes=(), inc=1, out_dma=False, sem=None):
        deps = {}

        def add(ev):
            if ev is None:
                return
            cur = deps.get(ev.e)
            if cur is None or ev.c > cur.c:
                deps[ev.e] = ev

        for b in reads:
            if b is not None:
                add(b.w)
        for b in writes:
            if b is not None:
                add(b.w)
                for ev in b.r:
                    add(ev)
        if self.bar:
            for p, v in self.bar.items():
                if v > 0 and p != eng:
                    cur = deps.get(p)
                    if cur is None or v > cur.c:
                        deps[p] = _Ev(p, v, {})
        waits = []
        myvc = self.vc[eng]
        if SERIALIZE or SER_SET:
            ser = ENGS if SERIALIZE else SER_SET
            for p in ser:
                if p != eng and self.cnt[p] > 0:
                    ev0 = _Ev(p, self.cnt[p], {})
                    cur = deps.get(p)
                    if cur is None or ev0.c > cur.c:
                        deps[p] = ev0
        for p in list(deps):
            if p in ("sync", "gpsimd"):
                ev0 = deps[p]
                if ev0.c < self.cnt[p]:
                    deps[p] = _Ev(p, self.cnt[p], ev0.vc)
        for p in sorted(deps):
            ev = deps[p]
            if p == "tensor" and eng == "tensor":
                continue
            if PRUNE and myvc.get(p, 0) >= ev.c:
                continue
            waits.append((p, ev.c))
            for q, v in ev.vc.items():
                if myvc.get(q, 0) < v:
                    myvc[q] = v
            if myvc.get(p, 0) < ev.c:
                myvc[p] = ev.c
        key = sem or eng
        self.cnt[key] = self.cnt.get(key, 0) + inc
        ev = _Ev(key, self.cnt[key], dict(myvc))
        ev.vc[key] = ev.c
        self.plan[eng].append((waits, fn, inc, key))
        for b in writes:
            if b is not None:
                b.w = ev
                b.r = []
        for b in reads:
            if b is not None:
                b.r.append(ev)
        if out_dma:
            self.outev[eng] = self.cnt[eng]
        return ev

    def cc(self, kind, op, in_ap, in_buf, out_ap, out_buf, key):
        def fn(g, kind=kind, op=op, i=in_ap, o=out_ap):
            return g.collective_compute(kind, op, replica_groups=RGROUPS,
                                        ins=[i.opt()], outs=[o.opt()])
        self.emit("gpsimd", fn, reads=[in_buf], writes=[out_buf], inc=1, sem=key)

    def dma(self, dst_ap, src_ap, dst_buf, src_buf=None, cast=False, out=False, eng=None):
        e = eng or ("gpsimd" if cast else "sync")
        self.emit(e, lambda E, d=dst_ap, s=src_ap: E.dma_start(d, s),
                  reads=[src_buf] if src_buf is not None else [],
                  writes=[dst_buf] if dst_buf is not None else [],
                  inc=16, out_dma=out)

    def mm(self, ps_ap, ps_buf, pairs, reads, accum=False):
        n = len(pairs)

        def fn(te, pairs=tuple(pairs), o=ps_ap, n=n, accum=accum):
            ins = None
            for i, (l, r) in enumerate(pairs):
                ins = te.matmul(o, l, r, start=(i == 0 and not accum), stop=(i == n - 1))
            return ins
        self.emit("tensor", fn, reads=list(reads), writes=[ps_buf])

    def tr(self, ps_ap, ps_buf, in_ap, in_buf, iden_ap, iden_buf):
        self.emit("tensor", lambda te, o=ps_ap, i=in_ap, d=iden_ap: te.transpose(o, i, d),
                  reads=[in_buf, iden_buf], writes=[ps_buf])

    def run(self, in_maps, core_ids, trace=False):
        for e in ("sync", "gpsimd"):
            v = self.cnt[e]
            if v <= 0:
                continue
            tgt = "sync"
            myvc = self.vc[tgt]
            if myvc.get(e, 0) < v:
                self.plan[tgt].append(([(e, v)], None, 0, None))
                myvc[e] = v
        nc = self.nc
        sems = {}
        with contextlib.ExitStack() as st:
            for e in list(ENGS) + [k for k in self.cnt if k not in ENGS]:
                sems[e] = st.enter_context(nc.semaphore("S_" + e))
            with nc.Block() as block:
                def mk(eng_name):
                    def body(eng):
                        for waits, fn, inc, key in self.plan[eng_name]:
                            for (p, v) in waits:
                                eng.wait_ge(sems[p], v)
                            if fn is not None:
                                ins = fn(eng)
                                if inc:
                                    ins.then_inc(sems[key], inc)
                    return body
                for name, dec in (("sync", block.sync), ("scalar", block.scalar),
                                  ("vector", block.vector), ("tensor", block.tensor),
                                  ("gpsimd", block.gpsimd)):
                    if self.plan[name]:
                        dec(mk(name))
        self.stack.close()
        return run_bass_kernel_spmd(nc, in_maps, core_ids=core_ids, trace=trace)


def _had(n):
    H = np.array([[1.0]], np.float32)
    while H.shape[0] < n:
        H = np.block([[H, H], [H, -H]]).astype(np.float32)
    return H


def host_consts():
    c = {}
    c["H128"] = _had(128)
    H64 = _had(64)
    c["H64I2"] = np.kron(H64, np.eye(2, dtype=np.float32))
    c["IDEN"] = np.eye(128, dtype=np.float32)
    c["ONES1"] = np.ones((1, 128), np.float32)
    c["ONES128"] = np.ones((128, 1), np.float32)
    t = np.arange(128)
    c["MASKW"] = np.where(t[None, :] < t[:, None], 0.0, 1e9).astype(np.float32)
    c["MASKAT"] = np.where(t[:, None] <= t[None, :], 0.0, 1e9).astype(np.float32)
    E16 = np.zeros((8, 128), np.float32)
    for e in range(8):
        E16[e, e * 16:(e + 1) * 16] = 1.0
    c["E16"] = E16
    return c


def emit_ln_resident(P, xt, yt, psf, scr, ones128, ones1, post_cols=None, eps=1e-5):
    o128, o128b = ones128
    o1, o1b = ones1
    m_sb, m_b = P.sb("ln_m", [1, 1024], F32)
    r_sb, r_b = P.sb("ln_r", [1, 1024], F32)
    for tb in range(2):
        sl = slice(tb * TB, (tb + 1) * TB)
        ps_s, ps_sb_ = psf[0]
        ps_q, ps_qb = psf[1]
        for i in range(ND):
            xh, xb = xt[i]
            sh, sb_ = scr[i % 2]
            P.emit("vector", lambda e, o=sh[:, :TB], a=xh[:, sl]: e.tensor_mul(o, a, a),
                   reads=[xb], writes=[sb_])
            P.mm(ps_s[0:1, :], ps_sb_, [(o128[:, :], xh[:, sl])], [o128b, xb], accum=(i > 0))
            P.mm(ps_q[0:1, :], ps_qb, [(o128[:, :], sh[:, :TB])], [o128b, sb_], accum=(i > 0))
        P.emit("vector", lambda e, o=m_sb[0:1, sl], i=ps_s[0:1, :]: e.tensor_scalar_mul(o, i, 1.0 / D),
               reads=[ps_sb_], writes=[m_b])
        sh, sb_ = scr[0]
        P.emit("vector", lambda e, o=sh[0:1, 0:TB], i=m_sb[0:1, sl]: e.tensor_mul(o, i, i),
               reads=[m_b], writes=[sb_])
        P.emit("vector", lambda e, o=r_sb[0:1, sl], i=ps_q[0:1, :], s=sh[0:1, 0:TB]:
               e.scalar_tensor_tensor(o, i, 1.0 / D, s, ALU.mult, ALU.subtract),
               reads=[ps_qb, sb_], writes=[r_b])
        P.emit("vector", lambda e, o=r_sb[0:1, sl], i=r_sb[0:1, sl]: e.tensor_scalar_add(o, i, eps),
               reads=[r_b], writes=[r_b])
        P.emit("scalar", lambda e, o=r_sb[0:1, sl], i=r_sb[0:1, sl]: e.activation(o, i, AF.Ln),
               reads=[r_b], writes=[r_b])
        P.emit("scalar", lambda e, o=r_sb[0:1, sl], i=r_sb[0:1, sl]: e.activation(o, i, AF.Exp, scale=-0.5),
               reads=[r_b], writes=[r_b])
        ps_m, ps_mb = psf[2]
        ps_r, ps_rb = psf[3]
        P.mm(ps_m[:, :], ps_mb, [(o1[:, :], m_sb[0:1, sl])], [o1b, m_b])
        P.mm(ps_r[:, :], ps_rb, [(o1[:, :], r_sb[0:1, sl])], [o1b, r_b])
        for i in range(ND):
            xh, xb = xt[i]
            yh, yb = yt[i]
            sh, sb_ = scr[i % 2]
            P.emit("vector", lambda e, o=sh[:, :TB], a=xh[:, sl], b=ps_m[:, :]: e.tensor_sub(o, a, b),
                   reads=[xb, ps_mb], writes=[sb_])
            if post_cols is not None:
                ch, cb, cols = post_cols
                P.emit("vector", lambda e, o=yh[:, sl], a=sh[:, :TB], s=ch[:, cols[i]:cols[i] + 1], b=ps_r[:, :]:
                       e.scalar_tensor_tensor(o, a, s, b, ALU.mult, ALU.mult),
                       reads=[sb_, cb, ps_rb], writes=[yb])
            else:
                P.emit("vector", lambda e, o=yh[:, sl], a=sh[:, :TB], b=ps_r[:, :]: e.tensor_mul(o, a, b),
                       reads=[sb_, ps_rb], writes=[yb])
    return (m_sb, m_b, r_sb, r_b)


def emit_ln_streaming(P, x_ap, x_dbuf, yt, psf, scr, xs, ones128, ones1, post_cols=None, eps=1e-5):
    o128, o128b = ones128
    o1, o1b = ones1
    xr = x_ap.rearrange("(i p) t -> i p t", p=128)
    m_sb, m_b = P.sb("lns_m", [1, 1024], F32)
    r_sb, r_b = P.sb("lns_r", [1, 1024], F32)
    for i in range(ND):
        xh, xb = xs[i % 2]
        P.dma(xh[:, :], xr[i], xb, src_buf=x_dbuf)
        for tb in range(2):
            sl = slice(tb * TB, (tb + 1) * TB)
            sh, sb_ = scr[(2 * i + tb) % 2]
            P.emit("vector", lambda e, o=sh[:, :TB], a=xh[:, sl]: e.tensor_mul(o, a, a),
                   reads=[xb], writes=[sb_])
            ps_s, ps_sb_ = psf[tb]
            ps_q, ps_qb = psf[2 + tb]
            P.mm(ps_s[0:1, :], ps_sb_, [(o128[:, :], xh[:, sl])], [o128b, xb], accum=(i > 0))
            P.mm(ps_q[0:1, :], ps_qb, [(o128[:, :], sh[:, :TB])], [o128b, sb_], accum=(i > 0))
    for tb in range(2):
        sl = slice(tb * TB, (tb + 1) * TB)
        ps_s, ps_sb_ = psf[tb]
        ps_q, ps_qb = psf[2 + tb]
        P.emit("vector", lambda e, o=m_sb[0:1, sl], i=ps_s[0:1, :]: e.tensor_scalar_mul(o, i, 1.0 / D),
               reads=[ps_sb_], writes=[m_b])
        sh, sb_ = scr[0]
        P.emit("vector", lambda e, o=sh[0:1, 0:TB], i=m_sb[0:1, sl]: e.tensor_mul(o, i, i),
               reads=[m_b], writes=[sb_])
        P.emit("vector", lambda e, o=r_sb[0:1, sl], i=ps_q[0:1, :], s=sh[0:1, 0:TB]:
               e.scalar_tensor_tensor(o, i, 1.0 / D, s, ALU.mult, ALU.subtract),
               reads=[ps_qb, sb_], writes=[r_b])
        P.emit("vector", lambda e, o=r_sb[0:1, sl], i=r_sb[0:1, sl]: e.tensor_scalar_add(o, i, eps),
               reads=[r_b], writes=[r_b])
        P.emit("scalar", lambda e, o=r_sb[0:1, sl], i=r_sb[0:1, sl]: e.activation(o, i, AF.Ln),
               reads=[r_b], writes=[r_b])
        P.emit("scalar", lambda e, o=r_sb[0:1, sl], i=r_sb[0:1, sl]: e.activation(o, i, AF.Exp, scale=-0.5),
               reads=[r_b], writes=[r_b])
    for tb in range(2):
        sl = slice(tb * TB, (tb + 1) * TB)
        ps_m, ps_mb = psf[0 + tb]
        ps_r, ps_rb = psf[2 + tb]
        P.mm(ps_m[:, :], ps_mb, [(o1[:, :], m_sb[0:1, sl])], [o1b, m_b])
        P.mm(ps_r[:, :], ps_rb, [(o1[:, :], r_sb[0:1, sl])], [o1b, r_b])
    for i in range(ND):
        xh, xb = xs[i % 2]
        P.dma(xh[:, :], xr[i], xb, src_buf=x_dbuf)
        for tb in range(2):
            sl = slice(tb * TB, (tb + 1) * TB)
            ps_m, ps_mb = psf[0 + tb]
            ps_r, ps_rb = psf[2 + tb]
            yh, yb = yt[i]
            sh, sb_ = scr[(2 * i + tb) % 2]
            P.emit("vector", lambda e, o=sh[:, :TB], a=xh[:, sl], b=ps_m[:, :]: e.tensor_sub(o, a, b),
                   reads=[xb, ps_mb], writes=[sb_])
            if post_cols is not None:
                ch, cb, cols = post_cols
                P.emit("vector", lambda e, o=yh[:, sl], a=sh[:, :TB], s=ch[:, cols[i]:cols[i] + 1], b=ps_r[:, :]:
                       e.scalar_tensor_tensor(o, a, s, b, ALU.mult, ALU.mult),
                       reads=[sb_, cb, ps_rb], writes=[yb])
            else:
                P.emit("vector", lambda e, o=yh[:, sl], a=sh[:, :TB], b=ps_r[:, :]: e.tensor_mul(o, a, b),
                       reads=[sb_, ps_rb], writes=[yb])


def _alloc_common(P):
    psf = [P.ps(f"psf{k}", [128, 512], F32) for k in range(6)]
    psb = [P.ps(f"psb{k}", [128, 1024], BF16) for k in range(2)]
    scr = [P.sb(f"scr{k}", [128, 512], F32) for k in range(2)]
    return psf, psb, scr


def emit_prep(P, wdp_t, h64_t, wdf_t):
    wdp, wdp_b = wdp_t
    h64, h64_b = h64_t
    wdf, wdf_b = wdf_t
    wdfr = wdf.rearrange("(a g j) d -> g a j d", a=64, g=64, j=2)
    h_sb, h_b = P.sb("h64sb", [128, 128], BF16)
    P.dma(h_sb[:, :], h64[:, :], h_b, src_buf=h64_b, cast=True)
    grp = [P.sb(f"grp{k}", [128, D], BF16) for k in range(2)]
    osb = [P.sb(f"osb{k}", [128, D], BF16) for k in range(2)]
    ps = [P.ps(f"pp{k}", [128, 512], F32) for k in range(2)]
    sc = 1.0 / np.sqrt(INTER)
    for g in range(64):
        gh, gb = grp[g % 2]
        oh, ob = osb[g % 2]
        P.dma(gh[:, :], wdp[g], gb, src_buf=wdp_b, cast=True)
        for j in range(4):
            sl = slice(j * 512, (j + 1) * 512)
            ph, pb = ps[j % 2]
            P.mm(ph[:, :], pb, [(h_sb[:, :], gh[:, sl])], [h_b, gb])
            P.emit("scalar", lambda e, o=oh[:, sl], i=ph[:, :]: e.mul(o, i, sc),
                   reads=[pb], writes=[ob])
        P.dma(wdfr[g], oh[:, :], wdf_b, src_buf=ob)


def emit_ffn0(P, x_t, xo_t, W):
    x_ap, x_b = x_t
    xo_ap, xo_b = xo_t
    wupT, wupT_b = W["wupT"]
    wgT, wgT_b = W["wgT"]
    wdf, wdf_b = W["wdf"]
    su_ug, su_ug_b = W["su_ug"]
    sw_up, sw_up_b = W["sw_up"]
    ws_g, ws_g_b = W["ws_g"]
    ws_d, ws_d_b = W["ws_d"]
    h128, h128_b = W["h128"]
    ones1, ones1_b = W["ones1"]
    ones128, ones128_b = W["ones128"]

    psf, psb, scr = _alloc_common(P)
    h_sb, h_b = P.sb("h128sb", [128, 128], BF16)
    P.dma(h_sb[:, :], h128[:, :], h_b, src_buf=h128_b, cast=True)
    o1_sb, o1_b = P.sb("o1", [1, 128], F32)
    P.dma(o1_sb[:, :], ones1[:, :], o1_b, src_buf=ones1_b)
    o128_sb, o128_b = P.sb("o128", [128, 1], F32)
    P.dma(o128_sb[:, :], ones128[:, :], o128_b, src_buf=ones128_b)
    suh, sub = P.sb("suh", [128, 16], F32)
    P.dma(suh[:, :], su_ug[:, :], sub, src_buf=su_ug_b)
    swh, swb = P.sb("swh", [128, 64], F32)
    P.dma(swh[:, :], sw_up[:, :], swb, src_buf=sw_up_b)
    wgh, wgb = P.sb("wsgh", [128, 64], F32)
    P.dma(wgh[:, :], ws_g[:, :], wgb, src_buf=ws_g_b)
    wdh, wdb = P.sb("wsdh", [128, 16], F32)
    P.dma(wdh[:, :], ws_d[:, :], wdb, src_buf=ws_d_b)

    xs = [P.sb(f"xs{k}", [128, TOK], F32) for k in range(2)]
    sA = [P.sb(f"sA{i}", [128, TOK], BF16) for i in range(ND)]
    sB = [P.sb(f"sB{i}", [128, TOK], BF16) for i in range(ND)]
    h2 = [P.sb(f"h2_{i}", [128, TOK], BF16) for i in range(32)]
    wu_s = [P.sb(f"wu{k}", [128, D], BF16) for k in range(3)]
    wg_s = [P.sb(f"wg{k}", [128, D], BF16) for k in range(3)]
    wd_s = [P.sb(f"wd{k}", [128, 4096], BF16) for k in range(2)]
    ug_scr = [P.sb(f"ug{k}", [128, TB], F32) for k in range(2)]
    ub = [P.sb(f"ub{k}", [128, TB], BF16) for k in range(2)]
    xn = [P.sb(f"xn{k}", [128, TOK], F32) for k in range(2)]

    emit_ln_streaming(P, x_ap, x_b, sA, psf, scr, xs, (o128_sb, o128_b), (o1_sb, o1_b),
                      post_cols=(suh, sub, list(range(16))))
    cur, nxt = sA, sB
    for st in range(4):
        h = 1 << st
        for a in range(ND):
            bpart = a ^ h
            ah, ab = cur[a]
            bh, bb = cur[bpart]
            oh, ob = nxt[a]
            op = ALU.add if (a & h) == 0 else ALU.subtract
            if op == ALU.add:
                P.emit("vector", lambda e, o=oh[:, :], p=ah[:, :], q=bh[:, :]: e.tensor_add(o, p, q),
                       reads=[ab, bb], writes=[ob])
            else:
                P.emit("vector", lambda e, o=oh[:, :], p=bh[:, :], q=ah[:, :]: e.tensor_sub(o, p, q),
                       reads=[ab, bb], writes=[ob])
        cur, nxt = nxt, cur
    for i in range(ND):
        ah, ab = cur[i]
        th, tb_ = nxt[i]
        for tb in range(2):
            sl = slice(tb * TB, (tb + 1) * TB)
            ph, pb = psf[4]
            P.mm(ph[:, :], pb, [(h_sb[:, :], ah[:, sl])], [h_b, ab])
            P.emit("scalar", lambda e, o=th[:, sl], i=ph[:, :]: e.copy(o, i),
                   reads=[pb], writes=[tb_])
    t_tiles = nxt
    wupr = wupT.rearrange("(i p) (a c) -> a p i c", p=128, c=128)
    wgr = wgT.rearrange("(i p) (a c) -> a p i c", p=128, c=128)
    wdfr = wdf.rearrange("(a p) (d c) -> d p a c", p=128, c=128)
    xr = x_ap.rearrange("(i p) t -> i p t", p=128)
    xor_ = xo_ap.rearrange("(i p) t -> i p t", p=128)
    for half in range(2):
        for a2 in range(32):
            a = half * 32 + a2
            wuh, wub = wu_s[a2 % 3]
            wgh2, wgb2 = wg_s[a2 % 3]
            P.dma(wuh[:, :], wupr[a], wub, src_buf=wupT_b, cast=True)
            P.dma(wgh2[:, :], wgr[a], wgb2, src_buf=wgT_b, cast=True)
            hh, hb = h2[a2]
            for tb in range(2):
                sl = slice(tb * TB, (tb + 1) * TB)
                pu, pub = psf[tb]
                pg, pgb = psf[2 + tb]
                pairs_u = [(wuh[:, i * 128:(i + 1) * 128], t_tiles[i][0][:, sl]) for i in range(ND)]
                pairs_g = [(wgh2[:, i * 128:(i + 1) * 128], t_tiles[i][0][:, sl]) for i in range(ND)]
                rd = [wub] + [t_tiles[i][1] for i in range(ND)]
                rdg = [wgb2] + [t_tiles[i][1] for i in range(ND)]
                P.mm(pu[:, :], pub, pairs_u, rd)
                P.mm(pg[:, :], pgb, pairs_g, rdg)
                gh, gb = ug_scr[tb]
                P.emit("scalar", lambda e, o=gh[:, :], i=pg[:, :], s=wgh[:, a:a + 1]:
                       e.activation(o, i, AF.Silu, scale=s),
                       reads=[pgb, wgb], writes=[gb])
                uh, ubf = ub[tb]
                P.emit("vector", lambda e, o=uh[:, :], i=gh[:, :], s=swh[:, a:a + 1], b=pu[:, :]:
                       e.scalar_tensor_tensor(o, i, s, b, ALU.mult, ALU.mult),
                       reads=[gb, swb, pub], writes=[ubf])
                ph2, pb2 = psf[4 + tb]
                P.mm(ph2[:, :], pb2, [(h_sb[:, :], uh[:, :])], [h_b, ubf])
                P.emit("scalar", lambda e, o=hh[:, sl], i=ph2[:, :]: e.copy(o, i),
                       reads=[pb2], writes=[hb])
        for d in range(ND):
            wdh2, wdb2 = wd_s[d % 2]
            P.dma(wdh2[:, :].rearrange("p (a c) -> p a c", c=128), wdfr[d][:, half * 32:(half + 1) * 32, :],
                  wdb2, src_buf=wdf_b)
            xh, xb = xs[d % 2]
            if half == 0:
                P.dma(xh[:, :], xr[d], xb, src_buf=x_b)
            else:
                P.dma(xh[:, :], xor_[d], xb, src_buf=xo_b)
            xnh, xnb = xn[d % 2]
            for tb in range(2):
                sl = slice(tb * TB, (tb + 1) * TB)
                pd, pdb = psf[2 + tb]
                pairs = [(wdh2[:, a2 * 128:(a2 + 1) * 128], h2[a2][0][:, sl]) for a2 in range(32)]
                rd = [wdb2] + [h2[a2][1] for a2 in range(32)]
                P.mm(pd[:, :], pdb, pairs, rd)
                P.emit("vector", lambda e, o=xnh[:, sl], i=pd[:, :], s=wdh[:, d:d + 1], b=xh[:, sl]:
                       e.scalar_tensor_tensor(o, i, s, b, ALU.mult, ALU.add),
                       reads=[pdb, wdb, xb], writes=[xnb])
            P.dma(xor_[d], xnh[:, :], xo_b, src_buf=xnb)


def emit_attn0(P, x_t, xo_t, W):
    x_ap, x_b = x_t
    xo_ap, xo_b = xo_t
    bT, bT_b = W["bT"]
    cT, cT_b = W["cT"]
    dvec, dvec_b = W["dvec"]
    ones1, ones1_b = W["ones1"]
    ones128, ones128_b = W["ones128"]

    psf, psb, scr = _alloc_common(P)
    o1_sb, o1_b = P.sb("o1", [1, 128], F32)
    P.dma(o1_sb[:, :], ones1[:, :], o1_b, src_buf=ones1_b)
    o128_sb, o128_b = P.sb("o128", [128, 1], F32)
    P.dma(o128_sb[:, :], ones128[:, :], o128_b, src_buf=ones128_b)
    dh, db = P.sb("dh", [128, 16], F32)
    P.dma(dh[:, :], dvec[:, :], db, src_buf=dvec_b)

    xt = [P.sb(f"x{i}", [128, TOK], F32) for i in range(ND)]
    yt = [P.sb(f"y{i}", [128, TOK], BF16) for i in range(ND)]
    st = [P.sb(f"st{i}", [128, TOK], BF16) for i in range(8)]
    wb_s = [P.sb(f"wb{k}", [128, D], BF16) for k in range(2)]
    wc_s = [P.sb(f"wc{k}", [128, 1024], BF16) for k in range(2)]
    xn = [P.sb(f"xn{k}", [128, TOK], F32) for k in range(2)]

    xr = x_ap.rearrange("(i p) t -> i p t", p=128)
    xor_ = xo_ap.rearrange("(i p) t -> i p t", p=128)
    for i in range(ND):
        P.dma(xt[i][0][:, :], xr[i], xt[i][1], src_buf=x_b)
    emit_ln_resident(P, xt, yt, psf, scr, (o128_sb, o128_b), (o1_sb, o1_b))
    bTr = bT.rearrange("(i p) (n c) -> n p i c", p=128, c=128)
    for n in range(8):
        wh, wb = wb_s[n % 2]
        P.dma(wh[:, :], bTr[n], wb, src_buf=bT_b, cast=True)
        for tb in range(2):
            sl = slice(tb * TB, (tb + 1) * TB)
            ph, pb = psf[tb]
            pairs = [(wh[:, i * 128:(i + 1) * 128], yt[i][0][:, sl]) for i in range(ND)]
            P.mm(ph[:, :], pb, pairs, [wb] + [yt[i][1] for i in range(ND)])
            P.emit("scalar", lambda e, o=st[n][0][:, sl], i=ph[:, :]: e.activation(o, i, AF.Tanh),
                   reads=[pb], writes=[st[n][1]])
    cTr = cT.rearrange("(n p) (d c) -> d p n c", p=128, c=128)
    for d in range(ND):
        wh, wb = wc_s[d % 2]
        P.dma(wh[:, :], cTr[d], wb, src_buf=cT_b, cast=True)
        xnh, xnb = xn[d % 2]
        for tb in range(2):
            sl = slice(tb * TB, (tb + 1) * TB)
            ph, pb = psf[2 + tb]
            pairs = [(wh[:, n * 128:(n + 1) * 128], st[n][0][:, sl]) for n in range(8)]
            P.mm(ph[:, :], pb, pairs, [wb] + [st[n][1] for n in range(8)])
            sh, sb_ = scr[tb]
            P.emit("vector", lambda e, o=sh[:, :], i=yt[d][0][:, sl], s=dh[:, d:d + 1], b=ph[:, :]:
                   e.scalar_tensor_tensor(o, i, s, b, ALU.mult, ALU.add),
                   reads=[yt[d][1], db, pb], writes=[sb_])
            P.emit("vector", lambda e, o=xnh[:, sl], a=xt[d][0][:, sl], b=sh[:, :]: e.tensor_add(o, a, b),
                   reads=[xt[d][1], sb_], writes=[xnb])
        P.dma(xor_[d], xnh[:, :], xo_b, src_buf=xnb)


def emit_moe(P, x_t, xo_t, W, final=False):
    x_ap, x_b = x_t
    xo_ap, xo_b = xo_t
    wr, wr_b = W["wr"]
    br, br_b = W["br"]
    wds, wds_b = W["wds"]
    bds, bds_b = W["bds"]
    wus, wus_b = W["wus"]
    bu, bu_b = W["bu"]
    e16, e16_b = W["e16"]
    iden, iden_b = W["iden"]
    ones1, ones1_b = W["ones1"]
    ones128, ones128_b = W["ones128"]

    psf, psb, scr = _alloc_common(P)
    o1_sb, o1_b = P.sb("o1", [1, 128], F32)
    P.dma(o1_sb[:, :], ones1[:, :], o1_b, src_buf=ones1_b)
    o128_sb, o128_b = P.sb("o128", [128, 1], F32)
    P.dma(o128_sb[:, :], ones128[:, :], o128_b, src_buf=ones128_b)
    idf, idf_b = P.sb("idf", [128, 128], F32)
    P.dma(idf[:, :], iden[:, :], idf_b, src_buf=iden_b)
    idb, idb_b = P.sb("idb", [128, 128], BF16)
    P.dma(idb[:, :], iden[:, :], idb_b, src_buf=iden_b, cast=True)
    e16s, e16sb = P.sb("e16s", [8, 128], BF16)
    P.dma(e16s[:, :], e16[:, :], e16sb, src_buf=e16_b, cast=True)
    wr32, wr32b = P.sb("wr32", [128, 128], F32)
    P.dma(wr32[:, :], wr.rearrange("(i p) e -> p i e", p=128), wr32b, src_buf=wr_b)
    brs, brsb = P.sb("brs", [8, 1], F32)
    P.dma(brs[:, :], br[:, :], brsb, src_buf=br_b)
    wdss, wdssb = P.sb("wdss", [128, D], BF16)
    P.dma(wdss[:, :], wds.rearrange("(i p) c -> p i c", p=128), wdssb, src_buf=wds_b, cast=True)
    bdss, bdssb = P.sb("bdss", [128, 1], F32)
    P.dma(bdss[:, :], bds[:, :], bdssb, src_buf=bds_b)
    wuss, wussb = P.sb("wuss", [128, D], BF16)
    P.dma(wuss[:, :], wus[:, :], wussb, src_buf=wus_b, cast=True)
    bus, busb = P.sb("bus", [8, D], BF16)
    P.dma(bus[:, :], bu[:, :], busb, src_buf=bu_b, cast=True)

    xt = [P.sb(f"x{i}", [128, TOK], F32) for i in range(ND)]
    yt = [P.sb(f"y{i}", [128, TOK], BF16) for i in range(ND)]
    xr = x_ap.rearrange("(i p) t -> i p t", p=128)
    xor_ = xo_ap.rearrange("(i p) t -> i p t", p=128)
    for i in range(ND):
        P.dma(xt[i][0][:, :], xr[i], xt[i][1], src_buf=x_b)
    m_sb, m_b, r_sb, r_b = emit_ln_resident(P, xt, yt, psf, scr, (o128_sb, o128_b), (o1_sb, o1_b))

    lg, lg_b = P.sb("lg", [8, TOK], F32)
    swr, swr_b = P.sb("swr", [8, 1], F32)
    s8a, s8a_b = P.sb("s8a", [8, TB], F32)
    s8b, s8b_b = P.sb("s8b", [8, TB], F32)
    ph, pb = psf[3]
    pairs = [(wr32[:, i * 8:(i + 1) * 8], o128_sb[:, :]) for i in range(ND)]
    P.mm(ph[0:8, 0:1], pb, pairs, [wr32b, o128_b])
    P.emit("scalar", lambda e, o=swr[:, :], i=ph[0:8, 0:1]: e.copy(o, i), reads=[pb], writes=[swr_b])
    for tb in range(2):
        sl = slice(tb * TB, (tb + 1) * TB)
        ph0, pb0 = psf[0]
        pairs = [(wr32[:, i * 8:(i + 1) * 8], xt[i][0][:, sl]) for i in range(ND)]
        P.mm(ph0[0:8, :], pb0, pairs, [wr32b] + [xt[i][1] for i in range(ND)])
        ph1, pb1 = psf[1]
        P.mm(ph1[0:8, :], pb1, [(o1_sb[0:1, 0:8], m_sb[0:1, sl])], [o1_b, m_b])
        ph2, pb2 = psf[2]
        P.mm(ph2[0:8, :], pb2, [(o1_sb[0:1, 0:8], r_sb[0:1, sl])], [o1_b, r_b])
        P.emit("vector", lambda e, o=s8a[:, :], i=ph1[0:8, :], sc=swr[:, :]: e.tensor_scalar_mul(o, i, sc),
               reads=[pb1, swr_b], writes=[s8a_b])
        P.emit("vector", lambda e, o=s8b[:, :], a=ph0[0:8, :], b=s8a[:, :]: e.tensor_sub(o, a, b),
               reads=[pb0, s8a_b], writes=[s8b_b])
        P.emit("vector", lambda e, o=lg[:, sl], a=s8b[:, :], b=ph2[0:8, :]: e.tensor_mul(o, a, b),
               reads=[s8b_b, pb2], writes=[lg_b])
        P.emit("vector", lambda e, o=lg[:, sl], i=lg[:, sl], s=brs[:, :]: e.tensor_scalar_add(o, i, s),
               reads=[lg_b, brsb], writes=[lg_b])
    lt, lt_b = P.sb("lt", [128, 8], F32)
    m1, m1_b = P.sb("m1", [128, 1], F32)
    m2, m2_b = P.sb("m2", [128, 1], F32)
    msk, msk_b = P.sb("msk", [128, 8], F32)
    gts, gts_b = P.sb("gts", [128, 8], F32)
    w8, w8_b = P.sb("w8", [128, 8], BF16)
    wT, wT_b = P.sb("wT", [8, TOK], BF16)
    for blk in range(8):
        bs = slice(blk * 128, (blk + 1) * 128)
        ph, pb = psf[1]
        P.tr(ph[0:128, 0:8], pb, lg[0:8, bs], lg_b, idf[0:8, 0:8], idf_b)
        P.emit("scalar", lambda e, o=lt[:, :], i=ph[0:128, 0:8]: e.copy(o, i), reads=[pb], writes=[lt_b])
        P.emit("vector", lambda e, o=m1[:, :], i=lt[:, :]: e.reduce_max(o, i, AX.X), reads=[lt_b], writes=[m1_b])
        P.emit("vector", lambda e, o=msk[:, :], i=lt[:, :], s=m1[:, :]:
               e.tensor_scalar(o, i, s, None, ALU.is_ge), reads=[lt_b, m1_b], writes=[msk_b])
        P.emit("vector", lambda e, o=msk[:, :], i=msk[:, :], b=lt[:, :]:
               e.scalar_tensor_tensor(o, i, -1e9, b, ALU.mult, ALU.add), reads=[msk_b, lt_b], writes=[msk_b])
        P.emit("vector", lambda e, o=m2[:, :], i=msk[:, :]: e.reduce_max(o, i, AX.X), reads=[msk_b], writes=[m2_b])
        P.emit("vector", lambda e, o=msk[:, :], i=lt[:, :], s=m2[:, :]:
               e.tensor_scalar(o, i, s, None, ALU.is_ge), reads=[lt_b, m2_b], writes=[msk_b])
        P.emit("vector", lambda e, o=m1[:, :], i=m1[:, :]: e.tensor_scalar_mul(o, i, -1.0), reads=[m1_b], writes=[m1_b])
        P.emit("scalar", lambda e, o=gts[:, :], i=lt[:, :], b=m1[:, :]: e.activation(o, i, AF.Exp, bias=b),
               reads=[lt_b, m1_b], writes=[gts_b])
        P.emit("vector", lambda e, o=gts[:, :], a=gts[:, :], b=msk[:, :]: e.tensor_mul(o, a, b),
               reads=[gts_b, msk_b], writes=[gts_b])
        P.emit("vector", lambda e, o=m2[:, :], i=gts[:, :]: e.reduce_sum(o, i, AX.X), reads=[gts_b], writes=[m2_b])
        P.emit("vector", lambda e, o=m2[:, :], i=m2[:, :]: e.reciprocal(o, i), reads=[m2_b], writes=[m2_b])
        P.emit("vector", lambda e, o=w8[:, :], i=gts[:, :], s=m2[:, :]: e.tensor_scalar_mul(o, i, s),
               reads=[gts_b, m2_b], writes=[w8_b])
        ph2, pb2 = psb[0]
        P.tr(ph2[0:8, 0:128], pb2, w8[:, :], w8_b, idb[:, :], idb_b)
        P.emit("scalar", lambda e, o=wT[:, bs], i=ph2[0:8, 0:128]: e.copy(o, i), reads=[pb2], writes=[wT_b])
    wS, wS_b = P.sb("wS", [128, TOK], BF16)
    hS, hS_b = P.sb("hS", [128, TOK], BF16)
    for tb in range(2):
        sl = slice(tb * TB, (tb + 1) * TB)
        ph, pb = psf[2]
        P.mm(ph[:, :], pb, [(e16s[:, :], wT[:, sl])], [e16sb, wT_b])
        P.emit("scalar", lambda e, o=wS[:, sl], i=ph[:, :]: e.copy(o, i), reads=[pb], writes=[wS_b])
        ph2, pb2 = psf[3]
        pairs = [(wdss[:, i * 128:(i + 1) * 128], yt[i][0][:, sl]) for i in range(ND)]
        P.mm(ph2[:, :], pb2, pairs, [wdssb] + [yt[i][1] for i in range(ND)])
        P.emit("scalar", lambda e, o=hS[:, sl], i=ph2[:, :], b=bdss[:, :]: e.activation(o, i, AF.Relu, bias=b),
               reads=[pb2, bdssb], writes=[hS_b])
        P.emit("vector", lambda e, o=hS[:, sl], a=hS[:, sl], b=wS[:, sl]: e.tensor_mul(o, a, b),
               reads=[hS_b, wS_b], writes=[hS_b])
    xn = [P.sb(f"xn{k}", [128, TOK], F32) for k in range(2)]
    for d in range(ND):
        xnh, xnb = xn[d % 2]
        for tb in range(2):
            sl = slice(tb * TB, (tb + 1) * TB)
            ph, pb = psf[4 + tb]
            pairs = [(wuss[:, d * 128:(d + 1) * 128], hS[:, sl]),
                     (bus[0:8, d * 128:(d + 1) * 128], wT[0:8, sl])]
            P.mm(ph[:, :], pb, pairs, [wussb, hS_b, busb, wT_b])
            P.emit("vector", lambda e, o=xnh[:, sl], i=ph[:, :], b=xt[d][0][:, sl]:
                   e.scalar_tensor_tensor(o, i, 0.5, b, ALU.mult, ALU.add),
                   reads=[pb, xt[d][1]], writes=[xnb])
        P.dma(xor_[d], xnh[:, :], xo_b, src_buf=xnb, out=final)


def emit_gdnz(P, x_t, W, xnin_t, xnf_t, zs_t):
    x_ap, x_b = x_t
    wz, wz_b = W["wz"]
    ones1, ones1_b = W["ones1"]
    ones128, ones128_b = W["ones128"]
    xnin_ap, xnin_b = xnin_t
    xnf_ap, xnf_b = xnf_t
    zs, zs_b = zs_t

    psf, psb, scr = _alloc_common(P)
    o1_sb, o1_b = P.sb("o1", [1, 128], F32)
    P.dma(o1_sb[:, :], ones1[:, :], o1_b, src_buf=ones1_b)
    o128_sb, o128_b = P.sb("o128", [128, 1], F32)
    P.dma(o128_sb[:, :], ones128[:, :], o128_b, src_buf=ones128_b)

    xt = [P.sb(f"x{i}", [128, TOK], F32) for i in range(ND)]
    yt = [P.sb(f"y{i}", [128, TOK], BF16) for i in range(ND)]
    xr = x_ap.rearrange("(i p) t -> i p t", p=128)
    for i in range(ND):
        P.dma(xt[i][0][:, :], xr[i], xt[i][1], src_buf=x_b)
    emit_ln_resident(P, xt, yt, psf, scr, (o128_sb, o128_b), (o1_sb, o1_b))

    xnr = xnin_ap.rearrange("(i p) t -> i p t", p=128)
    for i in range(ND):
        P.dma(xnr[i], yt[i][0][:, :], xnin_b, src_buf=yt[i][1])
    P.cc("AllGather", ALU.bypass, xnin_ap, xnin_b, xnf_ap, xnf_b, "cc0")

    wq_s = [P.sb(f"wq{k}", [128, D], BF16) for k in range(2)]
    mx = [P.sb(f"mx{k}", [128, TOK], BF16) for k in range(2)]
    wzr = wz.rearrange("(i p) (o c) -> o p i c", p=128, c=128)
    for o in range(32):
        wh, wb = wq_s[o % 2]
        P.dma(wh[:, :], wzr[o], wb, src_buf=wz_b, cast=True)
        mh, mb = mx[o % 2]
        for tb in range(2):
            sl = slice(tb * TB, (tb + 1) * TB)
            ph, pb = psf[(o % 2) * 2 + tb]
            pairs = [(wh[:, i * 128:(i + 1) * 128], yt[i][0][:, sl]) for i in range(ND)]
            P.mm(ph[:, :], pb, pairs, [wb] + [yt[i][1] for i in range(ND)])
            P.emit("scalar", lambda e, o_=mh[:, sl], i=ph[:, :]: e.activation(o_, i, AF.Silu),
                   reads=[pb], writes=[mb])
        P.dma(zs[o], mh[:, :], zs_b, src_buf=mb)


def emit_gdnpc(P, xnf_t, W, qin_t, kin_t, vin_t, beta_t, g_t):
    S = 2048
    xnf, xnf_b = xnf_t
    wqkvc, wqkvc_b = W["wqkvc"]
    wbac, wbac_b = W["wbac"]
    alogc, alogc_b = W["alogc"]
    dtbc, dtbc_b = W["dtbc"]
    qin, qin_b = qin_t
    kin, kin_b = kin_t
    vin, vin_b = vin_t
    beta_o, beta_ob = beta_t
    g_o, g_ob = g_t

    psf, psb, scr = _alloc_common(P)
    xnr = xnf.rearrange("(h i p) t -> h i p t", p=128, i=ND)
    xh = [[P.sb(f"xn{h}_{i}", [128, TOK], BF16) for i in range(ND)] for h in range(2)]
    for h in range(2):
        for i in range(ND):
            P.dma(xh[h][i][0][:, :], xnr[h][i], xh[h][i][1], src_buf=xnf_b)

    wq_s = [P.sb(f"wq{k}", [128, D], BF16) for k in range(2)]
    mx = [P.sb(f"mx{k}", [128, S], BF16) for k in range(2)]
    wqr = wqkvc.rearrange("(i p) (o c) -> o p i c", p=128, c=128)
    for o in range(32):
        wh, wb = wq_s[o % 2]
        P.dma(wh[:, :], wqr[o], wb, src_buf=wqkvc_b, cast=True)
        mh, mb = mx[o % 2]
        for h in range(2):
            for tb in range(2):
                sl = slice(tb * TB, (tb + 1) * TB)
                ph, pb = psf[2 * h + tb]
                pairs = [(wh[:, i * 128:(i + 1) * 128], xh[h][i][0][:, sl]) for i in range(ND)]
                P.mm(ph[:, :], pb, pairs, [wb] + [xh[h][i][1] for i in range(ND)])
                P.emit("scalar",
                       lambda e, o_=mh[:, h * TOK + tb * TB:h * TOK + (tb + 1) * TB], i=ph[:, :]: e.copy(o_, i),
                       reads=[pb], writes=[mb])
        if o < 8:
            dst, dstb = qin[o], qin_b
        elif o < 16:
            dst, dstb = kin[o - 8], kin_b
        else:
            dst, dstb = vin[o - 16], vin_b
        P.dma(dst, mh[:, :], dstb, src_buf=mb)
    wbas, wbasb = P.sb("wbas", [128, 512], BF16)
    P.dma(wbas[:, :], wbac.rearrange("(i p) e -> p i e", p=128), wbasb, src_buf=wbac_b, cast=True)
    blog, blog_b = P.sb("blog", [16, S], F32)
    alg, alg_b = P.sb("alg", [16, S], F32)
    for h in range(2):
        for tb in range(2):
            sl = slice(tb * TB, (tb + 1) * TB)
            cs = slice(h * TOK + tb * TB, h * TOK + (tb + 1) * TB)
            ph, pb = psf[4]
            pairs = [(wbas[:, i * 32:i * 32 + 16], xh[h][i][0][:, sl]) for i in range(ND)]
            P.mm(ph[0:16, :], pb, pairs, [wbasb] + [xh[h][i][1] for i in range(ND)])
            P.emit("scalar", lambda e, o=blog[:, cs], i=ph[0:16, :]: e.copy(o, i),
                   reads=[pb], writes=[blog_b])
            ph2, pb2 = psf[5]
            pairs = [(wbas[:, i * 32 + 16:i * 32 + 32], xh[h][i][0][:, sl]) for i in range(ND)]
            P.mm(ph2[0:16, :], pb2, pairs, [wbasb] + [xh[h][i][1] for i in range(ND)])
            P.emit("scalar", lambda e, o=alg[:, cs], i=ph2[0:16, :]: e.copy(o, i),
                   reads=[pb2], writes=[alg_b])
    bt, bt_b = P.sb("bt", [16, S], F32)
    P.emit("scalar", lambda e, o=bt[:, :], i=blog[:, :]: e.activation(o, i, AF.Sigmoid),
           reads=[blog_b], writes=[bt_b])
    P.dma(beta_o[:, :], bt[:, :], beta_ob, src_buf=bt_b)
    ad, ad_b = P.sb("ad", [16, S], F32)
    dtbs, dtbsb = P.sb("dtbs", [16, 1], F32)
    P.dma(dtbs[:, :], dtbc[:, :], dtbsb, src_buf=dtbc_b)
    P.emit("vector", lambda e, o=ad[:, :], i=alg[:, :], s=dtbs[:, :]: e.tensor_scalar_add(o, i, s),
           reads=[alg_b, dtbsb], writes=[ad_b])
    t1, t1_b = P.sb("spt1", [16, S], F32)
    t2, t2_b = P.sb("spt2", [16, S], F32)
    P.emit("scalar", lambda e, o=t1[:, :], i=ad[:, :]: e.activation(o, i, AF.Abs),
           reads=[ad_b], writes=[t1_b])
    P.emit("scalar", lambda e, o=t1[:, :], i=t1[:, :]: e.activation(o, i, AF.Exp, scale=-1.0),
           reads=[t1_b], writes=[t1_b])
    P.emit("vector", lambda e, o=t1[:, :], i=t1[:, :]: e.tensor_scalar_add(o, i, 1.0),
           reads=[t1_b], writes=[t1_b])
    P.emit("scalar", lambda e, o=t1[:, :], i=t1[:, :]: e.activation(o, i, AF.Ln),
           reads=[t1_b], writes=[t1_b])
    P.emit("vector", lambda e, o=t2[:, :], i=ad[:, :]: e.tensor_scalar_max(o, i, 0.0),
           reads=[ad_b], writes=[t2_b])
    P.emit("vector", lambda e, o=t2[:, :], a=t2[:, :], b=t1[:, :]: e.tensor_add(o, a, b),
           reads=[t2_b, t1_b], writes=[t2_b])
    nea, nea_b = P.sb("nea", [16, 1], F32)
    P.dma(nea[:, :], alogc[:, :], nea_b, src_buf=alogc_b)
    P.emit("scalar", lambda e, o=nea[:, :], i=nea[:, :]: e.activation(o, i, AF.Exp), reads=[nea_b], writes=[nea_b])
    P.emit("vector", lambda e, o=nea[:, :], i=nea[:, :]: e.tensor_scalar_mul(o, i, -1.0), reads=[nea_b], writes=[nea_b])
    P.emit("vector", lambda e, o=t2[:, :], i=t2[:, :], s=nea[:, :]: e.tensor_scalar_mul(o, i, s),
           reads=[t2_b, nea_b], writes=[t2_b])
    for h in range(2):
        P.emit("vector", lambda e, o=t1[:, h * TOK:(h + 1) * TOK], a=t2[:, h * TOK:(h + 1) * TOK],
               b=t2[:, h * TOK:(h + 1) * TOK]:
               e.tensor_tensor_scan(o, a, b, 0.0, ALU.add, ALU.bypass),
               reads=[t2_b], writes=[t1_b])
    P.dma(g_o[:, :], t1[:, :], g_ob, src_buf=t1_b)


def emit_gdnchunk(P, W, qin_t, kin_t, vin_t, beta_t, g_t, rsin_t):
    S = 2048
    NCH = S // CC
    qin, qin_b = qin_t
    kin, kin_b = kin_t
    vin, vin_b = vin_t
    beta_i, beta_ib = beta_t
    g_i, g_ib = g_t
    rsin_ap, rsin_b = rsin_t
    cwq, cwq_b = W["cwq"]
    cwk, cwk_b = W["cwk"]
    cwv, cwv_b = W["cwv"]
    monorm, monorm_b = W["monorm"]
    nw_i, nw_ib = W["nw"]
    maskw, maskw_b = W["maskw"]
    maskat, maskat_b = W["maskat"]
    iden, iden_b = W["iden"]
    ones1, ones1_b = W["ones1"]
    ones128, ones128_b = W["ones128"]

    psf = [P.ps(f"psf{k}", [128, 512], F32) for k in range(5)]
    psb = [P.ps(f"psb{k}", [128, 1024], BF16) for k in range(3)]
    mk_sb, mk_b = P.sb("mks", [128, 2], F32)
    P.dma(mk_sb[:, :], monorm[:, :], mk_b, src_buf=monorm_b)
    mscr = [P.sb("mscr0", [128, S], BF16)]
    rsv = rsin_ap.rearrange("(j r) p t -> r p j t", j=2)
    o1_sb, o1_b = P.sb("o1", [1, 128], F32)
    P.dma(o1_sb[:, :], ones1[:, :], o1_b, src_buf=ones1_b)
    o128_sb, o128_b = P.sb("o128", [128, 1], F32)
    P.dma(o128_sb[:, :], ones128[:, :], o128_b, src_buf=ones128_b)
    idf, idf_b = P.sb("idf", [128, 128], F32)
    P.dma(idf[:, :], iden[:, :], idf_b, src_buf=iden_b)
    idb, idb_b = P.sb("idb", [128, 128], BF16)
    P.dma(idb[:, :], iden[:, :], idb_b, src_buf=iden_b, cast=True)
    mw_sb, mw_b = P.sb("mw", [128, 128], F32)
    P.dma(mw_sb[:, :], maskw[:, :], mw_b, src_buf=maskw_b)
    ma_sb, ma_b = P.sb("ma", [128, 128], F32)
    P.dma(ma_sb[:, :], maskat[:, :], ma_b, src_buf=maskat_b)
    bsb, bsb_b = P.sb("bsb", [16, S], F32)
    P.dma(bsb[:, :], beta_i[:, :], bsb_b, src_buf=beta_ib)
    gsb, gsb_b = P.sb("gsb", [16, S], F32)
    P.dma(gsb[:, :], g_i[:, :], gsb_b, src_buf=g_ib)
    nws, nws_b = P.sb("nws", [1, 128], F32)
    P.dma(nws[:, :], nw_i[:, :], nws_b, src_buf=nw_ib)
    nwb, nwb_b = P.sb("nwb", [128, 128], F32)
    ph, pb = psf[0]
    P.mm(ph[:, 0:128], pb, [(o1_sb[:, :], nws[:, :])], [o1_b, nws_b])
    P.emit("scalar", lambda e, o=nwb[:, :], i=ph[:, 0:128]: e.copy(o, i), reads=[pb], writes=[nwb_b])

    cwq_s, cwq_sb = P.sb("cwqs", [128, 32], F32)
    P.dma(cwq_s[:, :].rearrange("p (i j) -> p i j", j=4), cwq.rearrange("i p j -> p i j"), cwq_sb, src_buf=cwq_b)
    cwk_s, cwk_sb = P.sb("cwks", [128, 32], F32)
    P.dma(cwk_s[:, :].rearrange("p (i j) -> p i j", j=4), cwk.rearrange("i p j -> p i j"), cwk_sb, src_buf=cwk_b)
    cwv_s, cwv_sb = P.sb("cwvs", [128, 64], F32)
    P.dma(cwv_s[:, :].rearrange("p (i j) -> p i j", j=4), cwv.rearrange("i p j -> p i j"), cwv_sb, src_buf=cwv_b)
    pad = [P.sb(f"pad{k}", [128, S + 3], BF16) for k in range(2)]
    acc = [P.sb(f"acc{k}", [128, 512], F32) for k in range(2)]
    sqs, sqs_b = P.sb("sqs", [128, 512], F32)
    rsr, rsr_b = P.sb("rsr", [1, S], F32)
    qt = [P.sb(f"qt{i}", [128, S], BF16) for i in range(8)]
    kt = [P.sb(f"kt{i}", [128, S], BF16) for i in range(8)]
    vt = [P.sb(f"vt{i}", [128, S], BF16) for i in range(16)]

    def conv_tile(dst, dst_b, src_ap, src_b, cw_ap, cw_b, idx, norm_scale):
        ph_, pb_ = pad[idx % 2]
        P.emit("vector", lambda e, o=ph_[:, 0:3]: e.memset(o, 0.0), writes=[pb_])
        P.dma(ph_[:, 3:S + 3], src_ap, pb_, src_buf=src_b)
        for t4 in range(4):
            off = t4 * 512
            sl = slice(off, off + 512)
            ah, ab = acc[t4 % 2]
            P.emit("vector", lambda e, o=ah[:, :], i=ph_[:, off:off + 512], s=cw_ap[:, 0:1]:
                   e.tensor_scalar_mul(o, i, s),
                   reads=[pb_, cw_b], writes=[ab])
            for j in range(1, 4):
                P.emit("vector", lambda e, o=ah[:, :], i=ph_[:, off + j:off + j + 512], s=cw_ap[:, j:j + 1], b=ah[:, :]:
                       e.scalar_tensor_tensor(o, i, s, b, ALU.mult, ALU.add),
                       reads=[pb_, cw_b, ab], writes=[ab])
            if norm_scale is None:
                P.emit("scalar", lambda e, o=dst[:, sl], i=ah[:, :]: e.activation(o, i, AF.Silu),
                       reads=[ab], writes=[dst_b])
                continue
            P.emit("scalar", lambda e, o=ah[:, :], i=ah[:, :]: e.activation(o, i, AF.Silu),
                   reads=[ab], writes=[ab])
            P.emit("vector", lambda e, o=sqs[:, :], a=ah[:, :]: e.tensor_mul(o, a, a), reads=[ab], writes=[sqs_b])
            ph2, pb2 = psf[t4 % 2]
            P.mm(ph2[0:1, :], pb2, [(o128_sb[:, :], sqs[:, :])], [o128_b, sqs_b])
            P.emit("vector", lambda e, o=rsr[0:1, sl], i=ph2[0:1, :]: e.tensor_scalar_add(o, i, 1e-6),
                   reads=[pb2], writes=[rsr_b])
            P.emit("scalar", lambda e, o=rsr[0:1, sl], i=rsr[0:1, sl]: e.activation(o, i, AF.Ln),
                   reads=[rsr_b], writes=[rsr_b])
            P.emit("scalar", lambda e, o=rsr[0:1, sl], i=rsr[0:1, sl]: e.activation(o, i, AF.Exp, scale=-0.5),
                   reads=[rsr_b], writes=[rsr_b])
            ph3, pb3 = psf[2 + t4 % 2]
            P.mm(ph3[:, :], pb3, [(o1_sb[:, :], rsr[0:1, sl])], [o1_b, rsr_b])
            P.emit("vector", lambda e, o=dst[:, sl], i=ah[:, :], b=ph3[:, :]:
                   e.scalar_tensor_tensor(o, i, norm_scale, b, ALU.mult, ALU.mult),
                   reads=[ab, pb3], writes=[dst_b])

    for i in range(8):
        conv_tile(qt[i][0], qt[i][1], qin[i], qin_b, cwq_s[:, i * 4:(i + 1) * 4], cwq_sb, i, float(DK ** -0.5))
    for i in range(8):
        conv_tile(kt[i][0], kt[i][1], kin[i], kin_b, cwk_s[:, i * 4:(i + 1) * 4], cwk_sb, i, 1.0)
    for i in range(16):
        conv_tile(vt[i][0], vt[i][1], vin[i], vin_b, cwv_s[:, i * 4:(i + 1) * 4], cwv_sb, i, None)

    gbt = [P.sb(f"gbt{c}", [128, 32], F32) for c in range(NCH)]
    npeb = [P.sb(f"npeb{c}", [128, 16], F32) for c in range(NCH)]
    ebt = [P.sb(f"ebt{c}", [128, 16], F32) for c in range(NCH)]
    acb = [P.sb(f"acb{c}", [128, 16], F32) for c in range(NCH)]
    rowt, rowt_b = P.sb("rowt", [1, 16], F32)
    for c in range(NCH):
        cs = slice(c * CC, (c + 1) * CC)
        gh, gb = gbt[c]
        ph1, pb1 = psf[0]
        P.tr(ph1[0:128, 0:16], pb1, gsb[0:16, cs], gsb_b, idf[0:16, 0:16], idf_b)
        P.emit("scalar", lambda e, o=gh[:, 0:16], i=ph1[0:128, 0:16]: e.copy(o, i), reads=[pb1], writes=[gb])
        ph2, pb2 = psf[1]
        P.tr(ph2[0:128, 0:16], pb2, bsb[0:16, cs], bsb_b, idf[0:16, 0:16], idf_b)
        P.emit("scalar", lambda e, o=gh[:, 16:32], i=ph2[0:128, 0:16]: e.copy(o, i), reads=[pb2], writes=[gb])
        nh, nb = npeb[c]
        if c % (NCH // 2) == 0:
            P.emit("vector", lambda e, o=nh[:, :]: e.memset(o, 0.0), writes=[nb])
        else:
            ph3, pb3 = psf[2]
            P.tr(ph3[0:1, 0:16], pb3, gsb[0:16, c * CC - 1:c * CC], gsb_b, idf[0:16, 0:16], idf_b)
            P.emit("scalar", lambda e, o=rowt[:, :], i=ph3[0:1, 0:16]: e.copy(o, i), reads=[pb3], writes=[rowt_b])
            ph4, pb4 = psf[3]
            P.mm(ph4[:, 0:16], pb4, [(o1_sb[:, :], rowt[:, :])], [o1_b, rowt_b])
            P.emit("scalar", lambda e, o=nh[:, :], i=ph4[:, 0:16]: e.mul(o, i, -1.0), reads=[pb4], writes=[nb])
        eh, eb = ebt[c]
        ph3, pb3 = psf[2]
        P.tr(ph3[0:1, 0:16], pb3, gsb[0:16, (c + 1) * CC - 1:(c + 1) * CC], gsb_b, idf[0:16, 0:16], idf_b)
        P.emit("scalar", lambda e, o=rowt[:, :], i=ph3[0:1, 0:16]: e.copy(o, i), reads=[pb3], writes=[rowt_b])
        ph4, pb4 = psf[3]
        P.mm(ph4[:, 0:16], pb4, [(o1_sb[:, :], rowt[:, :])], [o1_b, rowt_b])
        P.emit("scalar", lambda e, o=eh[:, :], i=ph4[:, 0:16]: e.copy(o, i), reads=[pb4], writes=[eb])
        ah2, ab2 = acb[c]
        P.emit("vector", lambda e, o=ah2[:, :], a=eh[:, :], b=nh[:, :]: e.tensor_add(o, a, b),
               reads=[eb, nb], writes=[ab2])
        P.emit("scalar", lambda e, o=ah2[:, :], i=ah2[:, :]: e.activation(o, i, AF.Exp), reads=[ab2], writes=[ab2])

    kkp = [P.sb(f"kk{k}", [128, 128], BF16) for k in range(2)]
    pqkp = [P.sb(f"pqk{k}", [128, 128], BF16) for k in range(2)]
    knp = [P.sb(f"kn{k}", [128, 128], BF16) for k in range(2)]
    SC = []
    for s_ in range(2):
        d = dict(
            cgrow=P.sb(f"cgrow{s_}", [1, 128], F32),
            cgb=P.sb(f"cgb{s_}", [128, 128], F32),
            t1w=P.sb(f"t1w{s_}", [128, 128], F32),
            ew=P.sb(f"ew{s_}", [128, 128], F32),
            vn=P.sb(f"vn{s_}", [128, 128], BF16),
            w=P.sb(f"wm{s_}", [128, 128], BF16),
            wt=P.sb(f"wtm{s_}", [128, 128], BF16),
            pw=[P.sb(f"pw{s_}_{k}", [128, 128], BF16) for k in range(2)],
            pts=[P.sb(f"pts{s_}_{k}", [128, 128], BF16) for k in range(7)],
            at=P.sb(f"atm{s_}", [128, 128], BF16),
            rr=P.sb(f"rr{s_}", [128, 256], BF16),
            uu=[P.sb(f"uu{s_}_{k}", [128, 256], BF16) for k in range(2)],
            ukt=P.sb(f"ukt{s_}", [128, 128], BF16),
            tc=P.sb(f"tc{s_}", [128, 128], BF16),
            alc=P.sb(f"alc{s_}", [128, 1], F32),
            abc=P.sb(f"abc{s_}", [128, 1], F32),
            ksc=P.sb(f"ksc{s_}", [128, 1], F32),
            ktl=P.sb(f"ktl{s_}", [128, 128], BF16),
            of=P.sb(f"of{s_}", [128, 128], F32),
            on=P.sb(f"on{s_}", [128, 128], BF16),
            ovar=P.sb(f"ovar{s_}", [128, 1], F32),
        )
        d["fA"] = psf[2 * s_]
        d["fB"] = psf[2 * s_ + 1]
        d["bT"] = psb[s_]
        SC.append(d)
    pre_f0 = psf[4]
    pre_f1 = psf[4]
    pre_b0 = psb[2]
    st32 = [P.sb(f"st32_{k}", [128, 128], F32) for k in range(2)]
    stbf = [P.sb(f"stbf_{k}", [128, 128], BF16) for k in range(2)]
    ohead = [P.sb(f"oh{k}", [128, S], BF16) for k in range(2)]

    def sub_ops(kh, c, sub):
        ops = []

        def E(*a, **k):
            ops.append(lambda a=a, k=k: P.emit(*a, **k))

        def M(*a, **k):
            ops.append(lambda a=a, k=k: P.mm(*a, **k))

        def T(*a):
            ops.append(lambda a=a: P.tr(*a))

        S_ = SC[sub]
        cs = slice(c * CC, (c + 1) * CC)
        h = 2 * kh + sub
        qh, qb = qt[kh]
        vh_, vb_ = vt[h]
        kkh, kkb = kkp[c % 2]
        pqh, pqb = pqkp[c % 2]
        knh, knb = knp[c % 2]
        sh32, sb32 = st32[sub]
        shbf, sbbf = stbf[sub]
        cgrow, cgrow_b = S_["cgrow"]
        cgb, cgb_b = S_["cgb"]
        t1w, t1w_b = S_["t1w"]
        ew, ew_b = S_["ew"]
        vn_sb, vn_b = S_["vn"]
        w_sb, w_b = S_["w"]
        wt_sb, wt_b = S_["wt"]
        pw = S_["pw"]
        pts = S_["pts"]
        at_sb, at_b = S_["at"]
        rr, rr_b = S_["rr"]
        uu = S_["uu"]
        ukt_sb, ukt_b = S_["ukt"]
        tc_sb, tc_b = S_["tc"]
        alc, alc_b = S_["alc"]
        abc, abc_b = S_["abc"]
        ksc, ksc_b = S_["ksc"]
        ktl, ktl_b = S_["ktl"]
        of_sb, of_b = S_["of"]
        on_sb, on_b = S_["on"]
        ovar, ovar_b = S_["ovar"]
        fA = S_["fA"]
        fB = S_["fB"]
        bT_ = S_["bT"]
        gcol = gbt[c][0][:, h:h + 1]
        gcol_b = gbt[c][1]
        bcol = gbt[c][0][:, 16 + h:16 + h + 1]
        ph3, pb3 = fA
        T(ph3[0:1, 0:128], pb3, gcol, gcol_b, idf[:, :], idf_b)
        E("scalar", lambda e, o=cgrow[:, :], i=ph3[0:1, 0:128]: e.copy(o, i), reads=[pb3], writes=[cgrow_b])
        ph4, pb4 = fB
        M(ph4[:, 0:128], pb4, [(o1_sb[:, :], cgrow[:, :])], [o1_b, cgrow_b])
        E("scalar", lambda e, o=cgb[:, :], i=ph4[:, 0:128]: e.copy(o, i), reads=[pb4], writes=[cgb_b])
        E("vector", lambda e, o=t1w[:, :], i=cgb[:, :], s=gcol, b=mw_sb[:, :]:
          e.scalar_tensor_tensor(o, i, s, b, ALU.subtract, ALU.add),
          reads=[cgb_b, gcol_b, mw_b], writes=[t1w_b])
        E("scalar", lambda e, o=ew[:, :], i=t1w[:, :]: e.activation(o, i, AF.Exp, scale=-1.0),
          reads=[t1w_b], writes=[ew_b])
        E("vector", lambda e, o=w_sb[:, :], i=ew[:, :], s=bcol, b=kkh[:, :]:
          e.scalar_tensor_tensor(o, i, s, b, ALU.mult, ALU.mult),
          reads=[ew_b, gcol_b, kkb], writes=[w_b])
        pbh, pbb = bT_
        T(pbh[:, 0:128], pbb, w_sb[:, :], w_b, idb[:, :], idb_b)
        E("scalar", lambda e, o=wt_sb[:, :], i=pbh[:, 0:128]: e.copy(o, i), reads=[pbb], writes=[wt_b])
        E("vector", lambda e, o=t1w[:, :], i=cgb[:, :], s=gcol, b=ma_sb[:, :]:
          e.scalar_tensor_tensor(o, i, s, b, ALU.subtract, ALU.subtract),
          reads=[cgb_b, gcol_b, ma_b], writes=[t1w_b])
        E("scalar", lambda e, o=ew[:, :], i=t1w[:, :]: e.activation(o, i, AF.Exp),
          reads=[t1w_b], writes=[ew_b])
        E("vector", lambda e, o=at_sb[:, :], a=ew[:, :], b=pqh[:, :]: e.tensor_mul(o, a, b),
          reads=[ew_b, pqb], writes=[at_b])
        E("scalar", lambda e, o=alc[:, :], i=gcol, b=npeb[c][0][:, h:h + 1]:
          e.activation(o, i, AF.Exp, bias=b),
          reads=[gcol_b, npeb[c][1]], writes=[alc_b])
        E("vector", lambda e, o=abc[:, :], a=alc[:, :], b=bcol: e.tensor_mul(o, a, b),
          reads=[alc_b, gcol_b], writes=[abc_b])
        E("scalar", lambda e, o=ksc[:, :], i=gcol, b=ebt[c][0][:, h:h + 1]:
          e.activation(o, i, AF.Exp, scale=-1.0, bias=b),
          reads=[gcol_b, ebt[c][1]], writes=[ksc_b])
        pbh2, pbb2 = bT_
        T(pbh2[:, 0:128], pbb2, vh_[:, cs], vb_, idb[:, :], idb_b)
        E("scalar", lambda e, o=vn_sb[:, :], i=pbh2[:, 0:128]: e.copy(o, i), reads=[pbb2], writes=[vn_b])
        E("vector", lambda e, o=rr[:, 0:128], i=vn_sb[:, :], s=bcol: e.tensor_scalar_mul(o, i, s),
          reads=[vn_b, gcol_b], writes=[rr_b])
        E("vector", lambda e, o=rr[:, 128:256], i=knh[:, :], s=abc[:, :]: e.tensor_scalar_mul(o, i, s),
          reads=[knb, abc_b], writes=[rr_b])
        E("vector", lambda e, o=pw[0][0][:, :], i=w_sb[:, :]: e.tensor_copy(o, i),
          reads=[w_b], writes=[pw[0][1]])
        E("vector", lambda e, o=pts[0][0][:, :], i=wt_sb[:, :]: e.tensor_copy(o, i),
          reads=[wt_b], writes=[pts[0][1]])
        for lvl in range(6):
            src_p, src_pb = pw[lvl % 2]
            dst_p, dst_pb = pw[(lvl + 1) % 2]
            pt_s, pt_sb = pts[lvl]
            pt_d, pt_db = pts[lvl + 1]
            pbh3, pbb3 = fA
            if lvl < 5:
                M(pbh3[:, 0:128], pbb3, [(pt_s[:, :], src_p[:, :])], [pt_sb, src_pb])
                E("scalar", lambda e, o=dst_p[:, :], i=pbh3[:, 0:128]: e.copy(o, i),
                  reads=[pbb3], writes=[dst_pb])
            pbh4, pbb4 = fB
            M(pbh4[:, 0:128], pbb4, [(src_p[:, :], pt_s[:, :])], [src_pb, pt_sb])
            E("scalar", lambda e, o=pt_d[:, :], i=pbh4[:, 0:128]: e.copy(o, i),
              reads=[pbb4], writes=[pt_db])
        E("vector", lambda e, o=uu[0][0][:, :], i=rr[:, :]: e.tensor_copy(o, i),
          reads=[rr_b], writes=[uu[0][1]])
        cu = 0
        for i in [6, 5, 4, 3, 2, 1]:
            ph5, pb5 = fA
            M(ph5[:, 0:256], pb5, [(pts[i][0][:, :], uu[cu][0][:, :])], [pts[i][1], uu[cu][1]])
            E("vector", lambda e, o=uu[1 - cu][0][:, :], a=uu[cu][0][:, :], b=ph5[:, 0:256]:
              e.tensor_add(o, a, b), reads=[uu[cu][1], pb5], writes=[uu[1 - cu][1]])
            cu = 1 - cu
        ph5, pb5 = fA
        M(ph5[:, 0:256], pb5, [(pts[0][0][:, :], uu[cu][0][:, :])], [pts[0][1], uu[cu][1]])
        E("vector", lambda e, o=uu[1 - cu][0][:, :], a=uu[cu][0][:, :], b=ph5[:, 0:256]:
          e.tensor_sub(o, a, b), reads=[uu[cu][1], pb5], writes=[uu[1 - cu][1]])
        cu = 1 - cu
        uh, ub_ = uu[cu]
        pbh5, pbb5 = bT_
        T(pbh5[:, 0:128], pbb5, uh[:, 128:256], ub_, idb[:, :], idb_b)
        E("scalar", lambda e, o=ukt_sb[:, :], i=pbh5[:, 0:128]: e.copy(o, i),
          reads=[pbb5], writes=[ukt_b])
        ph6, pb6 = fA
        M(ph6[:, 0:128], pb6, [(ukt_sb[:, :], shbf[:, :])], [ukt_b, sbbf])
        E("vector", lambda e, o=tc_sb[:, :], a=uh[:, 0:128], b=ph6[:, 0:128]:
          e.tensor_sub(o, a, b), reads=[ub_, pb6], writes=[tc_b])
        ph7, pb7 = fB
        M(ph7[:, 0:128], pb7, [(qh[:, cs], shbf[:, :])], [qb, sbbf])
        E("vector", lambda e, o=of_sb[:, :], i=ph7[:, 0:128], s=alc[:, :]:
          e.tensor_scalar_mul(o, i, s), reads=[pb7, alc_b], writes=[of_b])
        ph8, pb8 = fA
        M(ph8[:, 0:128], pb8, [(at_sb[:, :], tc_sb[:, :])], [at_b, tc_b])
        E("vector", lambda e, o=of_sb[:, :], a=of_sb[:, :], b=ph8[:, 0:128]:
          e.tensor_add(o, a, b), reads=[of_b, pb8], writes=[of_b])
        E("vector", lambda e, o=ew[:, :], a=of_sb[:, :], b=of_sb[:, :]: e.tensor_mul(o, a, b),
          reads=[of_b], writes=[ew_b])
        E("vector", lambda e, o=ovar[:, :], i=ew[:, :]: e.reduce_sum(o, i, AX.X),
          reads=[ew_b], writes=[ovar_b])
        E("vector", lambda e, o=ovar[:, :], i=ovar[:, :]: e.tensor_scalar(o, i, 1.0 / DV, 1e-6, ALU.mult, ALU.add),
          reads=[ovar_b], writes=[ovar_b])
        E("scalar", lambda e, o=ovar[:, :], i=ovar[:, :]: e.activation(o, i, AF.Ln),
          reads=[ovar_b], writes=[ovar_b])
        E("scalar", lambda e, o=ovar[:, :], i=ovar[:, :]: e.activation(o, i, AF.Exp, scale=-0.5),
          reads=[ovar_b], writes=[ovar_b])
        E("vector", lambda e, o=on_sb[:, :], i=of_sb[:, :], s=ovar[:, :], b=nwb[:, :]:
          e.scalar_tensor_tensor(o, i, s, b, ALU.mult, ALU.mult),
          reads=[of_b, ovar_b, nwb_b], writes=[on_b])
        oh, ob = ohead[sub]
        pbh6, pbb6 = bT_
        T(pbh6[:, 0:128], pbb6, on_sb[:, :], on_b, idb[:, :], idb_b)
        E("scalar", lambda e, o=oh[:, cs], i=pbh6[:, 0:128]: e.copy(o, i),
          reads=[pbb6], writes=[ob])
        E("vector", lambda e, o=ktl[:, :], i=knh[:, :], s=ksc[:, :]: e.tensor_scalar_mul(o, i, s),
          reads=[knb, ksc_b], writes=[ktl_b])
        ph9, pb9 = fB
        M(ph9[:, 0:128], pb9, [(ktl[:, :], tc_sb[:, :])], [ktl_b, tc_b])
        E("vector", lambda e, o=sh32[:, :], i=sh32[:, :], s=acb[c][0][:, h:h + 1], b=ph9[:, 0:128]:
          e.scalar_tensor_tensor(o, i, s, b, ALU.mult, ALU.add),
          reads=[sb32, acb[c][1], pb9], writes=[sb32])
        E("vector", lambda e, o=shbf[:, :], i=sh32[:, :]: e.tensor_copy(o, i),
          reads=[sb32], writes=[sbbf])
        return ops

    for kh in range(8):
        qh, qb = qt[kh]
        khh, khb = kt[kh]
        for sub in range(2):
            sh, sb_ = st32[sub]
            P.emit("vector", lambda e, o=sh[:, :]: e.memset(o, 0.0), writes=[sb_])
            bh, bb = stbf[sub]
            P.emit("vector", lambda e, o=bh[:, :]: e.memset(o, 0.0), writes=[bb])
        for c in range(NCH):
            cs = slice(c * CC, (c + 1) * CC)
            kkh, kkb = kkp[c % 2]
            pqh, pqb = pqkp[c % 2]
            knh, knb = knp[c % 2]
            ph1, pb1 = pre_f0
            P.mm(ph1[:, 0:128], pb1, [(khh[:, cs], khh[:, cs])], [khb])
            P.emit("scalar", lambda e, o=kkh[:, :], i=ph1[:, 0:128]: e.copy(o, i), reads=[pb1], writes=[kkb])
            ph2, pb2 = pre_f1
            P.mm(ph2[:, 0:128], pb2, [(khh[:, cs], qh[:, cs])], [khb, qb])
            P.emit("scalar", lambda e, o=pqh[:, :], i=ph2[:, 0:128]: e.copy(o, i), reads=[pb2], writes=[pqb])
            pb_h, pb_b = pre_b0
            P.tr(pb_h[:, 0:128], pb_b, khh[:, cs], khb, idb[:, :], idb_b)
            P.emit("scalar", lambda e, o=knh[:, :], i=pb_h[:, 0:128]: e.copy(o, i), reads=[pb_b], writes=[knb])
            o0 = sub_ops(kh, c, 0)
            o1 = sub_ops(kh, c, 1)
            for a, b in zip(o0, o1):
                a()
                b()
        for sub in range(2):
            h = 2 * kh + sub
            oh, ob = ohead[sub]
            for blk in range(2):
                mh2, mb2 = mscr[0]
                P.emit("vector", lambda e, o=mh2[:, :], i=oh[:, :], s=mk_sb[:, blk:blk + 1]:
                       e.tensor_scalar_mul(o, i, s), reads=[ob, mk_b], writes=[mb2])
                P.dma(rsv[16 * blk + h], mh2[:, :].rearrange("p (j t) -> p j t", j=2),
                      rsin_b, src_buf=mb2)


def emit_gdnout(P, x_t, xo_t, onorm_t, zs_t, wout_t):
    x_ap, x_b = x_t
    xo_ap, xo_b = xo_t
    onorm, onorm_b = onorm_t
    zs, zs_b = zs_t
    wout, wout_b = wout_t

    psf, psb, scr = _alloc_common(P)
    xt = [P.sb(f"x{k}", [128, TOK], F32) for k in range(2)]
    xn = [P.sb(f"xn{k}", [128, TOK], F32) for k in range(2)]
    of = [P.sb(f"of{i}", [128, TOK], BF16) for i in range(32)]
    zt = [P.sb(f"zt{k}", [128, TOK], BF16) for k in range(2)]
    for h in range(32):
        zh, zb = zt[h % 2]
        P.dma(zh[:, :], zs[h], zb, src_buf=zs_b)
        oh, ob = of[h]
        P.dma(oh[:, :], onorm[h], ob, src_buf=onorm_b)
        P.emit("vector", lambda e, o=oh[:, :], a=oh[:, :], b=zh[:, :]: e.tensor_mul(o, a, b),
               reads=[ob, zb], writes=[ob])
    wo_s = [P.sb(f"wo{k}", [128, 4096], BF16) for k in range(2)]
    wor = wout.rearrange("(h p) (d c) -> d p h c", p=128, c=128)
    xr = x_ap.rearrange("(i p) t -> i p t", p=128)
    xor_ = xo_ap.rearrange("(i p) t -> i p t", p=128)
    for d in range(ND):
        wh, wb = wo_s[d % 2]
        P.dma(wh[:, :], wor[d], wb, src_buf=wout_b, cast=True)
        xh, xb = xt[d % 2]
        P.dma(xh[:, :], xr[d], xb, src_buf=x_b)
        xnh, xnb = xn[d % 2]
        for tb in range(2):
            sl = slice(tb * TB, (tb + 1) * TB)
            ph, pb = psf[2 + tb]
            pairs = [(wh[:, hh * 128:(hh + 1) * 128], of[hh][0][:, sl]) for hh in range(32)]
            P.mm(ph[:, :], pb, pairs, [wb] + [of[hh][1] for hh in range(32)])
            P.emit("vector", lambda e, o=xnh[:, sl], a=xh[:, sl], b=ph[:, :]: e.tensor_add(o, a, b),
                   reads=[xb, pb], writes=[xnb])
        P.dma(xor_[d], xnh[:, :], xo_b, src_buf=xnb)


def _f32(a):
    return np.ascontiguousarray(a, dtype=np.float32)

def kernel(**inp):
    C = host_consts()

    x = _f32(inp["L0_attn_x"])
    xs = []
    for c in range(NCORE):
        b, hf = c // 2, c % 2
        xs.append(np.ascontiguousarray(x[b, hf * TOK:(hf + 1) * TOK, :].T))

    WupT = np.ascontiguousarray(_f32(inp["L0_ffn_W_up"]).T)
    WgT = np.ascontiguousarray(_f32(inp["L0_ffn_W_gate"]).T)
    WdT = np.ascontiguousarray(_f32(inp["L0_ffn_W_down"]).T)
    Wdp = np.ascontiguousarray(WdT.reshape(64, 64, 2, D).transpose(1, 0, 2, 3).reshape(64, 128, D))
    su_ug = np.ascontiguousarray(_f32(inp["L0_ffn_SU_ug"]).reshape(16, 128).T) / np.sqrt(np.float32(D))
    su_dp = np.ascontiguousarray(_f32(inp["L0_ffn_SU_dp"]).reshape(64, 128).T)
    ws_up = np.ascontiguousarray(_f32(inp["L0_ffn_Wscale_ug"])[:INTER].reshape(64, 128).T)
    ws_g = np.ascontiguousarray(_f32(inp["L0_ffn_Wscale_ug"])[INTER:].reshape(64, 128).T)
    sw_up = np.ascontiguousarray(su_dp * ws_up)
    ws_d = np.ascontiguousarray(_f32(inp["L0_ffn_Wscale_dp"]).reshape(16, 128).T) * 0.5
    BT = np.ascontiguousarray(_f32(inp["L0_attn_B"]).T)
    CT = np.ascontiguousarray(_f32(inp["L0_attn_C"]).T)
    Dv = np.ascontiguousarray(_f32(inp["L0_attn_D"]).reshape(16, 128).T)

    Wr = _f32(inp["L1_ffn_Wr"])
    br = _f32(inp["L1_ffn_br"]).reshape(8, 1)
    WdS = np.ascontiguousarray(_f32(inp["L1_ffn_Wd"]).transpose(1, 0, 2).reshape(D, 128))
    bdS = _f32(inp["L1_ffn_bd"]).reshape(128, 1)
    WuS = np.ascontiguousarray(_f32(inp["L1_ffn_Wu"]).reshape(128, D))
    bu = _f32(inp["L1_ffn_bu"])

    wqkv = _f32(inp["L1_attn_w_qkv"])
    wz = _f32(inp["L1_attn_w_z"])
    wba = np.ascontiguousarray(np.concatenate([_f32(inp["L1_attn_w_b"]), _f32(inp["L1_attn_w_a"])], axis=1))
    alog = _f32(inp["L1_attn_A_log"]).reshape(32, 1)
    dtb = _f32(inp["L1_attn_dt_bias"]).reshape(32, 1)
    cwr = np.ascontiguousarray(_f32(inp["L1_attn_conv_w"]).reshape(64, 128, 4))
    nw = _f32(inp["L1_attn_norm_weight"]).reshape(1, 128)
    wout = _f32(inp["L1_attn_w_out"])

    wqkvc, wbacs, alogcs, dtbcs, cwqs, cwks, cwvs, monorms = [], [], [], [], [], [], [], []
    for hg in range(2):
        wqkvc.append(np.ascontiguousarray(np.concatenate(
            [wqkv[:, 1024 * hg:1024 * hg + 1024],
             wqkv[:, 2048 + 1024 * hg:2048 + 1024 * hg + 1024],
             wqkv[:, 4096 + 2048 * hg:4096 + 2048 * hg + 2048]], axis=1)))
        wbacs.append(np.ascontiguousarray(np.concatenate(
            [wba[:, 16 * hg:16 * hg + 16], wba[:, 32 + 16 * hg:32 + 16 * hg + 16]], axis=1)))
        alogcs.append(np.ascontiguousarray(alog[16 * hg:16 * hg + 16]))
        dtbcs.append(np.ascontiguousarray(dtb[16 * hg:16 * hg + 16]))
        cwqs.append(np.ascontiguousarray(cwr[8 * hg:8 * hg + 8]))
        cwks.append(np.ascontiguousarray(cwr[16 + 8 * hg:16 + 8 * hg + 8]))
        cwvs.append(np.ascontiguousarray(cwr[32 + 16 * hg:32 + 16 * hg + 16]))
        monorms.append(np.tile(np.array([1.0 - hg, float(hg)], np.float32), (128, 1)))

    P = Prog()
    x0 = P.din("x", [D, TOK], F32)
    xo_t = P.dout("xo", [D, TOK], F32)
    ones1_t = P.din("ones1", [1, 128], F32)
    ones128_t = P.din("ones128", [128, 1], F32)
    wdp_t = P.din("wdp", [64, 128, D], F32)
    h64_t = P.din("h64i2", [128, 128], F32)
    iden_t = P.din("iden", [128, 128], F32)
    wdf_t = P.dint("wdf", [INTER, D], BF16)
    x1 = P.dint("x1", [D, TOK], F32)
    x2 = P.dint("x2", [D, TOK], F32)
    x3 = P.dint("x3", [D, TOK], F32)
    x4 = P.dint("x4", [D, TOK], F32)
    x5 = P.dint("x5", [D, TOK], F32)
    xnin = P.dint("xnin", [D, TOK], BF16)
    xnf = P.dint("xnf", [2 * D, TOK], BF16)
    zsd = P.dint("zsd", [32, 128, TOK], BF16)
    qind = P.dint("qind", [8, 128, 2048], BF16)
    kind_ = P.dint("kind", [8, 128, 2048], BF16)
    vind = P.dint("vind", [16, 128, 2048], BF16)
    betad = P.dint("betad", [16, 2048], F32)
    gd = P.dint("gd", [16, 2048], F32)
    rsin = P.dint("rsin", [64, 128, TOK], BF16)
    rsout = P.dint("rsout", [32, 128, TOK], BF16)

    W0 = dict(wupT=P.din("wupT", [D, INTER], F32), wgT=P.din("wgT", [D, INTER], F32),
              wdf=wdf_t, su_ug=P.din("su_ug", [128, 16], F32), sw_up=P.din("sw_up", [128, 64], F32),
              ws_g=P.din("ws_g", [128, 64], F32), ws_d=P.din("ws_d", [128, 16], F32),
              h128=P.din("h128", [128, 128], F32), ones1=ones1_t, ones128=ones128_t)
    WA = dict(bT=P.din("bT", [D, 1024], F32), cT=P.din("cT", [1024, D], F32),
              dvec=P.din("dvec", [128, 16], F32), ones1=ones1_t, ones128=ones128_t)
    WM = dict(wr=P.din("wr", [D, 8], F32), br=P.din("br", [8, 1], F32),
              wds=P.din("wds", [D, 128], F32), bds=P.din("bds", [128, 1], F32),
              wus=P.din("wus", [128, D], F32), bu=P.din("bu", [8, D], F32),
              e16=P.din("e16", [8, 128], F32), iden=iden_t, ones1=ones1_t, ones128=ones128_t)
    WZ = dict(wz=P.din("wz", [D, 4096], F32), ones1=ones1_t, ones128=ones128_t)
    WP = dict(wqkvc=P.din("wqkvc", [D, 4096], F32), wbac=P.din("wbac", [D, 32], F32),
              alogc=P.din("alogc", [16, 1], F32), dtbc=P.din("dtbc", [16, 1], F32))
    WC = dict(cwq=P.din("cwq", [8, 128, 4], F32), cwk=P.din("cwk", [8, 128, 4], F32),
              cwv=P.din("cwv", [16, 128, 4], F32), monorm=P.din("monorm", [128, 2], F32),
              nw=P.din("nw", [1, 128], F32), maskw=P.din("maskw", [128, 128], F32),
              maskat=P.din("maskat", [128, 128], F32), iden=iden_t,
              ones1=ones1_t, ones128=ones128_t)
    wout_t = P.din("wout", [4096, D], F32)

    P.push("s0_")
    emit_prep(P, wdp_t, h64_t, wdf_t)
    P.pop()
    P.push("s1_")
    emit_ffn0(P, x0, x1, W0)
    P.pop()
    P.push("s2_")
    emit_attn0(P, x1, x2, WA)
    P.pop()
    P.push("s3_")
    emit_ffn0(P, x2, x3, W0)
    P.pop()
    P.push("s4_")
    emit_moe(P, x3, x4, WM)
    P.pop()
    P.push("s5_")
    emit_gdnz(P, x4, WZ, xnin, xnf, zsd)
    P.pop()
    P.push("s6_")
    emit_gdnpc(P, xnf, WP, qind, kind_, vind, betad, gd)
    P.pop()
    P.push("s7_")
    emit_gdnchunk(P, WC, qind, kind_, vind, betad, gd, rsin)
    P.cc("ReduceScatter", ALU.add, rsin[0], rsin[1], rsout[0], rsout[1], "cc1")
    P.pop()
    P.push("s8_")
    emit_gdnout(P, x4, x5, rsout, zsd, wout_t)
    P.pop()
    P.push("s9_")
    emit_moe(P, x5, xo_t, WM, final=True)
    P.pop()

    maps = []
    for c in range(NCORE):
        hg = c % 2
        maps.append(dict(
            x=xs[c], ones1=C["ONES1"], ones128=C["ONES128"], wdp=Wdp, h64i2=C["H64I2"],
            iden=C["IDEN"], wupT=WupT, wgT=WgT, su_ug=su_ug, sw_up=sw_up, ws_g=ws_g,
            ws_d=ws_d, h128=C["H128"], bT=BT, cT=CT, dvec=Dv, wr=Wr, br=br, wds=WdS,
            bds=bdS, wus=WuS, bu=bu, e16=C["E16"], wz=wz, wqkvc=wqkvc[hg],
            wbac=wbacs[hg], alogc=alogcs[hg], dtbc=dtbcs[hg], cwq=cwqs[hg],
            cwk=cwks[hg], cwv=cwvs[hg], monorm=monorms[hg], nw=nw, maskw=C["MASKW"],
            maskat=C["MASKAT"], wout=wout))

    r = P.run(maps, core_ids=list(range(NCORE)))
    out = np.empty((4, 2048, D), np.float32)
    for c in range(NCORE):
        b, hf = c // 2, c % 2
        out[b, hf * TOK:(hf + 1) * TOK, :] = np.asarray(r.results[c]["xo"], np.float32).T
    return out


# version 11 (final)
# speedup vs baseline: 1.1986x; 1.0012x over previous
import contextlib
import numpy as np
import ml_dtypes
import concourse.bass as bass
import concourse.mybir as mybir
from concourse.bass_utils import run_bass_kernel_spmd

F32 = mybir.dt.float32
BF16 = mybir.dt.bfloat16
AF = mybir.ActivationFunctionType
ALU = mybir.AluOpType
AX = mybir.AxisListType
BF = ml_dtypes.bfloat16

NCORE = 8
TOK = 1024
TB = 512
D = 2048
ND = 16
INTER = 8192
NI = 64
HK, HV, DK, DV = 16, 32, 128, 128
CC = 128

ENGS = ("sync", "scalar", "vector", "tensor", "gpsimd")
SERIALIZE = False
PRUNE = True
SER_SET = ()


class _Buf:
    __slots__ = ("w", "r")

    def __init__(self):
        self.w = None
        self.r = []


class _Ev:
    __slots__ = ("e", "c", "vc")

    def __init__(self, e, c, vc):
        self.e = e
        self.c = c
        self.vc = vc


def _clb(b):
    nb = _Buf()
    nb.w = b.w
    nb.r = list(b.r)
    return nb


RGROUPS = [[0, 1], [2, 3], [4, 5], [6, 7]]


class Prog:
    def __init__(self):
        self.nc = bass.Bass("TRN2", target_bir_lowering=False)
        self.stack = contextlib.ExitStack()
        self.plan = {e: [] for e in ENGS}
        self.cnt = dict.fromkeys(ENGS, 0)
        self.vc = {e: {} for e in ENGS}
        self.outev = {}
        self.pfx = ""
        self.scopes = []
        self.bar = None

    def push(self, pfx):
        self.pfx = pfx
        self.scopes.append(contextlib.ExitStack())

    def pop(self):
        self.barrier()
        self.scopes.pop().close()

    def barrier(self):
        self.bar = {e: self.cnt.get(e, 0) for e in ENGS}

    def din(self, name, shape, dt):
        h = self.nc.dram_tensor(name, list(shape), dt, kind="ExternalInput")
        return h.ap(), _Buf()

    def dout(self, name, shape, dt):
        h = self.nc.dram_tensor(name, list(shape), dt, kind="ExternalOutput")
        return h.ap(), _Buf()

    def dint(self, name, shape, dt):
        h = self.nc.dram_tensor(name, list(shape), dt)
        return h.ap(), _Buf()

    def sb(self, name, shape, dt):
        st = self.scopes[-1] if self.scopes else self.stack
        h = st.enter_context(self.nc.sbuf_tensor(self.pfx + name, list(shape), dt))
        return h, _Buf()

    def ps(self, name, shape, dt):
        st = self.scopes[-1] if self.scopes else self.stack
        h = st.enter_context(self.nc.psum_tensor(self.pfx + name, list(shape), dt))
        return h, _Buf()

    def emit(self, eng, fn, reads=(), writes=(), inc=1, out_dma=False, sem=None):
        deps = {}

        def add(ev):
            if ev is None:
                return
            cur = deps.get(ev.e)
            if cur is None or ev.c > cur.c:
                deps[ev.e] = ev

        for b in reads:
            if b is not None:
                add(b.w)
        for b in writes:
            if b is not None:
                add(b.w)
                for ev in b.r:
                    add(ev)
        if self.bar:
            for p, v in self.bar.items():
                if v > 0 and p != eng:
                    cur = deps.get(p)
                    if cur is None or v > cur.c:
                        deps[p] = _Ev(p, v, {})
        waits = []
        myvc = self.vc[eng]
        if SERIALIZE or SER_SET:
            ser = ENGS if SERIALIZE else SER_SET
            for p in ser:
                if p != eng and self.cnt[p] > 0:
                    ev0 = _Ev(p, self.cnt[p], {})
                    cur = deps.get(p)
                    if cur is None or ev0.c > cur.c:
                        deps[p] = ev0
        for p in list(deps):
            if p in ("sync", "gpsimd"):
                ev0 = deps[p]
                if ev0.c < self.cnt[p]:
                    deps[p] = _Ev(p, self.cnt[p], ev0.vc)
        for p in sorted(deps):
            ev = deps[p]
            if p == "tensor" and eng == "tensor":
                continue
            if PRUNE and myvc.get(p, 0) >= ev.c:
                continue
            waits.append((p, ev.c))
            for q, v in ev.vc.items():
                if myvc.get(q, 0) < v:
                    myvc[q] = v
            if myvc.get(p, 0) < ev.c:
                myvc[p] = ev.c
        key = sem or eng
        self.cnt[key] = self.cnt.get(key, 0) + inc
        ev = _Ev(key, self.cnt[key], dict(myvc))
        ev.vc[key] = ev.c
        self.plan[eng].append((waits, fn, inc, key))
        for b in writes:
            if b is not None:
                b.w = ev
                b.r = []
        for b in reads:
            if b is not None:
                b.r.append(ev)
        if out_dma:
            self.outev[eng] = self.cnt[eng]
        return ev

    def cc(self, kind, op, in_ap, in_buf, out_ap, out_buf, key):
        def fn(g, kind=kind, op=op, i=in_ap, o=out_ap):
            return g.collective_compute(kind, op, replica_groups=RGROUPS,
                                        ins=[i.opt()], outs=[o.opt()])
        self.emit("gpsimd", fn, reads=[in_buf], writes=[out_buf], inc=1, sem=key)

    def dma(self, dst_ap, src_ap, dst_buf, src_buf=None, cast=False, out=False, eng=None):
        e = eng or ("gpsimd" if cast else "sync")
        self.emit(e, lambda E, d=dst_ap, s=src_ap: E.dma_start(d, s),
                  reads=[src_buf] if src_buf is not None else [],
                  writes=[dst_buf] if dst_buf is not None else [],
                  inc=16, out_dma=out)

    def mm(self, ps_ap, ps_buf, pairs, reads, accum=False):
        n = len(pairs)

        def fn(te, pairs=tuple(pairs), o=ps_ap, n=n, accum=accum):
            ins = None
            for i, (l, r) in enumerate(pairs):
                ins = te.matmul(o, l, r, start=(i == 0 and not accum), stop=(i == n - 1))
            return ins
        self.emit("tensor", fn, reads=list(reads), writes=[ps_buf])

    def tr(self, ps_ap, ps_buf, in_ap, in_buf, iden_ap, iden_buf):
        self.emit("tensor", lambda te, o=ps_ap, i=in_ap, d=iden_ap: te.transpose(o, i, d),
                  reads=[in_buf, iden_buf], writes=[ps_buf])

    def run(self, in_maps, core_ids, trace=False):
        for e in ("sync", "gpsimd"):
            v = self.cnt[e]
            if v <= 0:
                continue
            tgt = "sync"
            myvc = self.vc[tgt]
            if myvc.get(e, 0) < v:
                self.plan[tgt].append(([(e, v)], None, 0, None))
                myvc[e] = v
        nc = self.nc
        sems = {}
        with contextlib.ExitStack() as st:
            for e in list(ENGS) + [k for k in self.cnt if k not in ENGS]:
                sems[e] = st.enter_context(nc.semaphore("S_" + e))
            with nc.Block() as block:
                def mk(eng_name):
                    def body(eng):
                        for waits, fn, inc, key in self.plan[eng_name]:
                            for (p, v) in waits:
                                eng.wait_ge(sems[p], v)
                            if fn is not None:
                                ins = fn(eng)
                                if inc:
                                    ins.then_inc(sems[key], inc)
                    return body
                for name, dec in (("sync", block.sync), ("scalar", block.scalar),
                                  ("vector", block.vector), ("tensor", block.tensor),
                                  ("gpsimd", block.gpsimd)):
                    if self.plan[name]:
                        dec(mk(name))
        self.stack.close()
        return run_bass_kernel_spmd(nc, in_maps, core_ids=core_ids, trace=trace)


def _had(n):
    H = np.array([[1.0]], np.float32)
    while H.shape[0] < n:
        H = np.block([[H, H], [H, -H]]).astype(np.float32)
    return H


def host_consts():
    c = {}
    c["H128"] = _had(128)
    H64 = _had(64)
    c["H64I2"] = np.kron(H64, np.eye(2, dtype=np.float32))
    c["IDEN"] = np.eye(128, dtype=np.float32)
    c["ONES1"] = np.ones((1, 128), np.float32)
    c["ONES128"] = np.ones((128, 1), np.float32)
    t = np.arange(128)
    c["MASKW"] = np.where(t[None, :] < t[:, None], 0.0, 1e9).astype(np.float32)
    c["MASKAT"] = np.where(t[:, None] <= t[None, :], 0.0, 1e9).astype(np.float32)
    E16 = np.zeros((8, 128), np.float32)
    for e in range(8):
        E16[e, e * 16:(e + 1) * 16] = 1.0
    c["E16"] = E16
    return c


def emit_ln_resident(P, xt, yt, psf, scr, ones128, ones1, post_cols=None, eps=1e-5):
    o128, o128b = ones128
    o1, o1b = ones1
    m_sb, m_b = P.sb("ln_m", [1, 1024], F32)
    r_sb, r_b = P.sb("ln_r", [1, 1024], F32)
    for tb in range(2):
        sl = slice(tb * TB, (tb + 1) * TB)
        ps_s, ps_sb_ = psf[0]
        ps_q, ps_qb = psf[1]
        for i in range(ND):
            xh, xb = xt[i]
            sh, sb_ = scr[i % 2]
            P.emit("vector", lambda e, o=sh[:, :TB], a=xh[:, sl]: e.tensor_mul(o, a, a),
                   reads=[xb], writes=[sb_])
            P.mm(ps_s[0:1, :], ps_sb_, [(o128[:, :], xh[:, sl])], [o128b, xb], accum=(i > 0))
            P.mm(ps_q[0:1, :], ps_qb, [(o128[:, :], sh[:, :TB])], [o128b, sb_], accum=(i > 0))
        P.emit("vector", lambda e, o=m_sb[0:1, sl], i=ps_s[0:1, :]: e.tensor_scalar_mul(o, i, 1.0 / D),
               reads=[ps_sb_], writes=[m_b])
        sh, sb_ = scr[0]
        P.emit("vector", lambda e, o=sh[0:1, 0:TB], i=m_sb[0:1, sl]: e.tensor_mul(o, i, i),
               reads=[m_b], writes=[sb_])
        P.emit("vector", lambda e, o=r_sb[0:1, sl], i=ps_q[0:1, :], s=sh[0:1, 0:TB]:
               e.scalar_tensor_tensor(o, i, 1.0 / D, s, ALU.mult, ALU.subtract),
               reads=[ps_qb, sb_], writes=[r_b])
        P.emit("vector", lambda e, o=r_sb[0:1, sl], i=r_sb[0:1, sl]: e.tensor_scalar_add(o, i, eps),
               reads=[r_b], writes=[r_b])
        P.emit("scalar", lambda e, o=r_sb[0:1, sl], i=r_sb[0:1, sl]: e.activation(o, i, AF.Ln),
               reads=[r_b], writes=[r_b])
        P.emit("scalar", lambda e, o=r_sb[0:1, sl], i=r_sb[0:1, sl]: e.activation(o, i, AF.Exp, scale=-0.5),
               reads=[r_b], writes=[r_b])
        ps_m, ps_mb = psf[2]
        ps_r, ps_rb = psf[3]
        P.mm(ps_m[:, :], ps_mb, [(o1[:, :], m_sb[0:1, sl])], [o1b, m_b])
        P.mm(ps_r[:, :], ps_rb, [(o1[:, :], r_sb[0:1, sl])], [o1b, r_b])
        for i in range(ND):
            xh, xb = xt[i]
            yh, yb = yt[i]
            sh, sb_ = scr[i % 2]
            P.emit("vector", lambda e, o=sh[:, :TB], a=xh[:, sl], b=ps_m[:, :]: e.tensor_sub(o, a, b),
                   reads=[xb, ps_mb], writes=[sb_])
            if post_cols is not None:
                ch, cb, cols = post_cols
                P.emit("vector", lambda e, o=yh[:, sl], a=sh[:, :TB], s=ch[:, cols[i]:cols[i] + 1], b=ps_r[:, :]:
                       e.scalar_tensor_tensor(o, a, s, b, ALU.mult, ALU.mult),
                       reads=[sb_, cb, ps_rb], writes=[yb])
            else:
                P.emit("vector", lambda e, o=yh[:, sl], a=sh[:, :TB], b=ps_r[:, :]: e.tensor_mul(o, a, b),
                       reads=[sb_, ps_rb], writes=[yb])
    return (m_sb, m_b, r_sb, r_b)


def emit_ln_streaming(P, x_ap, x_dbuf, yt, psf, scr, xs, ones128, ones1, post_cols=None, eps=1e-5):
    o128, o128b = ones128
    o1, o1b = ones1
    xr = x_ap.rearrange("(i p) t -> i p t", p=128)
    m_sb, m_b = P.sb("lns_m", [1, 1024], F32)
    r_sb, r_b = P.sb("lns_r", [1, 1024], F32)
    for i in range(ND):
        xh, xb = xs[i % 2]
        P.dma(xh[:, :], xr[i], xb, src_buf=x_dbuf)
        for tb in range(2):
            sl = slice(tb * TB, (tb + 1) * TB)
            sh, sb_ = scr[(2 * i + tb) % 2]
            P.emit("vector", lambda e, o=sh[:, :TB], a=xh[:, sl]: e.tensor_mul(o, a, a),
                   reads=[xb], writes=[sb_])
            ps_s, ps_sb_ = psf[tb]
            ps_q, ps_qb = psf[2 + tb]
            P.mm(ps_s[0:1, :], ps_sb_, [(o128[:, :], xh[:, sl])], [o128b, xb], accum=(i > 0))
            P.mm(ps_q[0:1, :], ps_qb, [(o128[:, :], sh[:, :TB])], [o128b, sb_], accum=(i > 0))
    for tb in range(2):
        sl = slice(tb * TB, (tb + 1) * TB)
        ps_s, ps_sb_ = psf[tb]
        ps_q, ps_qb = psf[2 + tb]
        P.emit("vector", lambda e, o=m_sb[0:1, sl], i=ps_s[0:1, :]: e.tensor_scalar_mul(o, i, 1.0 / D),
               reads=[ps_sb_], writes=[m_b])
        sh, sb_ = scr[0]
        P.emit("vector", lambda e, o=sh[0:1, 0:TB], i=m_sb[0:1, sl]: e.tensor_mul(o, i, i),
               reads=[m_b], writes=[sb_])
        P.emit("vector", lambda e, o=r_sb[0:1, sl], i=ps_q[0:1, :], s=sh[0:1, 0:TB]:
               e.scalar_tensor_tensor(o, i, 1.0 / D, s, ALU.mult, ALU.subtract),
               reads=[ps_qb, sb_], writes=[r_b])
        P.emit("vector", lambda e, o=r_sb[0:1, sl], i=r_sb[0:1, sl]: e.tensor_scalar_add(o, i, eps),
               reads=[r_b], writes=[r_b])
        P.emit("scalar", lambda e, o=r_sb[0:1, sl], i=r_sb[0:1, sl]: e.activation(o, i, AF.Ln),
               reads=[r_b], writes=[r_b])
        P.emit("scalar", lambda e, o=r_sb[0:1, sl], i=r_sb[0:1, sl]: e.activation(o, i, AF.Exp, scale=-0.5),
               reads=[r_b], writes=[r_b])
    for tb in range(2):
        sl = slice(tb * TB, (tb + 1) * TB)
        ps_m, ps_mb = psf[0 + tb]
        ps_r, ps_rb = psf[2 + tb]
        P.mm(ps_m[:, :], ps_mb, [(o1[:, :], m_sb[0:1, sl])], [o1b, m_b])
        P.mm(ps_r[:, :], ps_rb, [(o1[:, :], r_sb[0:1, sl])], [o1b, r_b])
    for i in range(ND):
        xh, xb = xs[i % 2]
        P.dma(xh[:, :], xr[i], xb, src_buf=x_dbuf)
        for tb in range(2):
            sl = slice(tb * TB, (tb + 1) * TB)
            ps_m, ps_mb = psf[0 + tb]
            ps_r, ps_rb = psf[2 + tb]
            yh, yb = yt[i]
            sh, sb_ = scr[(2 * i + tb) % 2]
            P.emit("vector", lambda e, o=sh[:, :TB], a=xh[:, sl], b=ps_m[:, :]: e.tensor_sub(o, a, b),
                   reads=[xb, ps_mb], writes=[sb_])
            if post_cols is not None:
                ch, cb, cols = post_cols
                P.emit("vector", lambda e, o=yh[:, sl], a=sh[:, :TB], s=ch[:, cols[i]:cols[i] + 1], b=ps_r[:, :]:
                       e.scalar_tensor_tensor(o, a, s, b, ALU.mult, ALU.mult),
                       reads=[sb_, cb, ps_rb], writes=[yb])
            else:
                P.emit("vector", lambda e, o=yh[:, sl], a=sh[:, :TB], b=ps_r[:, :]: e.tensor_mul(o, a, b),
                       reads=[sb_, ps_rb], writes=[yb])


def _alloc_common(P):
    psf = [P.ps(f"psf{k}", [128, 512], F32) for k in range(6)]
    psb = [P.ps(f"psb{k}", [128, 1024], BF16) for k in range(2)]
    scr = [P.sb(f"scr{k}", [128, 512], F32) for k in range(2)]
    return psf, psb, scr


def emit_prep(P, wdp_t, h64_t, wdf_t, ps):
    wdp, wdp_b = wdp_t
    h64, h64_b = h64_t
    wdf, wdf_b = wdf_t
    wdfr = wdf.rearrange("(a g j) d -> g a j d", a=64, g=64, j=2)
    h_sb, h_b = P.sb("h64sb", [128, 128], BF16)
    P.dma(h_sb[:, :], h64[:, :], h_b, src_buf=h64_b, cast=True)
    grp = [P.sb(f"grp{k}", [128, D], BF16) for k in range(2)]
    osb = [P.sb("osb0", [128, D], BF16)]
    sc = 1.0 / np.sqrt(INTER)
    for g in range(64):
        gh, gb = grp[g % 2]
        oh, ob = osb[0]
        P.dma(gh[:, :], wdp[g], gb, src_buf=wdp_b, cast=True)
        for j in range(4):
            sl = slice(j * 512, (j + 1) * 512)
            ph, pb = ps[j % 2]
            P.mm(ph[:, :], pb, [(h_sb[:, :], gh[:, sl])], [h_b, gb])
            P.emit("scalar", lambda e, o=oh[:, sl], i=ph[:, :]: e.mul(o, i, sc),
                   reads=[pb], writes=[ob])
        P.dma(wdfr[g], oh[:, :], wdf_b, src_buf=ob)


def emit_ffn0(P, x_t, xo_t, W, psum=None):
    x_ap, x_b = x_t
    xo_ap, xo_b = xo_t
    wupT, wupT_b = W["wupT"]
    wgT, wgT_b = W["wgT"]
    wdf, wdf_b = W["wdf"]
    su_ug, su_ug_b = W["su_ug"]
    sw_up, sw_up_b = W["sw_up"]
    ws_g, ws_g_b = W["ws_g"]
    ws_d, ws_d_b = W["ws_d"]
    h128, h128_b = W["h128"]
    ones1, ones1_b = W["ones1"]
    ones128, ones128_b = W["ones128"]

    psf, psb, scr = psum if psum is not None else _alloc_common(P)
    h_sb, h_b = P.sb("h128sb", [128, 128], BF16)
    P.dma(h_sb[:, :], h128[:, :], h_b, src_buf=h128_b, cast=True)
    o1_sb, o1_b = P.sb("o1", [1, 128], F32)
    P.dma(o1_sb[:, :], ones1[:, :], o1_b, src_buf=ones1_b)
    o128_sb, o128_b = P.sb("o128", [128, 1], F32)
    P.dma(o128_sb[:, :], ones128[:, :], o128_b, src_buf=ones128_b)
    suh, sub = P.sb("suh", [128, 16], F32)
    P.dma(suh[:, :], su_ug[:, :], sub, src_buf=su_ug_b)
    swh, swb = P.sb("swh", [128, 64], F32)
    P.dma(swh[:, :], sw_up[:, :], swb, src_buf=sw_up_b)
    wgh, wgb = P.sb("wsgh", [128, 64], F32)
    P.dma(wgh[:, :], ws_g[:, :], wgb, src_buf=ws_g_b)
    wdh, wdb = P.sb("wsdh", [128, 16], F32)
    P.dma(wdh[:, :], ws_d[:, :], wdb, src_buf=ws_d_b)

    xs = [P.sb(f"xs{k}", [128, TOK], F32) for k in range(2)]
    sA = [P.sb(f"sA{i}", [128, TOK], BF16) for i in range(ND)]
    sB = [P.sb(f"sB{i}", [128, TOK], BF16) for i in range(ND)]
    h2 = [P.sb(f"h2_{i}", [128, TOK], BF16) for i in range(32)]
    wu_s = [P.sb(f"wu{k}", [128, D], BF16) for k in range(2)]
    wg_s = [P.sb(f"wg{k}", [128, D], BF16) for k in range(2)]
    wd_s = [P.sb(f"wd{k}", [128, 4096], BF16) for k in range(2)]
    ug_scr = [P.sb(f"ug{k}", [128, TB], F32) for k in range(2)]
    ub = [P.sb(f"ub{k}", [128, TB], BF16) for k in range(2)]
    xn = [P.sb(f"xn{k}", [128, TOK], F32) for k in range(2)]

    emit_ln_streaming(P, x_ap, x_b, sA, psf, scr, xs, (o128_sb, o128_b), (o1_sb, o1_b),
                      post_cols=(suh, sub, list(range(16))))
    cur, nxt = sA, sB
    for st in range(4):
        h = 1 << st
        for a in range(ND):
            bpart = a ^ h
            ah, ab = cur[a]
            bh, bb = cur[bpart]
            oh, ob = nxt[a]
            op = ALU.add if (a & h) == 0 else ALU.subtract
            if op == ALU.add:
                P.emit("vector", lambda e, o=oh[:, :], p=ah[:, :], q=bh[:, :]: e.tensor_add(o, p, q),
                       reads=[ab, bb], writes=[ob])
            else:
                P.emit("vector", lambda e, o=oh[:, :], p=bh[:, :], q=ah[:, :]: e.tensor_sub(o, p, q),
                       reads=[ab, bb], writes=[ob])
        cur, nxt = nxt, cur
    for i in range(ND):
        ah, ab = cur[i]
        th, tb_ = nxt[i]
        for tb in range(2):
            sl = slice(tb * TB, (tb + 1) * TB)
            ph, pb = psf[4]
            P.mm(ph[:, :], pb, [(h_sb[:, :], ah[:, sl])], [h_b, ab])
            P.emit("scalar", lambda e, o=th[:, sl], i=ph[:, :]: e.copy(o, i),
                   reads=[pb], writes=[tb_])
    t_tiles = nxt
    wupr = wupT.rearrange("(i p) (a c) -> a p i c", p=128, c=128)
    wgr = wgT.rearrange("(i p) (a c) -> a p i c", p=128, c=128)
    wdfr = wdf.rearrange("(a p) (d c) -> d p a c", p=128, c=128)
    xr = x_ap.rearrange("(i p) t -> i p t", p=128)
    xor_ = xo_ap.rearrange("(i p) t -> i p t", p=128)
    for half in range(2):
        for a2 in range(32):
            a = half * 32 + a2
            wuh, wub = wu_s[a2 % 2]
            wgh2, wgb2 = wg_s[a2 % 2]
            P.dma(wuh[:, :], wupr[a], wub, src_buf=wupT_b, cast=True)
            P.dma(wgh2[:, :], wgr[a], wgb2, src_buf=wgT_b, cast=True)
            hh, hb = h2[a2]
            for tb in range(2):
                sl = slice(tb * TB, (tb + 1) * TB)
                pu, pub = psf[tb]
                pg, pgb = psf[2 + tb]
                pairs_u = [(wuh[:, i * 128:(i + 1) * 128], t_tiles[i][0][:, sl]) for i in range(ND)]
                pairs_g = [(wgh2[:, i * 128:(i + 1) * 128], t_tiles[i][0][:, sl]) for i in range(ND)]
                rd = [wub] + [t_tiles[i][1] for i in range(ND)]
                rdg = [wgb2] + [t_tiles[i][1] for i in range(ND)]
                P.mm(pu[:, :], pub, pairs_u, rd)
                P.mm(pg[:, :], pgb, pairs_g, rdg)
                gh, gb = ug_scr[tb]
                P.emit("scalar", lambda e, o=gh[:, :], i=pg[:, :], s=wgh[:, a:a + 1]:
                       e.activation(o, i, AF.Silu, scale=s),
                       reads=[pgb, wgb], writes=[gb])
                uh, ubf = ub[tb]
                P.emit("vector", lambda e, o=uh[:, :], i=gh[:, :], s=swh[:, a:a + 1], b=pu[:, :]:
                       e.scalar_tensor_tensor(o, i, s, b, ALU.mult, ALU.mult),
                       reads=[gb, swb, pub], writes=[ubf])
                ph2, pb2 = psf[4 + tb]
                P.mm(ph2[:, :], pb2, [(h_sb[:, :], uh[:, :])], [h_b, ubf])
                P.emit("scalar", lambda e, o=hh[:, sl], i=ph2[:, :]: e.copy(o, i),
                       reads=[pb2], writes=[hb])
        for d in range(ND):
            wdh2, wdb2 = wd_s[d % 2]
            P.dma(wdh2[:, :].rearrange("p (a c) -> p a c", c=128), wdfr[d][:, half * 32:(half + 1) * 32, :],
                  wdb2, src_buf=wdf_b)
            xh, xb = xs[d % 2]
            if half == 0:
                P.dma(xh[:, :], xr[d], xb, src_buf=x_b)
            else:
                P.dma(xh[:, :], xor_[d], xb, src_buf=xo_b)
            xnh, xnb = xn[d % 2]
            for tb in range(2):
                sl = slice(tb * TB, (tb + 1) * TB)
                pd, pdb = psf[2 + tb]
                pairs = [(wdh2[:, a2 * 128:(a2 + 1) * 128], h2[a2][0][:, sl]) for a2 in range(32)]
                rd = [wdb2] + [h2[a2][1] for a2 in range(32)]
                P.mm(pd[:, :], pdb, pairs, rd)
                P.emit("vector", lambda e, o=xnh[:, sl], i=pd[:, :], s=wdh[:, d:d + 1], b=xh[:, sl]:
                       e.scalar_tensor_tensor(o, i, s, b, ALU.mult, ALU.add),
                       reads=[pdb, wdb, xb], writes=[xnb])
            P.dma(xor_[d], xnh[:, :], xo_b, src_buf=xnb)


def emit_attn0(P, x_t, xo_t, W):
    x_ap, x_b = x_t
    xo_ap, xo_b = xo_t
    bT, bT_b = W["bT"]
    cT, cT_b = W["cT"]
    dvec, dvec_b = W["dvec"]
    ones1, ones1_b = W["ones1"]
    ones128, ones128_b = W["ones128"]

    psf, psb, scr = _alloc_common(P)
    o1_sb, o1_b = P.sb("o1", [1, 128], F32)
    P.dma(o1_sb[:, :], ones1[:, :], o1_b, src_buf=ones1_b)
    o128_sb, o128_b = P.sb("o128", [128, 1], F32)
    P.dma(o128_sb[:, :], ones128[:, :], o128_b, src_buf=ones128_b)
    dh, db = P.sb("dh", [128, 16], F32)
    P.dma(dh[:, :], dvec[:, :], db, src_buf=dvec_b)

    xt = [P.sb(f"x{i}", [128, TOK], F32) for i in range(ND)]
    yt = [P.sb(f"y{i}", [128, TOK], BF16) for i in range(ND)]
    st = [P.sb(f"st{i}", [128, TOK], BF16) for i in range(8)]
    wb_s = [P.sb(f"wb{k}", [128, D], BF16) for k in range(2)]
    wc_s = [P.sb(f"wc{k}", [128, 1024], BF16) for k in range(2)]
    xn = [P.sb(f"xn{k}", [128, TOK], F32) for k in range(2)]

    xr = x_ap.rearrange("(i p) t -> i p t", p=128)
    xor_ = xo_ap.rearrange("(i p) t -> i p t", p=128)
    for i in range(ND):
        P.dma(xt[i][0][:, :], xr[i], xt[i][1], src_buf=x_b)
    emit_ln_resident(P, xt, yt, psf, scr, (o128_sb, o128_b), (o1_sb, o1_b))
    bTr = bT.rearrange("(i p) (n c) -> n p i c", p=128, c=128)
    for n in range(8):
        wh, wb = wb_s[n % 2]
        P.dma(wh[:, :], bTr[n], wb, src_buf=bT_b, cast=True)
        for tb in range(2):
            sl = slice(tb * TB, (tb + 1) * TB)
            ph, pb = psf[tb]
            pairs = [(wh[:, i * 128:(i + 1) * 128], yt[i][0][:, sl]) for i in range(ND)]
            P.mm(ph[:, :], pb, pairs, [wb] + [yt[i][1] for i in range(ND)])
            P.emit("scalar", lambda e, o=st[n][0][:, sl], i=ph[:, :]: e.activation(o, i, AF.Tanh),
                   reads=[pb], writes=[st[n][1]])
    cTr = cT.rearrange("(n p) (d c) -> d p n c", p=128, c=128)
    for d in range(ND):
        wh, wb = wc_s[d % 2]
        P.dma(wh[:, :], cTr[d], wb, src_buf=cT_b, cast=True)
        xnh, xnb = xn[d % 2]
        for tb in range(2):
            sl = slice(tb * TB, (tb + 1) * TB)
            ph, pb = psf[2 + tb]
            pairs = [(wh[:, n * 128:(n + 1) * 128], st[n][0][:, sl]) for n in range(8)]
            P.mm(ph[:, :], pb, pairs, [wb] + [st[n][1] for n in range(8)])
            sh, sb_ = scr[tb]
            P.emit("vector", lambda e, o=sh[:, :], i=yt[d][0][:, sl], s=dh[:, d:d + 1], b=ph[:, :]:
                   e.scalar_tensor_tensor(o, i, s, b, ALU.mult, ALU.add),
                   reads=[yt[d][1], db, pb], writes=[sb_])
            P.emit("vector", lambda e, o=xnh[:, sl], a=xt[d][0][:, sl], b=sh[:, :]: e.tensor_add(o, a, b),
                   reads=[xt[d][1], sb_], writes=[xnb])
        P.dma(xor_[d], xnh[:, :], xo_b, src_buf=xnb)


def emit_moe(P, x_t, xo_t, W, final=False):
    x_ap, x_b = x_t
    xo_ap, xo_b = xo_t
    wr, wr_b = W["wr"]
    br, br_b = W["br"]
    wds, wds_b = W["wds"]
    bds, bds_b = W["bds"]
    wus, wus_b = W["wus"]
    bu, bu_b = W["bu"]
    e16, e16_b = W["e16"]
    iden, iden_b = W["iden"]
    ones1, ones1_b = W["ones1"]
    ones128, ones128_b = W["ones128"]

    psf, psb, scr = _alloc_common(P)
    o1_sb, o1_b = P.sb("o1", [1, 128], F32)
    P.dma(o1_sb[:, :], ones1[:, :], o1_b, src_buf=ones1_b)
    o128_sb, o128_b = P.sb("o128", [128, 1], F32)
    P.dma(o128_sb[:, :], ones128[:, :], o128_b, src_buf=ones128_b)
    idf, idf_b = P.sb("idf", [128, 128], F32)
    P.dma(idf[:, :], iden[:, :], idf_b, src_buf=iden_b)
    idb, idb_b = P.sb("idb", [128, 128], BF16)
    P.dma(idb[:, :], iden[:, :], idb_b, src_buf=iden_b, cast=True)
    e16s, e16sb = P.sb("e16s", [8, 128], BF16)
    P.dma(e16s[:, :], e16[:, :], e16sb, src_buf=e16_b, cast=True)
    wr32, wr32b = P.sb("wr32", [128, 128], F32)
    P.dma(wr32[:, :], wr.rearrange("(i p) e -> p i e", p=128), wr32b, src_buf=wr_b)
    brs, brsb = P.sb("brs", [8, 1], F32)
    P.dma(brs[:, :], br[:, :], brsb, src_buf=br_b)
    wdss, wdssb = P.sb("wdss", [128, D], BF16)
    P.dma(wdss[:, :], wds.rearrange("(i p) c -> p i c", p=128), wdssb, src_buf=wds_b, cast=True)
    bdss, bdssb = P.sb("bdss", [128, 1], F32)
    P.dma(bdss[:, :], bds[:, :], bdssb, src_buf=bds_b)
    wuss, wussb = P.sb("wuss", [128, D], BF16)
    P.dma(wuss[:, :], wus[:, :], wussb, src_buf=wus_b, cast=True)
    bus, busb = P.sb("bus", [8, D], BF16)
    P.dma(bus[:, :], bu[:, :], busb, src_buf=bu_b, cast=True)

    xt = [P.sb(f"x{i}", [128, TOK], F32) for i in range(ND)]
    yt = [P.sb(f"y{i}", [128, TOK], BF16) for i in range(ND)]
    xr = x_ap.rearrange("(i p) t -> i p t", p=128)
    xor_ = xo_ap.rearrange("(i p) t -> i p t", p=128)
    for i in range(ND):
        P.dma(xt[i][0][:, :], xr[i], xt[i][1], src_buf=x_b)
    m_sb, m_b, r_sb, r_b = emit_ln_resident(P, xt, yt, psf, scr, (o128_sb, o128_b), (o1_sb, o1_b))

    lg, lg_b = P.sb("lg", [8, TOK], F32)
    swr, swr_b = P.sb("swr", [8, 1], F32)
    s8a, s8a_b = P.sb("s8a", [8, TB], F32)
    s8b, s8b_b = P.sb("s8b", [8, TB], F32)
    ph, pb = psf[3]
    pairs = [(wr32[:, i * 8:(i + 1) * 8], o128_sb[:, :]) for i in range(ND)]
    P.mm(ph[0:8, 0:1], pb, pairs, [wr32b, o128_b])
    P.emit("scalar", lambda e, o=swr[:, :], i=ph[0:8, 0:1]: e.copy(o, i), reads=[pb], writes=[swr_b])
    for tb in range(2):
        sl = slice(tb * TB, (tb + 1) * TB)
        ph0, pb0 = psf[0]
        pairs = [(wr32[:, i * 8:(i + 1) * 8], xt[i][0][:, sl]) for i in range(ND)]
        P.mm(ph0[0:8, :], pb0, pairs, [wr32b] + [xt[i][1] for i in range(ND)])
        ph1, pb1 = psf[1]
        P.mm(ph1[0:8, :], pb1, [(o1_sb[0:1, 0:8], m_sb[0:1, sl])], [o1_b, m_b])
        ph2, pb2 = psf[2]
        P.mm(ph2[0:8, :], pb2, [(o1_sb[0:1, 0:8], r_sb[0:1, sl])], [o1_b, r_b])
        P.emit("vector", lambda e, o=s8a[:, :], i=ph1[0:8, :], sc=swr[:, :]: e.tensor_scalar_mul(o, i, sc),
               reads=[pb1, swr_b], writes=[s8a_b])
        P.emit("vector", lambda e, o=s8b[:, :], a=ph0[0:8, :], b=s8a[:, :]: e.tensor_sub(o, a, b),
               reads=[pb0, s8a_b], writes=[s8b_b])
        P.emit("vector", lambda e, o=lg[:, sl], a=s8b[:, :], b=ph2[0:8, :]: e.tensor_mul(o, a, b),
               reads=[s8b_b, pb2], writes=[lg_b])
        P.emit("vector", lambda e, o=lg[:, sl], i=lg[:, sl], s=brs[:, :]: e.tensor_scalar_add(o, i, s),
               reads=[lg_b, brsb], writes=[lg_b])
    lt, lt_b = P.sb("lt", [128, 8], F32)
    m1, m1_b = P.sb("m1", [128, 1], F32)
    m2, m2_b = P.sb("m2", [128, 1], F32)
    msk, msk_b = P.sb("msk", [128, 8], F32)
    gts, gts_b = P.sb("gts", [128, 8], F32)
    w8, w8_b = P.sb("w8", [128, 8], BF16)
    wT, wT_b = P.sb("wT", [8, TOK], BF16)
    for blk in range(8):
        bs = slice(blk * 128, (blk + 1) * 128)
        ph, pb = psf[1]
        P.tr(ph[0:128, 0:8], pb, lg[0:8, bs], lg_b, idf[0:8, 0:8], idf_b)
        P.emit("scalar", lambda e, o=lt[:, :], i=ph[0:128, 0:8]: e.copy(o, i), reads=[pb], writes=[lt_b])
        P.emit("vector", lambda e, o=m1[:, :], i=lt[:, :]: e.reduce_max(o, i, AX.X), reads=[lt_b], writes=[m1_b])
        P.emit("vector", lambda e, o=msk[:, :], i=lt[:, :], s=m1[:, :]:
               e.tensor_scalar(o, i, s, None, ALU.is_ge), reads=[lt_b, m1_b], writes=[msk_b])
        P.emit("vector", lambda e, o=msk[:, :], i=msk[:, :], b=lt[:, :]:
               e.scalar_tensor_tensor(o, i, -1e9, b, ALU.mult, ALU.add), reads=[msk_b, lt_b], writes=[msk_b])
        P.emit("vector", lambda e, o=m2[:, :], i=msk[:, :]: e.reduce_max(o, i, AX.X), reads=[msk_b], writes=[m2_b])
        P.emit("vector", lambda e, o=msk[:, :], i=lt[:, :], s=m2[:, :]:
               e.tensor_scalar(o, i, s, None, ALU.is_ge), reads=[lt_b, m2_b], writes=[msk_b])
        P.emit("vector", lambda e, o=m1[:, :], i=m1[:, :]: e.tensor_scalar_mul(o, i, -1.0), reads=[m1_b], writes=[m1_b])
        P.emit("scalar", lambda e, o=gts[:, :], i=lt[:, :], b=m1[:, :]: e.activation(o, i, AF.Exp, bias=b),
               reads=[lt_b, m1_b], writes=[gts_b])
        P.emit("vector", lambda e, o=gts[:, :], a=gts[:, :], b=msk[:, :]: e.tensor_mul(o, a, b),
               reads=[gts_b, msk_b], writes=[gts_b])
        P.emit("vector", lambda e, o=m2[:, :], i=gts[:, :]: e.reduce_sum(o, i, AX.X), reads=[gts_b], writes=[m2_b])
        P.emit("vector", lambda e, o=m2[:, :], i=m2[:, :]: e.reciprocal(o, i), reads=[m2_b], writes=[m2_b])
        P.emit("vector", lambda e, o=w8[:, :], i=gts[:, :], s=m2[:, :]: e.tensor_scalar_mul(o, i, s),
               reads=[gts_b, m2_b], writes=[w8_b])
        ph2, pb2 = psb[0]
        P.tr(ph2[0:8, 0:128], pb2, w8[:, :], w8_b, idb[:, :], idb_b)
        P.emit("scalar", lambda e, o=wT[:, bs], i=ph2[0:8, 0:128]: e.copy(o, i), reads=[pb2], writes=[wT_b])
    wS, wS_b = P.sb("wS", [128, TOK], BF16)
    hS, hS_b = P.sb("hS", [128, TOK], BF16)
    for tb in range(2):
        sl = slice(tb * TB, (tb + 1) * TB)
        ph, pb = psf[2]
        P.mm(ph[:, :], pb, [(e16s[:, :], wT[:, sl])], [e16sb, wT_b])
        P.emit("scalar", lambda e, o=wS[:, sl], i=ph[:, :]: e.copy(o, i), reads=[pb], writes=[wS_b])
        ph2, pb2 = psf[3]
        pairs = [(wdss[:, i * 128:(i + 1) * 128], yt[i][0][:, sl]) for i in range(ND)]
        P.mm(ph2[:, :], pb2, pairs, [wdssb] + [yt[i][1] for i in range(ND)])
        P.emit("scalar", lambda e, o=hS[:, sl], i=ph2[:, :], b=bdss[:, :]: e.activation(o, i, AF.Relu, bias=b),
               reads=[pb2, bdssb], writes=[hS_b])
        P.emit("vector", lambda e, o=hS[:, sl], a=hS[:, sl], b=wS[:, sl]: e.tensor_mul(o, a, b),
               reads=[hS_b, wS_b], writes=[hS_b])
    xn = [P.sb(f"xn{k}", [128, TOK], F32) for k in range(2)]
    for d in range(ND):
        xnh, xnb = xn[d % 2]
        for tb in range(2):
            sl = slice(tb * TB, (tb + 1) * TB)
            ph, pb = psf[4 + tb]
            pairs = [(wuss[:, d * 128:(d + 1) * 128], hS[:, sl]),
                     (bus[0:8, d * 128:(d + 1) * 128], wT[0:8, sl])]
            P.mm(ph[:, :], pb, pairs, [wussb, hS_b, busb, wT_b])
            P.emit("vector", lambda e, o=xnh[:, sl], i=ph[:, :], b=xt[d][0][:, sl]:
                   e.scalar_tensor_tensor(o, i, 0.5, b, ALU.mult, ALU.add),
                   reads=[pb, xt[d][1]], writes=[xnb])
        P.dma(xor_[d], xnh[:, :], xo_b, src_buf=xnb, out=final)


def emit_gdnz(P, x_t, W, xnin_t, xnf_t, zs_t):
    x_ap, x_b = x_t
    wz, wz_b = W["wz"]
    ones1, ones1_b = W["ones1"]
    ones128, ones128_b = W["ones128"]
    xnin_ap, xnin_b = xnin_t
    xnf_ap, xnf_b = xnf_t
    zs, zs_b = zs_t

    psf, psb, scr = _alloc_common(P)
    o1_sb, o1_b = P.sb("o1", [1, 128], F32)
    P.dma(o1_sb[:, :], ones1[:, :], o1_b, src_buf=ones1_b)
    o128_sb, o128_b = P.sb("o128", [128, 1], F32)
    P.dma(o128_sb[:, :], ones128[:, :], o128_b, src_buf=ones128_b)

    xt = [P.sb(f"x{i}", [128, TOK], F32) for i in range(ND)]
    yt = [P.sb(f"y{i}", [128, TOK], BF16) for i in range(ND)]
    xr = x_ap.rearrange("(i p) t -> i p t", p=128)
    for i in range(ND):
        P.dma(xt[i][0][:, :], xr[i], xt[i][1], src_buf=x_b)
    emit_ln_resident(P, xt, yt, psf, scr, (o128_sb, o128_b), (o1_sb, o1_b))

    xnr = xnin_ap.rearrange("(i p) t -> i p t", p=128)
    for i in range(ND):
        P.dma(xnr[i], yt[i][0][:, :], xnin_b, src_buf=yt[i][1])
    P.cc("AllGather", ALU.bypass, xnin_ap, xnin_b, xnf_ap, xnf_b, "cc0")

    wq_s = [P.sb(f"wq{k}", [128, D], BF16) for k in range(2)]
    mx = [P.sb(f"mx{k}", [128, TOK], BF16) for k in range(2)]
    wzr = wz.rearrange("(i p) (o c) -> o p i c", p=128, c=128)
    for o in range(32):
        wh, wb = wq_s[o % 2]
        P.dma(wh[:, :], wzr[o], wb, src_buf=wz_b, cast=True)
        mh, mb = mx[o % 2]
        for tb in range(2):
            sl = slice(tb * TB, (tb + 1) * TB)
            ph, pb = psf[(o % 2) * 2 + tb]
            pairs = [(wh[:, i * 128:(i + 1) * 128], yt[i][0][:, sl]) for i in range(ND)]
            P.mm(ph[:, :], pb, pairs, [wb] + [yt[i][1] for i in range(ND)])
            P.emit("scalar", lambda e, o_=mh[:, sl], i=ph[:, :]: e.activation(o_, i, AF.Silu),
                   reads=[pb], writes=[mb])
        P.dma(zs[o], mh[:, :], zs_b, src_buf=mb)


def emit_gdnpc(P, xnf_t, W, qin_t, kin_t, vin_t, beta_t, g_t):
    S = 2048
    xnf, xnf_b = xnf_t
    wqkvc, wqkvc_b = W["wqkvc"]
    wbac, wbac_b = W["wbac"]
    alogc, alogc_b = W["alogc"]
    dtbc, dtbc_b = W["dtbc"]
    qin, qin_b = qin_t
    kin, kin_b = kin_t
    vin, vin_b = vin_t
    beta_o, beta_ob = beta_t
    g_o, g_ob = g_t

    psf, psb, scr = _alloc_common(P)
    xnr = xnf.rearrange("(h i p) t -> h i p t", p=128, i=ND)
    xh = [[P.sb(f"xn{h}_{i}", [128, TOK], BF16) for i in range(ND)] for h in range(2)]
    for h in range(2):
        for i in range(ND):
            P.dma(xh[h][i][0][:, :], xnr[h][i], xh[h][i][1], src_buf=xnf_b)

    wq_s = [P.sb(f"wq{k}", [128, D], BF16) for k in range(2)]
    mx = [P.sb(f"mx{k}", [128, S], BF16) for k in range(2)]
    wqr = wqkvc.rearrange("(i p) (o c) -> o p i c", p=128, c=128)
    for o in range(32):
        wh, wb = wq_s[o % 2]
        P.dma(wh[:, :], wqr[o], wb, src_buf=wqkvc_b, cast=True)
        mh, mb = mx[o % 2]
        for h in range(2):
            for tb in range(2):
                sl = slice(tb * TB, (tb + 1) * TB)
                ph, pb = psf[2 * h + tb]
                pairs = [(wh[:, i * 128:(i + 1) * 128], xh[h][i][0][:, sl]) for i in range(ND)]
                P.mm(ph[:, :], pb, pairs, [wb] + [xh[h][i][1] for i in range(ND)])
                P.emit("scalar",
                       lambda e, o_=mh[:, h * TOK + tb * TB:h * TOK + (tb + 1) * TB], i=ph[:, :]: e.copy(o_, i),
                       reads=[pb], writes=[mb])
        if o < 8:
            dst, dstb = qin[o], qin_b
        elif o < 16:
            dst, dstb = kin[o - 8], kin_b
        else:
            dst, dstb = vin[o - 16], vin_b
        P.dma(dst, mh[:, :], dstb, src_buf=mb)
    wbas, wbasb = P.sb("wbas", [128, 512], BF16)
    P.dma(wbas[:, :], wbac.rearrange("(i p) e -> p i e", p=128), wbasb, src_buf=wbac_b, cast=True)
    blog, blog_b = P.sb("blog", [16, S], F32)
    alg, alg_b = P.sb("alg", [16, S], F32)
    for h in range(2):
        for tb in range(2):
            sl = slice(tb * TB, (tb + 1) * TB)
            cs = slice(h * TOK + tb * TB, h * TOK + (tb + 1) * TB)
            ph, pb = psf[4]
            pairs = [(wbas[:, i * 32:i * 32 + 16], xh[h][i][0][:, sl]) for i in range(ND)]
            P.mm(ph[0:16, :], pb, pairs, [wbasb] + [xh[h][i][1] for i in range(ND)])
            P.emit("scalar", lambda e, o=blog[:, cs], i=ph[0:16, :]: e.copy(o, i),
                   reads=[pb], writes=[blog_b])
            ph2, pb2 = psf[5]
            pairs = [(wbas[:, i * 32 + 16:i * 32 + 32], xh[h][i][0][:, sl]) for i in range(ND)]
            P.mm(ph2[0:16, :], pb2, pairs, [wbasb] + [xh[h][i][1] for i in range(ND)])
            P.emit("scalar", lambda e, o=alg[:, cs], i=ph2[0:16, :]: e.copy(o, i),
                   reads=[pb2], writes=[alg_b])
    bt, bt_b = P.sb("bt", [16, S], F32)
    P.emit("scalar", lambda e, o=bt[:, :], i=blog[:, :]: e.activation(o, i, AF.Sigmoid),
           reads=[blog_b], writes=[bt_b])
    P.dma(beta_o[:, :], bt[:, :], beta_ob, src_buf=bt_b)
    ad, ad_b = P.sb("ad", [16, S], F32)
    dtbs, dtbsb = P.sb("dtbs", [16, 1], F32)
    P.dma(dtbs[:, :], dtbc[:, :], dtbsb, src_buf=dtbc_b)
    P.emit("vector", lambda e, o=ad[:, :], i=alg[:, :], s=dtbs[:, :]: e.tensor_scalar_add(o, i, s),
           reads=[alg_b, dtbsb], writes=[ad_b])
    t1, t1_b = P.sb("spt1", [16, S], F32)
    t2, t2_b = P.sb("spt2", [16, S], F32)
    P.emit("scalar", lambda e, o=t1[:, :], i=ad[:, :]: e.activation(o, i, AF.Abs),
           reads=[ad_b], writes=[t1_b])
    P.emit("scalar", lambda e, o=t1[:, :], i=t1[:, :]: e.activation(o, i, AF.Exp, scale=-1.0),
           reads=[t1_b], writes=[t1_b])
    P.emit("vector", lambda e, o=t1[:, :], i=t1[:, :]: e.tensor_scalar_add(o, i, 1.0),
           reads=[t1_b], writes=[t1_b])
    P.emit("scalar", lambda e, o=t1[:, :], i=t1[:, :]: e.activation(o, i, AF.Ln),
           reads=[t1_b], writes=[t1_b])
    P.emit("vector", lambda e, o=t2[:, :], i=ad[:, :]: e.tensor_scalar_max(o, i, 0.0),
           reads=[ad_b], writes=[t2_b])
    P.emit("vector", lambda e, o=t2[:, :], a=t2[:, :], b=t1[:, :]: e.tensor_add(o, a, b),
           reads=[t2_b, t1_b], writes=[t2_b])
    nea, nea_b = P.sb("nea", [16, 1], F32)
    P.dma(nea[:, :], alogc[:, :], nea_b, src_buf=alogc_b)
    P.emit("scalar", lambda e, o=nea[:, :], i=nea[:, :]: e.activation(o, i, AF.Exp), reads=[nea_b], writes=[nea_b])
    P.emit("vector", lambda e, o=nea[:, :], i=nea[:, :]: e.tensor_scalar_mul(o, i, -1.0), reads=[nea_b], writes=[nea_b])
    P.emit("vector", lambda e, o=t2[:, :], i=t2[:, :], s=nea[:, :]: e.tensor_scalar_mul(o, i, s),
           reads=[t2_b, nea_b], writes=[t2_b])
    for h in range(2):
        P.emit("vector", lambda e, o=t1[:, h * TOK:(h + 1) * TOK], a=t2[:, h * TOK:(h + 1) * TOK],
               b=t2[:, h * TOK:(h + 1) * TOK]:
               e.tensor_tensor_scan(o, a, b, 0.0, ALU.add, ALU.bypass),
               reads=[t2_b], writes=[t1_b])
    P.dma(g_o[:, :], t1[:, :], g_ob, src_buf=t1_b)


def emit_gdnchunk(P, W, qin_t, kin_t, vin_t, beta_t, g_t, rsin_t):
    S = 2048
    NCH = S // CC
    qin, qin_b = qin_t
    kin, kin_b = kin_t
    vin, vin_b = vin_t
    beta_i, beta_ib = beta_t
    g_i, g_ib = g_t
    rsin_ap, rsin_b = rsin_t
    cwq, cwq_b = W["cwq"]
    cwk, cwk_b = W["cwk"]
    cwv, cwv_b = W["cwv"]
    monorm, monorm_b = W["monorm"]
    nw_i, nw_ib = W["nw"]
    maskw, maskw_b = W["maskw"]
    maskat, maskat_b = W["maskat"]
    iden, iden_b = W["iden"]
    ones1, ones1_b = W["ones1"]
    ones128, ones128_b = W["ones128"]

    psf = [P.ps(f"psf{k}", [128, 512], F32) for k in range(5)]
    psb = [P.ps(f"psb{k}", [128, 1024], BF16) for k in range(3)]
    mk_sb, mk_b = P.sb("mks", [128, 2], F32)
    P.dma(mk_sb[:, :], monorm[:, :], mk_b, src_buf=monorm_b)
    mscr = [P.sb("mscr0", [128, S], BF16)]
    rsv = rsin_ap.rearrange("(j r) p t -> r p j t", j=2)
    o1_sb, o1_b = P.sb("o1", [1, 128], F32)
    P.dma(o1_sb[:, :], ones1[:, :], o1_b, src_buf=ones1_b)
    o128_sb, o128_b = P.sb("o128", [128, 1], F32)
    P.dma(o128_sb[:, :], ones128[:, :], o128_b, src_buf=ones128_b)
    idf, idf_b = P.sb("idf", [128, 128], F32)
    P.dma(idf[:, :], iden[:, :], idf_b, src_buf=iden_b)
    idb, idb_b = P.sb("idb", [128, 128], BF16)
    P.dma(idb[:, :], iden[:, :], idb_b, src_buf=iden_b, cast=True)
    mw_sb, mw_b = P.sb("mw", [128, 128], F32)
    P.dma(mw_sb[:, :], maskw[:, :], mw_b, src_buf=maskw_b)
    ma_sb, ma_b = P.sb("ma", [128, 128], F32)
    P.dma(ma_sb[:, :], maskat[:, :], ma_b, src_buf=maskat_b)
    bsb, bsb_b = P.sb("bsb", [16, S], F32)
    P.dma(bsb[:, :], beta_i[:, :], bsb_b, src_buf=beta_ib)
    gsb, gsb_b = P.sb("gsb", [16, S], F32)
    P.dma(gsb[:, :], g_i[:, :], gsb_b, src_buf=g_ib)
    nws, nws_b = P.sb("nws", [1, 128], F32)
    P.dma(nws[:, :], nw_i[:, :], nws_b, src_buf=nw_ib)
    nwb, nwb_b = P.sb("nwb", [128, 128], F32)
    ph, pb = psf[0]
    P.mm(ph[:, 0:128], pb, [(o1_sb[:, :], nws[:, :])], [o1_b, nws_b])
    P.emit("scalar", lambda e, o=nwb[:, :], i=ph[:, 0:128]: e.copy(o, i), reads=[pb], writes=[nwb_b])

    cwq_s, cwq_sb = P.sb("cwqs", [128, 32], F32)
    P.dma(cwq_s[:, :].rearrange("p (i j) -> p i j", j=4), cwq.rearrange("i p j -> p i j"), cwq_sb, src_buf=cwq_b)
    cwk_s, cwk_sb = P.sb("cwks", [128, 32], F32)
    P.dma(cwk_s[:, :].rearrange("p (i j) -> p i j", j=4), cwk.rearrange("i p j -> p i j"), cwk_sb, src_buf=cwk_b)
    cwv_s, cwv_sb = P.sb("cwvs", [128, 64], F32)
    P.dma(cwv_s[:, :].rearrange("p (i j) -> p i j", j=4), cwv.rearrange("i p j -> p i j"), cwv_sb, src_buf=cwv_b)
    pad = [P.sb(f"pad{k}", [128, S + 3], BF16) for k in range(2)]
    acc = [P.sb(f"acc{k}", [128, 512], F32) for k in range(2)]
    sqs, sqs_b = P.sb("sqs", [128, 512], F32)
    rsr, rsr_b = P.sb("rsr", [1, S], F32)
    qt = [P.sb(f"qt{i}", [128, S], BF16) for i in range(8)]
    kt = [P.sb(f"kt{i}", [128, S], BF16) for i in range(8)]
    vt = [P.sb(f"vt{i}", [128, S], BF16) for i in range(16)]

    def conv_tile(dst, dst_b, src_ap, src_b, cw_ap, cw_b, idx, norm_scale):
        ph_, pb_ = pad[idx % 2]
        P.emit("vector", lambda e, o=ph_[:, 0:3]: e.memset(o, 0.0), writes=[pb_])
        P.dma(ph_[:, 3:S + 3], src_ap, pb_, src_buf=src_b)
        for t4 in range(4):
            off = t4 * 512
            sl = slice(off, off + 512)
            ah, ab = acc[t4 % 2]
            P.emit("vector", lambda e, o=ah[:, :], i=ph_[:, off:off + 512], s=cw_ap[:, 0:1]:
                   e.tensor_scalar_mul(o, i, s),
                   reads=[pb_, cw_b], writes=[ab])
            for j in range(1, 4):
                P.emit("vector", lambda e, o=ah[:, :], i=ph_[:, off + j:off + j + 512], s=cw_ap[:, j:j + 1], b=ah[:, :]:
                       e.scalar_tensor_tensor(o, i, s, b, ALU.mult, ALU.add),
                       reads=[pb_, cw_b, ab], writes=[ab])
            if norm_scale is None:
                P.emit("scalar", lambda e, o=dst[:, sl], i=ah[:, :]: e.activation(o, i, AF.Silu),
                       reads=[ab], writes=[dst_b])
                continue
            P.emit("scalar", lambda e, o=ah[:, :], i=ah[:, :]: e.activation(o, i, AF.Silu),
                   reads=[ab], writes=[ab])
            P.emit("vector", lambda e, o=sqs[:, :], a=ah[:, :]: e.tensor_mul(o, a, a), reads=[ab], writes=[sqs_b])
            ph2, pb2 = psf[t4 % 2]
            P.mm(ph2[0:1, :], pb2, [(o128_sb[:, :], sqs[:, :])], [o128_b, sqs_b])
            P.emit("vector", lambda e, o=rsr[0:1, sl], i=ph2[0:1, :]: e.tensor_scalar_add(o, i, 1e-6),
                   reads=[pb2], writes=[rsr_b])
            P.emit("scalar", lambda e, o=rsr[0:1, sl], i=rsr[0:1, sl]: e.activation(o, i, AF.Ln),
                   reads=[rsr_b], writes=[rsr_b])
            P.emit("scalar", lambda e, o=rsr[0:1, sl], i=rsr[0:1, sl]: e.activation(o, i, AF.Exp, scale=-0.5),
                   reads=[rsr_b], writes=[rsr_b])
            ph3, pb3 = psf[2 + t4 % 2]
            P.mm(ph3[:, :], pb3, [(o1_sb[:, :], rsr[0:1, sl])], [o1_b, rsr_b])
            P.emit("vector", lambda e, o=dst[:, sl], i=ah[:, :], b=ph3[:, :]:
                   e.scalar_tensor_tensor(o, i, norm_scale, b, ALU.mult, ALU.mult),
                   reads=[ab, pb3], writes=[dst_b])

    for i in range(8):
        conv_tile(qt[i][0], qt[i][1], qin[i], qin_b, cwq_s[:, i * 4:(i + 1) * 4], cwq_sb, i, float(DK ** -0.5))
    for i in range(8):
        conv_tile(kt[i][0], kt[i][1], kin[i], kin_b, cwk_s[:, i * 4:(i + 1) * 4], cwk_sb, i, 1.0)
    for i in range(16):
        conv_tile(vt[i][0], vt[i][1], vin[i], vin_b, cwv_s[:, i * 4:(i + 1) * 4], cwv_sb, i, None)

    gbt = [P.sb(f"gbt{c}", [128, 32], F32) for c in range(NCH)]
    npeb = [P.sb(f"npeb{c}", [128, 16], F32) for c in range(NCH)]
    ebt = [P.sb(f"ebt{c}", [128, 16], F32) for c in range(NCH)]
    acb = [P.sb(f"acb{c}", [128, 16], F32) for c in range(NCH)]
    rowt, rowt_b = P.sb("rowt", [1, 16], F32)
    for c in range(NCH):
        cs = slice(c * CC, (c + 1) * CC)
        gh, gb = gbt[c]
        ph1, pb1 = psf[0]
        P.tr(ph1[0:128, 0:16], pb1, gsb[0:16, cs], gsb_b, idf[0:16, 0:16], idf_b)
        P.emit("scalar", lambda e, o=gh[:, 0:16], i=ph1[0:128, 0:16]: e.copy(o, i), reads=[pb1], writes=[gb])
        ph2, pb2 = psf[1]
        P.tr(ph2[0:128, 0:16], pb2, bsb[0:16, cs], bsb_b, idf[0:16, 0:16], idf_b)
        P.emit("scalar", lambda e, o=gh[:, 16:32], i=ph2[0:128, 0:16]: e.copy(o, i), reads=[pb2], writes=[gb])
        nh, nb = npeb[c]
        if c % (NCH // 2) == 0:
            P.emit("vector", lambda e, o=nh[:, :]: e.memset(o, 0.0), writes=[nb])
        else:
            ph3, pb3 = psf[2]
            P.tr(ph3[0:1, 0:16], pb3, gsb[0:16, c * CC - 1:c * CC], gsb_b, idf[0:16, 0:16], idf_b)
            P.emit("scalar", lambda e, o=rowt[:, :], i=ph3[0:1, 0:16]: e.copy(o, i), reads=[pb3], writes=[rowt_b])
            ph4, pb4 = psf[3]
            P.mm(ph4[:, 0:16], pb4, [(o1_sb[:, :], rowt[:, :])], [o1_b, rowt_b])
            P.emit("scalar", lambda e, o=nh[:, :], i=ph4[:, 0:16]: e.mul(o, i, -1.0), reads=[pb4], writes=[nb])
        eh, eb = ebt[c]
        ph3, pb3 = psf[2]
        P.tr(ph3[0:1, 0:16], pb3, gsb[0:16, (c + 1) * CC - 1:(c + 1) * CC], gsb_b, idf[0:16, 0:16], idf_b)
        P.emit("scalar", lambda e, o=rowt[:, :], i=ph3[0:1, 0:16]: e.copy(o, i), reads=[pb3], writes=[rowt_b])
        ph4, pb4 = psf[3]
        P.mm(ph4[:, 0:16], pb4, [(o1_sb[:, :], rowt[:, :])], [o1_b, rowt_b])
        P.emit("scalar", lambda e, o=eh[:, :], i=ph4[:, 0:16]: e.copy(o, i), reads=[pb4], writes=[eb])
        ah2, ab2 = acb[c]
        P.emit("vector", lambda e, o=ah2[:, :], a=eh[:, :], b=nh[:, :]: e.tensor_add(o, a, b),
               reads=[eb, nb], writes=[ab2])
        P.emit("scalar", lambda e, o=ah2[:, :], i=ah2[:, :]: e.activation(o, i, AF.Exp), reads=[ab2], writes=[ab2])

    kkp = [P.sb(f"kk{k}", [128, 128], BF16) for k in range(2)]
    pqkp = [P.sb(f"pqk{k}", [128, 128], BF16) for k in range(2)]
    knp = [P.sb(f"kn{k}", [128, 128], BF16) for k in range(2)]
    SC = []
    for s_ in range(2):
        d = dict(
            cgrow=P.sb(f"cgrow{s_}", [1, 128], F32),
            cgb=P.sb(f"cgb{s_}", [128, 128], F32),
            t1w=P.sb(f"t1w{s_}", [128, 128], F32),
            ew=P.sb(f"ew{s_}", [128, 128], F32),
            vn=P.sb(f"vn{s_}", [128, 128], BF16),
            w=P.sb(f"wm{s_}", [128, 128], BF16),
            wt=P.sb(f"wtm{s_}", [128, 128], BF16),
            pw=[P.sb(f"pw{s_}_{k}", [128, 128], BF16) for k in range(2)],
            pts=[P.sb(f"pts{s_}_{k}", [128, 128], BF16) for k in range(7)],
            at=P.sb(f"atm{s_}", [128, 128], BF16),
            rr=P.sb(f"rr{s_}", [128, 256], BF16),
            uu=[P.sb(f"uu{s_}_{k}", [128, 256], BF16) for k in range(2)],
            ukt=P.sb(f"ukt{s_}", [128, 128], BF16),
            tc=P.sb(f"tc{s_}", [128, 128], BF16),
            alc=P.sb(f"alc{s_}", [128, 1], F32),
            abc=P.sb(f"abc{s_}", [128, 1], F32),
            ksc=P.sb(f"ksc{s_}", [128, 1], F32),
            ktl=P.sb(f"ktl{s_}", [128, 128], BF16),
            of=P.sb(f"of{s_}", [128, 128], F32),
            on=P.sb(f"on{s_}", [128, 128], BF16),
            ovar=P.sb(f"ovar{s_}", [128, 1], F32),
        )
        d["fA"] = psf[2 * s_]
        d["fB"] = psf[2 * s_ + 1]
        d["bT"] = psb[s_]
        SC.append(d)
    pre_f0 = psf[4]
    pre_f1 = psf[4]
    pre_b0 = psb[2]
    st32 = [P.sb(f"st32_{k}", [128, 128], F32) for k in range(2)]
    stbf = [P.sb(f"stbf_{k}", [128, 128], BF16) for k in range(2)]
    ohead = [P.sb(f"oh{k}", [128, S], BF16) for k in range(2)]

    def sub_ops(kh, c, sub):
        ops = []

        def E(*a, **k):
            ops.append(lambda a=a, k=k: P.emit(*a, **k))

        def M(*a, **k):
            ops.append(lambda a=a, k=k: P.mm(*a, **k))

        def T(*a):
            ops.append(lambda a=a: P.tr(*a))

        S_ = SC[sub]
        cs = slice(c * CC, (c + 1) * CC)
        h = 2 * kh + sub
        qh, qb = qt[kh]
        vh_, vb_ = vt[h]
        kkh, kkb = kkp[c % 2]
        pqh, pqb = pqkp[c % 2]
        knh, knb = knp[c % 2]
        sh32, sb32 = st32[sub]
        shbf, sbbf = stbf[sub]
        cgrow, cgrow_b = S_["cgrow"]
        cgb, cgb_b = S_["cgb"]
        t1w, t1w_b = S_["t1w"]
        ew, ew_b = S_["ew"]
        vn_sb, vn_b = S_["vn"]
        w_sb, w_b = S_["w"]
        wt_sb, wt_b = S_["wt"]
        pw = S_["pw"]
        pts = S_["pts"]
        at_sb, at_b = S_["at"]
        rr, rr_b = S_["rr"]
        uu = S_["uu"]
        ukt_sb, ukt_b = S_["ukt"]
        tc_sb, tc_b = S_["tc"]
        alc, alc_b = S_["alc"]
        abc, abc_b = S_["abc"]
        ksc, ksc_b = S_["ksc"]
        ktl, ktl_b = S_["ktl"]
        of_sb, of_b = S_["of"]
        on_sb, on_b = S_["on"]
        ovar, ovar_b = S_["ovar"]
        fA = S_["fA"]
        fB = S_["fB"]
        bT_ = S_["bT"]
        gcol = gbt[c][0][:, h:h + 1]
        gcol_b = gbt[c][1]
        bcol = gbt[c][0][:, 16 + h:16 + h + 1]
        ph3, pb3 = fA
        T(ph3[0:1, 0:128], pb3, gcol, gcol_b, idf[:, :], idf_b)
        E("scalar", lambda e, o=cgrow[:, :], i=ph3[0:1, 0:128]: e.copy(o, i), reads=[pb3], writes=[cgrow_b])
        ph4, pb4 = fB
        M(ph4[:, 0:128], pb4, [(o1_sb[:, :], cgrow[:, :])], [o1_b, cgrow_b])
        E("scalar", lambda e, o=cgb[:, :], i=ph4[:, 0:128]: e.copy(o, i), reads=[pb4], writes=[cgb_b])
        E("vector", lambda e, o=t1w[:, :], i=cgb[:, :], s=gcol, b=mw_sb[:, :]:
          e.scalar_tensor_tensor(o, i, s, b, ALU.subtract, ALU.add),
          reads=[cgb_b, gcol_b, mw_b], writes=[t1w_b])
        E("scalar", lambda e, o=ew[:, :], i=t1w[:, :]: e.activation(o, i, AF.Exp, scale=-1.0),
          reads=[t1w_b], writes=[ew_b])
        E("vector", lambda e, o=w_sb[:, :], i=ew[:, :], s=bcol, b=kkh[:, :]:
          e.scalar_tensor_tensor(o, i, s, b, ALU.mult, ALU.mult),
          reads=[ew_b, gcol_b, kkb], writes=[w_b])
        pbh, pbb = bT_
        T(pbh[:, 0:128], pbb, w_sb[:, :], w_b, idb[:, :], idb_b)
        E("scalar", lambda e, o=wt_sb[:, :], i=pbh[:, 0:128]: e.copy(o, i), reads=[pbb], writes=[wt_b])
        E("vector", lambda e, o=t1w[:, :], i=cgb[:, :], s=gcol, b=ma_sb[:, :]:
          e.scalar_tensor_tensor(o, i, s, b, ALU.subtract, ALU.subtract),
          reads=[cgb_b, gcol_b, ma_b], writes=[t1w_b])
        E("scalar", lambda e, o=ew[:, :], i=t1w[:, :]: e.activation(o, i, AF.Exp),
          reads=[t1w_b], writes=[ew_b])
        E("vector", lambda e, o=at_sb[:, :], a=ew[:, :], b=pqh[:, :]: e.tensor_mul(o, a, b),
          reads=[ew_b, pqb], writes=[at_b])
        E("scalar", lambda e, o=alc[:, :], i=gcol, b=npeb[c][0][:, h:h + 1]:
          e.activation(o, i, AF.Exp, bias=b),
          reads=[gcol_b, npeb[c][1]], writes=[alc_b])
        E("vector", lambda e, o=abc[:, :], a=alc[:, :], b=bcol: e.tensor_mul(o, a, b),
          reads=[alc_b, gcol_b], writes=[abc_b])
        E("scalar", lambda e, o=ksc[:, :], i=gcol, b=ebt[c][0][:, h:h + 1]:
          e.activation(o, i, AF.Exp, scale=-1.0, bias=b),
          reads=[gcol_b, ebt[c][1]], writes=[ksc_b])
        pbh2, pbb2 = bT_
        T(pbh2[:, 0:128], pbb2, vh_[:, cs], vb_, idb[:, :], idb_b)
        E("scalar", lambda e, o=vn_sb[:, :], i=pbh2[:, 0:128]: e.copy(o, i), reads=[pbb2], writes=[vn_b])
        E("vector", lambda e, o=rr[:, 0:128], i=vn_sb[:, :], s=bcol: e.tensor_scalar_mul(o, i, s),
          reads=[vn_b, gcol_b], writes=[rr_b])
        E("vector", lambda e, o=rr[:, 128:256], i=knh[:, :], s=abc[:, :]: e.tensor_scalar_mul(o, i, s),
          reads=[knb, abc_b], writes=[rr_b])
        E("vector", lambda e, o=pw[0][0][:, :], i=w_sb[:, :]: e.tensor_copy(o, i),
          reads=[w_b], writes=[pw[0][1]])
        E("vector", lambda e, o=pts[0][0][:, :], i=wt_sb[:, :]: e.tensor_copy(o, i),
          reads=[wt_b], writes=[pts[0][1]])
        for lvl in range(6):
            src_p, src_pb = pw[lvl % 2]
            dst_p, dst_pb = pw[(lvl + 1) % 2]
            pt_s, pt_sb = pts[lvl]
            pt_d, pt_db = pts[lvl + 1]
            pbh3, pbb3 = fA
            if lvl < 5:
                M(pbh3[:, 0:128], pbb3, [(pt_s[:, :], src_p[:, :])], [pt_sb, src_pb])
                E("scalar", lambda e, o=dst_p[:, :], i=pbh3[:, 0:128]: e.copy(o, i),
                  reads=[pbb3], writes=[dst_pb])
            pbh4, pbb4 = fB
            M(pbh4[:, 0:128], pbb4, [(src_p[:, :], pt_s[:, :])], [src_pb, pt_sb])
            E("scalar", lambda e, o=pt_d[:, :], i=pbh4[:, 0:128]: e.copy(o, i),
              reads=[pbb4], writes=[pt_db])
        E("vector", lambda e, o=uu[0][0][:, :], i=rr[:, :]: e.tensor_copy(o, i),
          reads=[rr_b], writes=[uu[0][1]])
        cu = 0
        for i in [6, 5, 4, 3, 2, 1]:
            ph5, pb5 = fA
            M(ph5[:, 0:256], pb5, [(pts[i][0][:, :], uu[cu][0][:, :])], [pts[i][1], uu[cu][1]])
            E("vector", lambda e, o=uu[1 - cu][0][:, :], a=uu[cu][0][:, :], b=ph5[:, 0:256]:
              e.tensor_add(o, a, b), reads=[uu[cu][1], pb5], writes=[uu[1 - cu][1]])
            cu = 1 - cu
        ph5, pb5 = fA
        M(ph5[:, 0:256], pb5, [(pts[0][0][:, :], uu[cu][0][:, :])], [pts[0][1], uu[cu][1]])
        E("vector", lambda e, o=uu[1 - cu][0][:, :], a=uu[cu][0][:, :], b=ph5[:, 0:256]:
          e.tensor_sub(o, a, b), reads=[uu[cu][1], pb5], writes=[uu[1 - cu][1]])
        cu = 1 - cu
        uh, ub_ = uu[cu]
        pbh5, pbb5 = bT_
        T(pbh5[:, 0:128], pbb5, uh[:, 128:256], ub_, idb[:, :], idb_b)
        E("scalar", lambda e, o=ukt_sb[:, :], i=pbh5[:, 0:128]: e.copy(o, i),
          reads=[pbb5], writes=[ukt_b])
        ph6, pb6 = fA
        M(ph6[:, 0:128], pb6, [(ukt_sb[:, :], shbf[:, :])], [ukt_b, sbbf])
        E("vector", lambda e, o=tc_sb[:, :], a=uh[:, 0:128], b=ph6[:, 0:128]:
          e.tensor_sub(o, a, b), reads=[ub_, pb6], writes=[tc_b])
        ph7, pb7 = fB
        M(ph7[:, 0:128], pb7, [(qh[:, cs], shbf[:, :])], [qb, sbbf])
        E("vector", lambda e, o=of_sb[:, :], i=ph7[:, 0:128], s=alc[:, :]:
          e.tensor_scalar_mul(o, i, s), reads=[pb7, alc_b], writes=[of_b])
        ph8, pb8 = fA
        M(ph8[:, 0:128], pb8, [(at_sb[:, :], tc_sb[:, :])], [at_b, tc_b])
        E("vector", lambda e, o=of_sb[:, :], a=of_sb[:, :], b=ph8[:, 0:128]:
          e.tensor_add(o, a, b), reads=[of_b, pb8], writes=[of_b])
        E("vector", lambda e, o=ew[:, :], a=of_sb[:, :], b=of_sb[:, :]: e.tensor_mul(o, a, b),
          reads=[of_b], writes=[ew_b])
        E("vector", lambda e, o=ovar[:, :], i=ew[:, :]: e.reduce_sum(o, i, AX.X),
          reads=[ew_b], writes=[ovar_b])
        E("vector", lambda e, o=ovar[:, :], i=ovar[:, :]: e.tensor_scalar(o, i, 1.0 / DV, 1e-6, ALU.mult, ALU.add),
          reads=[ovar_b], writes=[ovar_b])
        E("scalar", lambda e, o=ovar[:, :], i=ovar[:, :]: e.activation(o, i, AF.Ln),
          reads=[ovar_b], writes=[ovar_b])
        E("scalar", lambda e, o=ovar[:, :], i=ovar[:, :]: e.activation(o, i, AF.Exp, scale=-0.5),
          reads=[ovar_b], writes=[ovar_b])
        E("vector", lambda e, o=on_sb[:, :], i=of_sb[:, :], s=ovar[:, :], b=nwb[:, :]:
          e.scalar_tensor_tensor(o, i, s, b, ALU.mult, ALU.mult),
          reads=[of_b, ovar_b, nwb_b], writes=[on_b])
        oh, ob = ohead[sub]
        pbh6, pbb6 = bT_
        T(pbh6[:, 0:128], pbb6, on_sb[:, :], on_b, idb[:, :], idb_b)
        E("scalar", lambda e, o=oh[:, cs], i=pbh6[:, 0:128]: e.copy(o, i),
          reads=[pbb6], writes=[ob])
        E("vector", lambda e, o=ktl[:, :], i=knh[:, :], s=ksc[:, :]: e.tensor_scalar_mul(o, i, s),
          reads=[knb, ksc_b], writes=[ktl_b])
        ph9, pb9 = fB
        M(ph9[:, 0:128], pb9, [(ktl[:, :], tc_sb[:, :])], [ktl_b, tc_b])
        E("vector", lambda e, o=sh32[:, :], i=sh32[:, :], s=acb[c][0][:, h:h + 1], b=ph9[:, 0:128]:
          e.scalar_tensor_tensor(o, i, s, b, ALU.mult, ALU.add),
          reads=[sb32, acb[c][1], pb9], writes=[sb32])
        E("vector", lambda e, o=shbf[:, :], i=sh32[:, :]: e.tensor_copy(o, i),
          reads=[sb32], writes=[sbbf])
        return ops

    for kh in range(8):
        qh, qb = qt[kh]
        khh, khb = kt[kh]
        for sub in range(2):
            sh, sb_ = st32[sub]
            P.emit("vector", lambda e, o=sh[:, :]: e.memset(o, 0.0), writes=[sb_])
            bh, bb = stbf[sub]
            P.emit("vector", lambda e, o=bh[:, :]: e.memset(o, 0.0), writes=[bb])
        for c in range(NCH):
            cs = slice(c * CC, (c + 1) * CC)
            kkh, kkb = kkp[c % 2]
            pqh, pqb = pqkp[c % 2]
            knh, knb = knp[c % 2]
            ph1, pb1 = pre_f0
            P.mm(ph1[:, 0:128], pb1, [(khh[:, cs], khh[:, cs])], [khb])
            P.emit("scalar", lambda e, o=kkh[:, :], i=ph1[:, 0:128]: e.copy(o, i), reads=[pb1], writes=[kkb])
            ph2, pb2 = pre_f1
            P.mm(ph2[:, 0:128], pb2, [(khh[:, cs], qh[:, cs])], [khb, qb])
            P.emit("scalar", lambda e, o=pqh[:, :], i=ph2[:, 0:128]: e.copy(o, i), reads=[pb2], writes=[pqb])
            pb_h, pb_b = pre_b0
            P.tr(pb_h[:, 0:128], pb_b, khh[:, cs], khb, idb[:, :], idb_b)
            P.emit("scalar", lambda e, o=knh[:, :], i=pb_h[:, 0:128]: e.copy(o, i), reads=[pb_b], writes=[knb])
            o0 = sub_ops(kh, c, 0)
            o1 = sub_ops(kh, c, 1)
            for a, b in zip(o0, o1):
                a()
                b()
        for sub in range(2):
            h = 2 * kh + sub
            oh, ob = ohead[sub]
            for blk in range(2):
                mh2, mb2 = mscr[0]
                P.emit("vector", lambda e, o=mh2[:, :], i=oh[:, :], s=mk_sb[:, blk:blk + 1]:
                       e.tensor_scalar_mul(o, i, s), reads=[ob, mk_b], writes=[mb2])
                P.dma(rsv[16 * blk + h], mh2[:, :].rearrange("p (j t) -> p j t", j=2),
                      rsin_b, src_buf=mb2)


def emit_gdnout(P, x_t, xo_t, onorm_t, zs_t, wout_t):
    x_ap, x_b = x_t
    xo_ap, xo_b = xo_t
    onorm, onorm_b = onorm_t
    zs, zs_b = zs_t
    wout, wout_b = wout_t

    psf, psb, scr = _alloc_common(P)
    xt = [P.sb(f"x{k}", [128, TOK], F32) for k in range(2)]
    xn = [P.sb(f"xn{k}", [128, TOK], F32) for k in range(2)]
    of = [P.sb(f"of{i}", [128, TOK], BF16) for i in range(32)]
    zt = [P.sb(f"zt{k}", [128, TOK], BF16) for k in range(2)]
    for h in range(32):
        zh, zb = zt[h % 2]
        P.dma(zh[:, :], zs[h], zb, src_buf=zs_b)
        oh, ob = of[h]
        P.dma(oh[:, :], onorm[h], ob, src_buf=onorm_b)
        P.emit("vector", lambda e, o=oh[:, :], a=oh[:, :], b=zh[:, :]: e.tensor_mul(o, a, b),
               reads=[ob, zb], writes=[ob])
    wo_s = [P.sb(f"wo{k}", [128, 4096], BF16) for k in range(2)]
    wor = wout.rearrange("(h p) (d c) -> d p h c", p=128, c=128)
    xr = x_ap.rearrange("(i p) t -> i p t", p=128)
    xor_ = xo_ap.rearrange("(i p) t -> i p t", p=128)
    for d in range(ND):
        wh, wb = wo_s[d % 2]
        P.dma(wh[:, :], wor[d], wb, src_buf=wout_b, cast=True)
        xh, xb = xt[d % 2]
        P.dma(xh[:, :], xr[d], xb, src_buf=x_b)
        xnh, xnb = xn[d % 2]
        for tb in range(2):
            sl = slice(tb * TB, (tb + 1) * TB)
            ph, pb = psf[2 + tb]
            pairs = [(wh[:, hh * 128:(hh + 1) * 128], of[hh][0][:, sl]) for hh in range(32)]
            P.mm(ph[:, :], pb, pairs, [wb] + [of[hh][1] for hh in range(32)])
            P.emit("vector", lambda e, o=xnh[:, sl], a=xh[:, sl], b=ph[:, :]: e.tensor_add(o, a, b),
                   reads=[xb, pb], writes=[xnb])
        P.dma(xor_[d], xnh[:, :], xo_b, src_buf=xnb)


def _f32(a):
    return np.ascontiguousarray(a, dtype=np.float32)

def kernel(**inp):
    C = host_consts()

    x = _f32(inp["L0_attn_x"])
    xs = []
    for c in range(NCORE):
        b, hf = c // 2, c % 2
        xs.append(np.ascontiguousarray(x[b, hf * TOK:(hf + 1) * TOK, :].T))

    WupT = np.ascontiguousarray(_f32(inp["L0_ffn_W_up"]).T)
    WgT = np.ascontiguousarray(_f32(inp["L0_ffn_W_gate"]).T)
    WdT = np.ascontiguousarray(_f32(inp["L0_ffn_W_down"]).T)
    Wdp = np.ascontiguousarray(WdT.reshape(64, 64, 2, D).transpose(1, 0, 2, 3).reshape(64, 128, D))
    su_ug = np.ascontiguousarray(_f32(inp["L0_ffn_SU_ug"]).reshape(16, 128).T) / np.sqrt(np.float32(D))
    su_dp = np.ascontiguousarray(_f32(inp["L0_ffn_SU_dp"]).reshape(64, 128).T)
    ws_up = np.ascontiguousarray(_f32(inp["L0_ffn_Wscale_ug"])[:INTER].reshape(64, 128).T)
    ws_g = np.ascontiguousarray(_f32(inp["L0_ffn_Wscale_ug"])[INTER:].reshape(64, 128).T)
    sw_up = np.ascontiguousarray(su_dp * ws_up)
    ws_d = np.ascontiguousarray(_f32(inp["L0_ffn_Wscale_dp"]).reshape(16, 128).T) * 0.5
    BT = np.ascontiguousarray(_f32(inp["L0_attn_B"]).T)
    CT = np.ascontiguousarray(_f32(inp["L0_attn_C"]).T)
    Dv = np.ascontiguousarray(_f32(inp["L0_attn_D"]).reshape(16, 128).T)

    Wr = _f32(inp["L1_ffn_Wr"])
    br = _f32(inp["L1_ffn_br"]).reshape(8, 1)
    WdS = np.ascontiguousarray(_f32(inp["L1_ffn_Wd"]).transpose(1, 0, 2).reshape(D, 128))
    bdS = _f32(inp["L1_ffn_bd"]).reshape(128, 1)
    WuS = np.ascontiguousarray(_f32(inp["L1_ffn_Wu"]).reshape(128, D))
    bu = _f32(inp["L1_ffn_bu"])

    wqkv = _f32(inp["L1_attn_w_qkv"])
    wz = _f32(inp["L1_attn_w_z"])
    wba = np.ascontiguousarray(np.concatenate([_f32(inp["L1_attn_w_b"]), _f32(inp["L1_attn_w_a"])], axis=1))
    alog = _f32(inp["L1_attn_A_log"]).reshape(32, 1)
    dtb = _f32(inp["L1_attn_dt_bias"]).reshape(32, 1)
    cwr = np.ascontiguousarray(_f32(inp["L1_attn_conv_w"]).reshape(64, 128, 4))
    nw = _f32(inp["L1_attn_norm_weight"]).reshape(1, 128)
    wout = _f32(inp["L1_attn_w_out"])

    wqkvc, wbacs, alogcs, dtbcs, cwqs, cwks, cwvs, monorms = [], [], [], [], [], [], [], []
    for hg in range(2):
        wqkvc.append(np.ascontiguousarray(np.concatenate(
            [wqkv[:, 1024 * hg:1024 * hg + 1024],
             wqkv[:, 2048 + 1024 * hg:2048 + 1024 * hg + 1024],
             wqkv[:, 4096 + 2048 * hg:4096 + 2048 * hg + 2048]], axis=1)))
        wbacs.append(np.ascontiguousarray(np.concatenate(
            [wba[:, 16 * hg:16 * hg + 16], wba[:, 32 + 16 * hg:32 + 16 * hg + 16]], axis=1)))
        alogcs.append(np.ascontiguousarray(alog[16 * hg:16 * hg + 16]))
        dtbcs.append(np.ascontiguousarray(dtb[16 * hg:16 * hg + 16]))
        cwqs.append(np.ascontiguousarray(cwr[8 * hg:8 * hg + 8]))
        cwks.append(np.ascontiguousarray(cwr[16 + 8 * hg:16 + 8 * hg + 8]))
        cwvs.append(np.ascontiguousarray(cwr[32 + 16 * hg:32 + 16 * hg + 16]))
        monorms.append(np.tile(np.array([1.0 - hg, float(hg)], np.float32), (128, 1)))

    P = Prog()
    x0 = P.din("x", [D, TOK], F32)
    xo_t = P.dout("xo", [D, TOK], F32)
    ones1_t = P.din("ones1", [1, 128], F32)
    ones128_t = P.din("ones128", [128, 1], F32)
    wdp_t = P.din("wdp", [64, 128, D], F32)
    h64_t = P.din("h64i2", [128, 128], F32)
    iden_t = P.din("iden", [128, 128], F32)
    wdf_t = P.dint("wdf", [INTER, D], BF16)
    x1 = P.dint("x1", [D, TOK], F32)
    x2 = P.dint("x2", [D, TOK], F32)
    x3 = P.dint("x3", [D, TOK], F32)
    x4 = P.dint("x4", [D, TOK], F32)
    x5 = P.dint("x5", [D, TOK], F32)
    xnin = P.dint("xnin", [D, TOK], BF16)
    xnf = P.dint("xnf", [2 * D, TOK], BF16)
    zsd = P.dint("zsd", [32, 128, TOK], BF16)
    qind = P.dint("qind", [8, 128, 2048], BF16)
    kind_ = P.dint("kind", [8, 128, 2048], BF16)
    vind = P.dint("vind", [16, 128, 2048], BF16)
    betad = P.dint("betad", [16, 2048], F32)
    gd = P.dint("gd", [16, 2048], F32)
    rsin = P.dint("rsin", [64, 128, TOK], BF16)
    rsout = P.dint("rsout", [32, 128, TOK], BF16)

    W0 = dict(wupT=P.din("wupT", [D, INTER], F32), wgT=P.din("wgT", [D, INTER], F32),
              wdf=wdf_t, su_ug=P.din("su_ug", [128, 16], F32), sw_up=P.din("sw_up", [128, 64], F32),
              ws_g=P.din("ws_g", [128, 64], F32), ws_d=P.din("ws_d", [128, 16], F32),
              h128=P.din("h128", [128, 128], F32), ones1=ones1_t, ones128=ones128_t)
    WA = dict(bT=P.din("bT", [D, 1024], F32), cT=P.din("cT", [1024, D], F32),
              dvec=P.din("dvec", [128, 16], F32), ones1=ones1_t, ones128=ones128_t)
    WM = dict(wr=P.din("wr", [D, 8], F32), br=P.din("br", [8, 1], F32),
              wds=P.din("wds", [D, 128], F32), bds=P.din("bds", [128, 1], F32),
              wus=P.din("wus", [128, D], F32), bu=P.din("bu", [8, D], F32),
              e16=P.din("e16", [8, 128], F32), iden=iden_t, ones1=ones1_t, ones128=ones128_t)
    WZ = dict(wz=P.din("wz", [D, 4096], F32), ones1=ones1_t, ones128=ones128_t)
    WP = dict(wqkvc=P.din("wqkvc", [D, 4096], F32), wbac=P.din("wbac", [D, 32], F32),
              alogc=P.din("alogc", [16, 1], F32), dtbc=P.din("dtbc", [16, 1], F32))
    WC = dict(cwq=P.din("cwq", [8, 128, 4], F32), cwk=P.din("cwk", [8, 128, 4], F32),
              cwv=P.din("cwv", [16, 128, 4], F32), monorm=P.din("monorm", [128, 2], F32),
              nw=P.din("nw", [1, 128], F32), maskw=P.din("maskw", [128, 128], F32),
              maskat=P.din("maskat", [128, 128], F32), iden=iden_t,
              ones1=ones1_t, ones128=ones128_t)
    wout_t = P.din("wout", [4096, D], F32)

    P.push("s0_")
    com = _alloc_common(P)
    emit_prep(P, wdp_t, h64_t, wdf_t, com[0])
    emit_ffn0(P, x0, x1, W0, psum=com)
    P.pop()
    P.push("s2_")
    emit_attn0(P, x1, x2, WA)
    P.pop()
    P.push("s3_")
    emit_ffn0(P, x2, x3, W0)
    P.pop()
    P.push("s4_")
    emit_moe(P, x3, x4, WM)
    P.pop()
    P.push("s5_")
    emit_gdnz(P, x4, WZ, xnin, xnf, zsd)
    P.pop()
    P.push("s6_")
    emit_gdnpc(P, xnf, WP, qind, kind_, vind, betad, gd)
    P.pop()
    P.push("s7_")
    emit_gdnchunk(P, WC, qind, kind_, vind, betad, gd, rsin)
    P.cc("ReduceScatter", ALU.add, rsin[0], rsin[1], rsout[0], rsout[1], "cc1")
    P.pop()
    P.push("s8_")
    emit_gdnout(P, x4, x5, rsout, zsd, wout_t)
    P.pop()
    P.push("s9_")
    emit_moe(P, x5, xo_t, WM, final=True)
    P.pop()

    maps = []
    for c in range(NCORE):
        hg = c % 2
        maps.append(dict(
            x=xs[c], ones1=C["ONES1"], ones128=C["ONES128"], wdp=Wdp, h64i2=C["H64I2"],
            iden=C["IDEN"], wupT=WupT, wgT=WgT, su_ug=su_ug, sw_up=sw_up, ws_g=ws_g,
            ws_d=ws_d, h128=C["H128"], bT=BT, cT=CT, dvec=Dv, wr=Wr, br=br, wds=WdS,
            bds=bdS, wus=WuS, bu=bu, e16=C["E16"], wz=wz, wqkvc=wqkvc[hg],
            wbac=wbacs[hg], alogc=alogcs[hg], dtbc=dtbcs[hg], cwq=cwqs[hg],
            cwk=cwks[hg], cwv=cwvs[hg], monorm=monorms[hg], nw=nw, maskw=C["MASKW"],
            maskat=C["MASKAT"], wout=wout))

    r = P.run(maps, core_ids=list(range(NCORE)))
    out = np.empty((4, 2048, D), np.float32)
    for c in range(NCORE):
        b, hf = c // 2, c % 2
        out[b, hf * TOK:(hf + 1) * TOK, :] = np.asarray(r.results[c]["xo"], np.float32).T
    return out
